# version 03
# baseline (speedup 1.0000x reference)
import numpy as np
from contextlib import ExitStack

import concourse.bass as bass
import concourse.mybir as mybir
from concourse.bass_core_rust import IndirectOffsetOnAxis
from concourse.bass_utils import run_bass_kernel_spmd

F32 = mybir.dt.float32
BF16 = mybir.dt.bfloat16
U32 = mybir.dt.uint32
I16 = mybir.dt.int16
AF = mybir.ActivationFunctionType
OP = mybir.AluOpType

T_SEQ = 2048
C = 1024
E = 8
CAP = 320
H = 4096
QH = 4
NCC = C // 128
NSI = 32
GRP = 4
NHC = H // 128
EPS = 1e-5
TSTRIDE = 384
XN_RING = 9


class Sem:
    def __init__(self, es, nc, name):
        self.h = es.enter_context(nc.semaphore(name))
        self.n = 0

    def inc(self, k):
        self.n += k
        return self.n


_CHAIN_SKIP = {"wait_ge", "dma_start", "indirect_dma_start", "dma_start_transpose",
               "register", "reg_mov", "snap", "sem_inc", "wait_op", "nop"}


class _InstProxy:

    def __init__(self, eng, ch):
        self._eng = eng
        self._ch = ch

    def then_inc(self, sem, k):
        m = self._eng.sem_inc(sem, k)
        m._wait_ge(self._ch.h, self._ch.n)
        return self


class ChainEng:

    def __init__(self, eng, ch):
        object.__setattr__(self, "_e", eng)
        object.__setattr__(self, "_c", ch)

    def __getattr__(self, name):
        attr = getattr(self._e, name)
        if (not callable(attr)) or name in _CHAIN_SKIP or name.startswith("_"):
            return attr
        ch = self._c
        eng = self._e

        def wrap(*a, **k):
            inst = attr(*a, **k)
            if inst is not None and hasattr(inst, "then_inc"):
                if ch.n > 0:
                    inst._wait_ge(ch.h, ch.n)
                inst.then_inc(ch.h, 1)
                ch.n += 1
                return _InstProxy(eng, ch)
            return inst
        return wrap


def build_program(nb=4):
    NTOK = nb * T_SEQ
    NTIL = NTOK // 128
    TPB = T_SEQ // 128
    NGRP = NTIL // GRP
    GPB = TPB // GRP
    NSLOT = E * nb * CAP
    ZROW = NSLOT
    TBL_ROWS = ((E * nb * TSTRIDE + 127) // 128) * 128

    nc = bass.Bass("TRN2", target_bir_lowering=False)

    x_d = nc.dram_tensor("x", [NTOK, C], F32, kind="ExternalInput").ap()
    wfc_d = nc.dram_tensor("wfc", [C, C], F32, kind="ExternalInput").ap()
    wproj_d = nc.dram_tensor("wproj", [C, C], F32, kind="ExternalInput").ap()
    lng_d = nc.dram_tensor("lng", [NCC, 128], F32, kind="ExternalInput").ap()
    lnb_d = nc.dram_tensor("lnb", [NCC, 128], F32, kind="ExternalInput").ap()
    wr_d = nc.dram_tensor("wr", [C, E], F32, kind="ExternalInput").ap()
    w1_d = nc.dram_tensor("w1", [E * C, H], F32, kind="ExternalInput").ap()
    b1_d = nc.dram_tensor("b1", [E * H // 128, 128], F32, kind="ExternalInput").ap()
    w2_d = nc.dram_tensor("w2", [E * H, C], F32, kind="ExternalInput").ap()
    b2_d = nc.dram_tensor("b2", [E, C], F32, kind="ExternalInput").ap()
    out_d = nc.dram_tensor("out", [NTOK, C], F32, kind="ExternalOutput").ap()

    xn_d = nc.dram_tensor("xn_scr", [NTOK, C], BF16, kind="Internal").ap()
    s_d = nc.dram_tensor("s_scr", [NTOK, C], BF16, kind="Internal").ap()
    y_d = nc.dram_tensor("y_scr", [NSLOT + 1, C], BF16, kind="Internal").ap()
    tbl_d = nc.dram_tensor("tbl_scr", [TBL_ROWS, 2], U32, kind="Internal").ap()
    w1t_d = nc.dram_tensor("w1t_scr", [E * 8 * 128, H], BF16, kind="Internal").ap()
    w2t_d = nc.dram_tensor("w2t_scr", [E * H, C], BF16, kind="Internal").ap()

    es = ExitStack()
    es_p1 = ExitStack()

    def sb(stack, name, shape, dtype=F32):
        return stack.enter_context(nc.sbuf_tensor(name, shape, dtype))

    def ps(stack, name, shape, dtype=F32):
        return stack.enter_context(nc.psum_tensor(name, shape, dtype))

    ident_f = sb(es, "ident_f", [128, 128], F32)
    iota8 = sb(es, "iota8", [128, 8], F32)
    iotap = sb(es, "iotap", [128, 1], F32)
    iotad = sb(es, "iotad", [128, 128], I16)
    eps_ap = sb(es, "eps_ap", [128, 1], F32)
    ones_row_bf = sb(es, "ones_row_bf", [1, 512], BF16)
    ones_col_bf = sb(es, "ones_col_bf", [128, 1], BF16)
    g_sb = sb(es, "g_sb", [128, NCC], F32)
    battn_bf = sb(es, "battn_bf", [128, NCC], BF16)
    wr_hi = sb(es, "wr_hi", [128, NCC * E], BF16)
    wr_lo = sb(es, "wr_lo", [128, NCC * E], BF16)
    wrsum_bc = sb(es, "wrsum_bc", [128, E], F32)
    bk_sb = sb(es, "bk_sb", [128, NSI * QH], F32)
    b1_sb = sb(es, "b1_sb", [128, E * NHC], F32)
    c12_all = sb(es, "c12_all", [128, 2 * NTIL], F32)
    yidx_all = sb(es, "yidx_all", [128, 2 * NTIL], U32)
    zero_bf = sb(es, "zero_bf", [128, C], BF16)

    sem_xp = [Sem(es, nc, "sx0"), Sem(es, nc, "sx1")]
    sem_tr = Sem(es, nc, "s_tr")
    sem_gxn = Sem(es, nc, "s_gxn")
    sem_gs = [Sem(es, nc, "s_gs0"), Sem(es, nc, "s_gs1")]
    sem_gsc = Sem(es, nc, "s_gsc")
    sem_ld = Sem(es, nc, "s_ld")
    sem_yp = [Sem(es, nc, "s_y0"), Sem(es, nc, "s_y1")]
    sem_yfree = Sem(es, nc, "s_yfree")
    sem_ydn = Sem(es, nc, "s_ydn")
    sem_gix = Sem(es, nc, "s_gix")
    sem_b2 = Sem(es, nc, "s_b2")
    sem_wp1 = [Sem(es, nc, "s_wp1a"), Sem(es, nc, "s_wp1b")]
    sem_ws1 = [Sem(es, nc, "s_ws1a"), Sem(es, nc, "s_ws1b")]
    sem_wp2 = [Sem(es, nc, "s_wp2a"), Sem(es, nc, "s_wp2b")]
    sem_ws2 = [Sem(es, nc, "s_ws2a"), Sem(es, nc, "s_ws2b")]
    sem_gg = [Sem(es, nc, "s_gg0"), Sem(es, nc, "s_gg1")]
    sem_gyg = [Sem(es, nc, "s_gyg0"), Sem(es, nc, "s_gyg1")]
    sem_p3p = [Sem(es, nc, "s_p30"), Sem(es, nc, "s_p31")]
    sem_op = [Sem(es, nc, "s_op0"), Sem(es, nc, "s_op1")]
    sem_ofree = Sem(es, nc, "s_ofree")
    sem_out = Sem(es, nc, "s_out")
    sem_t = Sem(es, nc, "s_t")
    sem_v = Sem(es, nc, "s_v")
    sem_s = Sem(es, nc, "s_s")
    sem_gc = Sem(es, nc, "s_gc")
    sem_ph = Sem(es, nc, "s_ph")
    sem_wq = [Sem(es, nc, "s_wq%d" % i) for i in range(4)]
    ch_v = Sem(es, nc, "chv")
    ch_s = Sem(es, nc, "chs")

    with nc.Block() as block:
        def vec(f):
            block.vector(lambda v: f(ChainEng(v, ch_v)))

        def sca(f):
            block.scalar(lambda x: f(ChainEng(x, ch_s)))

        def _phase_barrier():
            tgt = sem_ph.n + 5

            def _arr_t(t_):
                t_.sem_inc(sem_ph.h, 1)
                t_.wait_ge(sem_ph.h, tgt)
            block.tensor(_arr_t)

            def _arr_s(sy):
                sy.sem_inc(sem_ph.h, 1)
                sy.wait_ge(sem_ph.h, tgt)
            block.sync(_arr_s)

            def _arr_g(g):
                g.sem_inc(sem_ph.h, 1)
                g.wait_ge(sem_ph.h, tgt)
            block.gpsimd(_arr_g)

            def _arr_v(v):
                v.wait_ge(ch_v.h, ch_v.n)
                v.sem_inc(sem_ph.h, 1)
                v.wait_ge(sem_ph.h, tgt)
            block.vector(_arr_v)

            def _arr_sc(x):
                x.wait_ge(ch_s.h, ch_s.n)
                x.sem_inc(sem_ph.h, 1)
                x.wait_ge(sem_ph.h, tgt)
            block.scalar(_arr_sc)
            sem_ph.inc(5)

        def _dma_drain():
            def s_drain(sy):
                for sm_ in (sem_tr, sem_ld, sem_xp[0], sem_xp[1], sem_yp[0],
                            sem_yp[1], sem_out, sem_op[0], sem_op[1],
                            sem_p3p[0], sem_p3p[1]):
                    if sm_.n:
                        sy.wait_ge(sm_.h, sm_.n)
            block.sync(s_drain)

            def g_drain(g):
                for sm_ in (sem_gxn, sem_gs[0], sem_gs[1], sem_gsc, sem_gix,
                            sem_gg[0], sem_gg[1], sem_gyg[0], sem_gyg[1],
                            sem_wq[0], sem_wq[1], sem_wq[2], sem_wq[3], sem_b2,
                            sem_wp1[0], sem_wp1[1], sem_ws1[0], sem_ws1[1],
                            sem_wp2[0], sem_wp2[1], sem_ws2[0], sem_ws2[1]):
                    if sm_.n:
                        g.wait_ge(sm_.h, sm_.n)
            block.gpsimd(g_drain)
        st_sb = ExitStack()
        wfcT = sb(es_p1, "wfcT", [128, NCC * C], BF16)
        wprojT = sb(es_p1, "wprojT", [128, NCC * C], BF16)
        wstage = sb(st_sb, "wstage", [128, C], F32)
        wstage_bf = sb(st_sb, "wstage_bf", [128, 2 * C], BF16)
        small_f = sb(st_sb, "small_f", [128, 128], F32)
        small_f2 = sb(st_sb, "small_f2", [128, 128], F32)
        wsum8 = sb(st_sb, "wsum8", [8, 1], F32)
        wsrow_bf = sb(st_sb, "wsrow_bf", [1, 8], BF16)
        ztbl = sb(st_sb, "ztbl", [128, TBL_ROWS * 2 // 128], U32)
        ps_set = ps(st_sb, "ps_set", [128, 512], F32)
        ps_set2 = ps(st_sb, "ps_set2", [128, 8], F32)

        def g_setup(g):
            g.iota(iotad[:], pattern=[[1, 128]], base=0, channel_multiplier=-1)
            g.iota(iota8[:], pattern=[[1, 8]], base=0, channel_multiplier=0,
                   allow_small_or_imprecise_dtypes=True)
            g.iota(iotap[:], pattern=[[0, 1]], base=0, channel_multiplier=1,
                   allow_small_or_imprecise_dtypes=True)
            g.memset(ztbl[:], 0).then_inc(sem_gc.h, 1)
        block.gpsimd(g_setup)
        sem_gc.inc(1)

        def v_setup0(v):
            v.wait_ge(sem_gc.h, sem_gc.n)
            v.memset(eps_ap[:], EPS)
            v.memset(ones_row_bf[:], 1.0)
            v.memset(ones_col_bf[:], 1.0)
            v.memset(zero_bf[:], 0.0)
            v.tensor_scalar(out=ident_f[:], in0=iotad[:], scalar1=0.0, scalar2=None,
                            op0=OP.is_equal).then_inc(sem_v.h, 1)
        vec(v_setup0)
        sem_v.inc(1)

        def g_zero(g):
            g.wait_ge(sem_v.h, sem_v.n)
            g.dma_start(out=tbl_d.rearrange("(p k) c -> p (k c)", p=128),
                        in_=ztbl[:]).then_inc(sem_gxn.h, 16)
            g.dma_start(out=y_d[ZROW:ZROW + 1, :],
                        in_=zero_bf[0:1, :]).then_inc(sem_gxn.h, 16)
        block.gpsimd(g_zero)
        sem_gxn.inc(32)

        def s_loads(sy):
            sy.dma_start(out=small_f[0:NCC, :], in_=lng_d[:, :]).then_inc(sem_ld.h, 16)
            sy.dma_start(out=small_f2[0:NCC, :], in_=lnb_d[:, :]).then_inc(sem_ld.h, 16)
            sy.dma_start(out=wstage[:, 0:NCC * E].rearrange("p (cc e) -> p cc e", e=E),
                         in_=wr_d[:, :].rearrange("(cc p) e -> p cc e", p=128)
                         ).then_inc(sem_ld.h, 16)
        block.sync(s_loads)
        sem_ld.inc(48)

        def t_gb(t):
            t.wait_ge(sem_v.h, sem_v.n)
            t.wait_ge(sem_ld.h, sem_ld.n)
            t.transpose(ps_set[:, 0:8], small_f[0:NCC, :], ident_f[0:NCC, 0:NCC])
            t.transpose(ps_set[:, 8:16], small_f2[0:NCC, :],
                        ident_f[0:NCC, 0:NCC]).then_inc(sem_t.h, 1)
        block.tensor(t_gb)
        sem_t.inc(1)

        def v_gb(v):
            v.wait_ge(sem_t.h, sem_t.n)
            v.tensor_copy(out=g_sb[:], in_=ps_set[:, 0:8])
            v.tensor_copy(out=battn_bf[:], in_=ps_set[:, 8:16])
            v.tensor_copy(out=wr_hi[:], in_=wstage[:, 0:NCC * E])
            v.tensor_sub(out=wr_lo[:], in0=wstage[:, 0:NCC * E], in1=wr_hi[:]
                         ).then_inc(sem_v.h, 1)
        vec(v_gb)
        sem_v.inc(1)

        def s_b1ld(sy):
            sy.wait_ge(sem_t.h, sem_t.n)
            sy.dma_start(out=small_f[:], in_=b1_d[0:128, :]).then_inc(sem_ld.h, 16)
            sy.dma_start(out=small_f2[:], in_=b1_d[128:256, :]).then_inc(sem_ld.h, 16)
        block.sync(s_b1ld)
        sem_ld.inc(32)


        def t_b1(t):
            t.wait_ge(sem_ld.h, sem_ld.n)
            t.wait_ge(sem_v.h, sem_v.n)
            t.transpose(ps_set[:, 0:128], small_f[:], ident_f[:])
            t.transpose(ps_set[:, 128:256], small_f2[:], ident_f[:]
                        ).then_inc(sem_t.h, 1)
        block.tensor(t_b1)
        sem_t.inc(1)

        def v_b1(v):
            v.wait_ge(sem_t.h, sem_t.n)
            v.tensor_copy(out=b1_sb[:], in_=ps_set[:, 0:256]).then_inc(sem_v.h, 1)
        vec(v_b1)
        sem_v.inc(1)

        ntr = 0
        for (src, dst) in ((wfc_d, wfcT), (wproj_d, wprojT)):
            for rc in range(8):
                slot = ntr % 2

                def s_wload(sy, src=src, rc=rc, vw=None):
                    sy.wait_ge(sem_v.h, sem_v.n)
                    sy.dma_start(out=wstage[:], in_=src[rc * 128:(rc + 1) * 128, :]
                                 ).then_inc(sem_ld.h, 16)
                block.sync(s_wload)
                sem_ld.inc(16)

                def v_wcast(v, slot=slot, ntr=ntr):
                    v.wait_ge(sem_ld.h, sem_ld.n)
                    if ntr >= 2:
                        v.wait_ge(sem_tr.h, 16 * (ntr - 1))
                    v.tensor_copy(out=wstage_bf[:, slot * C:(slot + 1) * C],
                                  in_=wstage[:]).then_inc(sem_v.h, 1)
                vec(v_wcast)
                sem_v.inc(1)

                def s_wtr(sy, dst=dst, rc=rc, slot=slot):
                    sy.wait_ge(sem_v.h, sem_v.n)
                    sy.wait_ge(sem_gxn.h, sem_gxn.n)
                    if sem_tr.n:
                        sy.wait_ge(sem_tr.h, sem_tr.n)
                    outap = dst[:, :].rearrange("p (cc r) -> p cc r", cc=NCC)[
                        :, :, rc * 128:(rc + 1) * 128]
                    sy.dma_start(out=outap, in_=wstage_bf[:, slot * C:(slot + 1) * C],
                                 transpose=True).then_inc(sem_tr.h, 16)
                block.sync(s_wtr)
                sem_tr.inc(16)
                ntr += 1

        for si in range(NSI):
            q = si // 8

            def t_bk(t, si=si, q=q):
                if si == 0:
                    t.wait_ge(sem_tr.h, sem_tr.n)
                t.wait_ge(sem_v.h, sem_v.n)
                last = None
                for h in range(QH):
                    for dc in range(2):
                        base = (2 * q + dc) * C + (si % 8) * 128
                        last = t.matmul(ps_set2[:, h:h + 1],
                                        wfcT[:, base:base + 128],
                                        battn_bf[:, 2 * h + dc:2 * h + dc + 1],
                                        start=(dc == 0), stop=(dc == 1))
                last.then_inc(sem_t.h, 1)
            block.tensor(t_bk)
            sem_t.inc(1)

            def v_bk(v, si=si):
                v.wait_ge(sem_t.h, sem_t.n)
                v.tensor_copy(out=bk_sb[:, si * 4:si * 4 + 4],
                              in_=ps_set2[:, 0:4]).then_inc(sem_v.h, 1)
            vec(v_bk)
            sem_v.inc(1)

        def t_ws(t):
            t.wait_ge(sem_v.h, sem_v.n)
            last = None
            for i, w in enumerate((wr_hi, wr_lo)):
                for cc in range(NCC):
                    last = t.matmul(ps_set2[0:8, 0:1], w[:, cc * 8:cc * 8 + 8],
                                    ones_col_bf[:, :], start=(i == 0 and cc == 0),
                                    stop=(i == 1 and cc == NCC - 1))
            last.then_inc(sem_t.h, 1)
        block.tensor(t_ws)
        sem_t.inc(1)

        def v_ws(v):
            v.wait_ge(sem_t.h, sem_t.n)
            v.tensor_copy(out=wsum8[:], in_=ps_set2[0:8, 0:1]).then_inc(sem_v.h, 1)
        vec(v_ws)
        sem_v.inc(1)

        def t_ws2(t):
            t.wait_ge(sem_v.h, sem_v.n)
            t.transpose(ps_set2[0:1, 0:8], wsum8[:], ident_f[0:8, 0:8]
                        ).then_inc(sem_t.h, 1)
        block.tensor(t_ws2)
        sem_t.inc(1)

        def v_ws2(v):
            v.wait_ge(sem_t.h, sem_t.n)
            v.tensor_copy(out=wsrow_bf[:], in_=ps_set2[0:1, 0:8]).then_inc(sem_v.h, 1)
        vec(v_ws2)
        sem_v.inc(1)

        def t_ws3(t):
            t.wait_ge(sem_v.h, sem_v.n)
            t.matmul(ps_set2[:, 0:8], ones_row_bf[0:1, 0:128], wsrow_bf[:],
                     start=True, stop=True).then_inc(sem_t.h, 1)
        block.tensor(t_ws3)
        sem_t.inc(1)

        def v_ws3(v):
            v.wait_ge(sem_t.h, sem_t.n)
            v.tensor_copy(out=wrsum_bc[:], in_=ps_set2[:, 0:8]).then_inc(sem_v.h, 1)
        vec(v_ws3)
        sem_v.inc(1)

        _dma_drain()
        _phase_barrier()
        st_sb.close()

        xb = sb(es_p1, "xb", [128, 2 * C], F32)
        xnf = sb(es_p1, "xnf", [128, 2 * C], F32)
        xnhi = sb(es_p1, "xnhi", [128, XN_RING * C], BF16)
        zf = sb(es_p1, "zf", [128, C], F32)
        zhi = sb(es_p1, "zhi", [128, 2 * C], BF16)
        zlo = sb(es_p1, "zlo", [128, 2 * C], BF16)
        zThi = sb(es_p1, "zThi", [128, 2 * NCC * 512], BF16)
        zTlo = sb(es_p1, "zTlo", [128, 2 * NCC * 512], BF16)
        zgT = sb(es_p1, "zgT", [128, NCC * 512], BF16)
        pT = sb(es_p1, "pT", [128, NSI * 512], BF16)
        s_stage = sb(es_p1, "s_stage", [128, 2 * GRP * C], BF16)
        routb = sb(es_p1, "routb", [8, 2 * T_SEQ], F32)
        hT = routb[0:8, 0:T_SEQ]
        cex = routb[0:8, T_SEQ:2 * T_SEQ]
        is1_b = sb(es_p1, "is1_b", [128, 8 * TPB], F32)
        is2_b = sb(es_p1, "is2_b", [128, 8 * TPB], F32)
        p12_b = sb(es_p1, "p12_b", [128, 2 * TPB], F32)
        e12_b = sb(es_p1, "e12_b", [128, 2 * TPB], F32)
        sd_ring = sb(es_p1, "sd_ring", [128, GRP], F32)
        mu_ring = sb(es_p1, "mu_ring", [128, GRP], F32)
        bns = sb(es_p1, "bns", [128, 12], F32)
        bn2 = sb(es_p1, "bn2", [128, 2], F32)
        bn2i = sb(es_p1, "bn2i", [128, 2], F32)
        lnv = sb(es_p1, "lnv", [128, 1], F32)
        r1 = sb(es_p1, "r1", [128, 1], F32)
        nmr = sb(es_p1, "nmr", [128, 1], F32)
        lg8 = sb(es_p1, "lg8", [128, 8], F32)
        m8 = sb(es_p1, "m8", [128, 8], F32)
        i8u = sb(es_p1, "i8u", [128, 8], U32)
        e8x = sb(es_p1, "e8x", [128, 8], F32)
        zsum = sb(es_p1, "zsum", [128, 1], F32)
        rz = sb(es_p1, "rz", [128, 1], F32)
        negm1 = sb(es_p1, "negm1", [128, 1], F32)
        p12e = sb(es_p1, "p12e", [128, 2], F32)
        hmaskf = sb(es_p1, "hmaskf", [128, 8], F32)
        scr8 = sb(es_p1, "scr8", [128, 8], F32)
        pos12 = sb(es_p1, "pos12", [128, 2], F32)
        keep12 = sb(es_p1, "keep12", [128, 2], F32)
        slotf = sb(es_p1, "slotf", [128, 1], F32)
        tmp1 = sb(es_p1, "tmp1", [128, 1], F32)
        payload = sb(es_p1, "payload", [128, 2 * TPB], U32)
        idxa_u = sb(es_p1, "idxa_u", [128, 2 * TPB], U32)
        tokf = sb(es_p1, "tokf", [128, 1], F32)

        wpre1 = sb(es_p1, "wpre1", [128, 2 * 2048], BF16)
        wpre2 = sb(es_p1, "wpre2", [128, 2 * C], BF16)

        psA = [ps(es_p1, f"psA{i}", [128, 512], F32) for i in range(2)]
        psS2 = [ps(es_p1, "psS2a", [128, 256], F32), ps(es_p1, "psS2b", [128, 256], F32)]
        psR = ps(es_p1, "psR", [128, 8], F32)
        psTr = ps(es_p1, "psTr", [128, 128], F32)

        def v_p1init(v):
            v.memset(routb[:, :], 0.0)
        vec(v_p1init)

        def s_x0(sy):
            sy.dma_start(out=xb[:, 0:C], in_=x_d[0:128, :]).then_inc(sem_xp[0].h, 16)
        block.sync(s_x0)
        sem_xp[0].inc(16)

        vx_done = [0] * NTIL
        vz_done = [0] * NTIL
        tr_done = [0] * NTIL
        vh_done = [0] * NTIL
        vsm1_done = [0] * NTIL
        ht_evac = [0] * NTIL
        t_att_end = [0] * NGRP
        v_s2_done = [0] * NGRP
        g_sw_done = [0] * NGRP
        endb_v_done = [0] * nb
        scat_g_done = [0] * nb

        def emit_tile(t):
            b = t // TPB
            tt = t % TPB
            gg = t // GRP
            tig = t % GRP
            gp = gg % 2
            xslot = t % 2
            xnslot = t % XN_RING

            def v_ln1(v):
                v.wait_ge(sem_xp[t % 2].h, 16 * (t // 2 + 1))
                if t >= XN_RING:
                    v.wait_ge(sem_v.h, v_s2_done[(t - XN_RING) // GRP])
                v.engine_nop()
                v.bn_stats(out=bns[:, 0:6], in_=xb[:, xslot * C:xslot * C + 512])
                v.bn_stats(out=bns[:, 6:12],
                           in_=xb[:, xslot * C + 512:xslot * C + 1024])
                v.bn_aggr(out=bn2[:], in_=bns[:])
                v.engine_nop().then_inc(sem_v.h, 1)
            vec(v_ln1)
            sem_v.inc(1)

            def s_rs1(s):
                s.wait_ge(sem_v.h, sem_v.n)
                s.activation(out=lnv[:], in_=bn2[:, 1:2], func=AF.Ln,
                             bias=eps_ap[:], scale=1.0)
                s.activation(out=r1[:], in_=lnv[:], func=AF.Exp, bias=0.0,
                             scale=-0.5).then_inc(sem_s.h, 1)
            sca(s_rs1)
            sem_s.inc(1)

            def v_ln2(v):
                v.wait_ge(sem_s.h, sem_s.n)
                v.tensor_scalar(out=nmr[:], in0=bn2[:, 0:1], scalar1=r1[:],
                                scalar2=-1.0, op0=OP.mult, op1=OP.mult)
                v.tensor_scalar(out=xnf[:, xslot * C:(xslot + 1) * C],
                                in0=xb[:, xslot * C:(xslot + 1) * C],
                                scalar1=r1[:], scalar2=nmr[:],
                                op0=OP.mult, op1=OP.add).then_inc(sem_v.h, 1)
                v.tensor_copy(out=xnhi[:, xnslot * C:(xnslot + 1) * C],
                              in_=xnf[:, xslot * C:(xslot + 1) * C])
                v.engine_nop()
                v.bn_stats(out=bns[:, 0:6], in_=xnf[:, xslot * C:xslot * C + 512])
                v.bn_stats(out=bns[:, 6:12],
                           in_=xnf[:, xslot * C + 512:xslot * C + 1024])
                v.bn_aggr(out=bn2i[:], in_=bns[:])
                v.engine_nop().then_inc(sem_v.h, 1)
            vec(v_ln2)
            sem_v.inc(1)
            vx_done[t] = sem_v.n
            sem_v.inc(1)
            ln2_done = sem_v.n

            def s_rs2(s):
                s.wait_ge(sem_v.h, sem_v.n)
                s.activation(out=lnv[:], in_=bn2i[:, 1:2], func=AF.Ln,
                             bias=eps_ap[:], scale=1.0)
                s.activation(out=r1[:], in_=lnv[:], func=AF.Exp, bias=0.0,
                             scale=-0.5)
                s.activation(out=sd_ring[:, tig:tig + 1], in_=lnv[:], func=AF.Exp,
                             bias=0.0, scale=0.5).then_inc(sem_s.h, 1)
            sca(s_rs2)
            sem_s.inc(1)

            def v_ln3(v):
                v.wait_ge(sem_s.h, sem_s.n)
                v.tensor_copy(out=mu_ring[:, tig:tig + 1], in_=bn2i[:, 0:1])
                v.tensor_scalar(out=nmr[:], in0=bn2i[:, 0:1], scalar1=r1[:],
                                scalar2=-1.0, op0=OP.mult, op1=OP.mult)
                v.tensor_scalar(out=zf[:], in0=xnf[:, xslot * C:(xslot + 1) * C],
                                scalar1=r1[:], scalar2=nmr[:], op0=OP.mult,
                                op1=OP.add)
                v.tensor_copy(out=zhi[:, xslot * C:(xslot + 1) * C], in_=zf[:])
                v.tensor_sub(out=zlo[:, xslot * C:(xslot + 1) * C], in0=zf[:],
                             in1=zhi[:, xslot * C:(xslot + 1) * C]
                             ).then_inc(sem_v.h, 1)
            vec(v_ln3)
            sem_v.inc(1)
            vz_done[t] = sem_v.n

            def g_xn(g, ln2_done=None):
                g.wait_ge(sem_v.h, ln2_done)
                if sem_tr.n:
                    g.wait_ge(sem_tr.h, sem_tr.n)
                g.dma_start(out=xn_d[t * 128:(t + 1) * 128, :],
                            in_=xnhi[:, xnslot * C:(xnslot + 1) * C]
                            ).then_inc(sem_gxn.h, 16)
            block.gpsimd(lambda g, l=ln2_done: g_xn(g, l))
            sem_gxn.inc(16)

            def g_wpre(g):
                if sem_tr.n:
                    g.wait_ge(sem_tr.h, sem_tr.n)
                n1 = 128 // NTIL if 128 % NTIL == 0 else (128 + NTIL - 1) // NTIL
                for half in range(n1):
                    u = n1 * t + half
                    if u >= 128:
                        break
                    e1, cc, hf = u // 16, (u // 2) % 8, u % 2
                    sl = u % 2
                    if u >= 2:
                        g.wait_ge(sem_ws1[sl].h, sem_ws1[sl].n)
                    g.dma_start(out=wpre1[:, sl * 2048:(sl + 1) * 2048],
                                in_=w1_d[e1 * C + cc * 128:e1 * C + (cc + 1) * 128,
                                         hf * 2048:(hf + 1) * 2048],
                                max_dma_last_dim=8192).then_inc(sem_wp1[sl].h, 16)
                    g.wait_ge(sem_wp1[sl].h, sem_wp1[sl].n + 16)
                    outap = w1t_d[(e1 * 8 + hf * 4) * 128:(e1 * 8 + hf * 4 + 4) * 128,
                                  cc * 512:(cc + 1) * 512].rearrange(
                        "(hq p) jj -> p hq jj", p=128)
                    g.dma_start(out=outap,
                                in_=wpre1[:, sl * 2048:(sl + 1) * 2048].rearrange(
                                    "p (hq jj) -> p hq jj", hq=4)
                                ).then_inc(sem_ws1[sl].h, 16)
                    sem_wp1[sl].inc(16)
                    sem_ws1[sl].inc(16)
                n2 = 256 // NTIL if 256 % NTIL == 0 else (256 + NTIL - 1) // NTIL
                for i in range(n2):
                    u2 = t * n2 + i
                    if u2 >= 256:
                        break
                    e2, k2 = u2 // 32, u2 % 32
                    s2 = u2 % 2
                    if u2 >= 2:
                        g.wait_ge(sem_ws2[s2].h, sem_ws2[s2].n)
                    g.dma_start(out=wpre2[:, s2 * C:(s2 + 1) * C],
                                in_=w2_d[e2 * H + k2 * 128:
                                         e2 * H + (k2 + 1) * 128, :],
                                max_dma_last_dim=8192).then_inc(sem_wp2[s2].h, 16)
                    g.wait_ge(sem_wp2[s2].h, sem_wp2[s2].n + 16)
                    g.dma_start(out=w2t_d[e2 * H + k2 * 128:
                                          e2 * H + (k2 + 1) * 128, :],
                                in_=wpre2[:, s2 * C:(s2 + 1) * C]
                                ).then_inc(sem_ws2[s2].h, 16)
                    sem_wp2[s2].inc(16)
                    sem_ws2[s2].inc(16)
            block.gpsimd(g_wpre)

            def s_tile(sy):
                if t + 1 < NTIL:
                    nxt = (t + 1) % 2
                    if t >= 1:
                        sy.wait_ge(sem_v.h, vx_done[t])
                    sy.dma_start(out=xb[:, nxt * C:(nxt + 1) * C],
                                 in_=x_d[(t + 1) * 128:(t + 2) * 128, :]
                                 ).then_inc(sem_xp[(t + 1) % 2].h, 16)
                sy.wait_ge(sem_v.h, vz_done[t])
                sy.wait_ge(sem_gxn.h, sem_gxn.n)
                for smq in (sem_wp1, sem_ws1, sem_wp2, sem_ws2, sem_gs):
                    for par in range(2):
                        if smq[par].n:
                            sy.wait_ge(smq[par].h, smq[par].n)
                if sem_gsc.n:
                    sy.wait_ge(sem_gsc.h, sem_gsc.n)
                sy.wait_ge(sem_xp[0].h, sem_xp[0].n)
                sy.wait_ge(sem_xp[1].h, sem_xp[1].n)
                if sem_tr.n:
                    sy.wait_ge(sem_tr.h, sem_tr.n)
                if gg >= 2:
                    sy.wait_ge(sem_t.h, t_att_end[gg - 2])
                outhi = zThi[:, gp * NCC * 512:(gp + 1) * NCC * 512].rearrange(
                    "p (cc q) -> p cc q", cc=NCC)[:, :, tig * 128:(tig + 1) * 128]
                sy.dma_start(out=outhi, in_=zhi[:, xslot * C:(xslot + 1) * C],
                             transpose=True).then_inc(sem_tr.h, 16)
                sy.wait_ge(sem_tr.h, sem_tr.n + 16)
                outlo = zTlo[:, gp * NCC * 512:(gp + 1) * NCC * 512].rearrange(
                    "p (cc q) -> p cc q", cc=NCC)[:, :, tig * 128:(tig + 1) * 128]
                sy.dma_start(out=outlo, in_=zlo[:, xslot * C:(xslot + 1) * C],
                             transpose=True).then_inc(sem_tr.h, 16)
            block.sync(s_tile)
            if t + 1 < NTIL:
                sem_xp[(t + 1) % 2].inc(16)
            sem_tr.inc(32)
            tr_done[t] = sem_tr.n

            def t_router(t_):
                t_.wait_ge(sem_tr.h, tr_done[t])
                if t >= 1:
                    t_.wait_ge(sem_v.h, vsm1_done[t - 1])
                last = None
                n = 0
                for cc in range(NCC):
                    base = gp * NCC * 512 + cc * 512 + tig * 128
                    for lhs, rhs in ((zThi, wr_hi), (zThi, wr_lo), (zTlo, wr_hi)):
                        n += 1
                        last = t_.matmul(psR[:], lhs[:, base:base + 128],
                                         rhs[:, cc * 8:cc * 8 + 8],
                                         start=(n == 1), stop=(n == 3 * NCC))
                last.then_inc(sem_t.h, 1)
            block.tensor(t_router)
            sem_t.inc(1)
            rd_t = sem_t.n

            def v_sm1(v):
                v.wait_ge(sem_t.h, rd_t)
                v.tensor_scalar(out=lg8[:], in0=psR[:],
                                scalar1=sd_ring[:, tig:tig + 1], scalar2=None,
                                op0=OP.mult)
                v.scalar_tensor_tensor(out=lg8[:], in0=wrsum_bc[:],
                                       scalar=mu_ring[:, tig:tig + 1], in1=lg8[:],
                                       op0=OP.mult, op1=OP.add)
                v.engine_nop()
                v.max(out=m8[:], in_=lg8[:])
                v.engine_nop()
                v.max_index(out=i8u[:], in_max=m8[:], in_values=lg8[:])
                v.engine_nop()
                v.tensor_scalar(out=negm1[:], in0=m8[:, 0:1], scalar1=-1.0,
                                scalar2=None, op0=OP.mult).then_inc(sem_v.h, 1)
            vec(v_sm1)
            sem_v.inc(1)
            vsm1_done[t] = sem_v.n

            def s_exp(s):
                s.wait_ge(sem_v.h, sem_v.n)
                s.activation(out=e8x[:], in_=lg8[:], func=AF.Exp, bias=negm1[:],
                             scale=1.0, accum_out=zsum[:])
                s.activation(out=p12e[:], in_=m8[:, 0:2], func=AF.Exp, bias=negm1[:],
                             scale=1.0).then_inc(sem_s.h, 1)
            sca(s_exp)
            sem_s.inc(1)

            def v_sm2(v):
                v.wait_ge(sem_s.h, sem_s.n)
                v.engine_nop()
                v.reciprocal(out=rz[:], in_=zsum[:])
                v.engine_nop()
                v.tensor_scalar(out=p12_b[:, tt * 2:tt * 2 + 2], in0=p12e[:],
                                scalar1=rz[:], scalar2=None, op0=OP.mult)
                v.tensor_copy(out=e12_b[:, tt * 2:tt * 2 + 1], in_=i8u[:, 0:1])
                v.tensor_copy(out=e12_b[:, tt * 2 + 1:tt * 2 + 2], in_=i8u[:, 1:2])
                v.tensor_scalar(out=is1_b[:, tt * 8:tt * 8 + 8], in0=iota8[:],
                                scalar1=e12_b[:, tt * 2:tt * 2 + 1], scalar2=None,
                                op0=OP.is_equal)
                v.tensor_scalar(out=is2_b[:, tt * 8:tt * 8 + 8], in0=iota8[:],
                                scalar1=e12_b[:, tt * 2 + 1:tt * 2 + 2],
                                scalar2=None, op0=OP.is_equal)
                v.tensor_add(out=hmaskf[:], in0=is1_b[:, tt * 8:tt * 8 + 8],
                             in1=is2_b[:, tt * 8:tt * 8 + 8]).then_inc(sem_v.h, 1)
            vec(v_sm2)
            sem_v.inc(1)
            vh_done[t] = sem_v.n

            def t_hm(t_):
                t_.wait_ge(sem_v.h, vh_done[t])
                if tt:
                    t_.wait_ge(sem_v.h, ht_evac[t - 1])
                elif b:
                    t_.wait_ge(sem_v.h, endb_v_done[b - 1])
                t_.transpose(psTr[0:8, 0:128], hmaskf[:], ident_f[:]
                             ).then_inc(sem_t.h, 1)
            block.tensor(t_hm)
            sem_t.inc(1)
            hm_t = sem_t.n

            def v_hm(v):
                v.wait_ge(sem_t.h, hm_t)
                v.tensor_copy(out=routb[0:8, tt * 128:(tt + 1) * 128],
                              in_=psTr[0:8, 0:128]).then_inc(sem_v.h, 1)
            vec(v_hm)
            sem_v.inc(1)
            ht_evac[t] = sem_v.n

        def emit_attn_group(gg):
            gp = gg % 2
            sp = gg % 2

            def v_zg(v):
                v.wait_ge(sem_tr.h, tr_done[gg * GRP + GRP - 1])
                if gg >= 1:
                    v.wait_ge(sem_t.h, t_att_end[gg - 1])
                for cc in range(NCC):
                    v.tensor_scalar(
                        out=zgT[:, cc * 512:(cc + 1) * 512],
                        in0=zThi[:, gp * NCC * 512 + cc * 512:
                                 gp * NCC * 512 + (cc + 1) * 512],
                        scalar1=g_sb[:, cc:cc + 1], scalar2=None, op0=OP.mult)
                v.engine_nop().then_inc(sem_v.h, 1)
            vec(v_zg)
            sem_v.inc(1)
            vzg = sem_v.n

            t0 = sem_t.n
            v0 = sem_v.n
            s0 = sem_s.n

            def t_attn(t_):
                t_.wait_ge(sem_v.h, vzg)
                t_.wait_ge(sem_s.h, s0)
                if gg >= 1:
                    t_.wait_ge(sem_v.h, v_s2_done[gg - 1])
                for h in range(QH):
                    for si in range(NSI):
                        q = si // 8
                        k = h * 32 + si
                        if k >= 2:
                            t_.wait_ge(sem_s.h, s0 + k - 1)
                        last = None
                        for dc in range(2):
                            kb = (2 * q + dc) * C + (si % 8) * 128
                            last = t_.matmul(
                                psA[si % 2][:], wfcT[:, kb:kb + 128],
                                zgT[:, (2 * h + dc) * 512:(2 * h + dc + 1) * 512],
                                start=(dc == 0), stop=(dc == 1))
                        last.then_inc(sem_t.h, 1)
                    for tc in range(GRP):
                        if h == 0 and tc < 2:
                            pass
                        elif tc < 2:
                            t_.wait_ge(sem_v.h, v0 + h * 36 + tc - 1)
                        else:
                            t_.wait_ge(sem_v.h, v0 + h * 36 + 31 + tc)
                        t_.wait_ge(sem_s.h, s0 + h * 32 + 32)
                        last = None
                        for si in range(NSI):
                            q = si // 8
                            last = t_.matmul(
                                psS2[tc % 2][:],
                                pT[:, si * 512 + tc * 128:si * 512 + tc * 128 + 128],
                                wprojT[:, (si % 8) * C + q * 256:
                                       (si % 8) * C + q * 256 + 256],
                                start=(si == 0), stop=(si == NSI - 1))
                        last.then_inc(sem_t.h, 1)
            block.tensor(t_attn)
            sem_t.inc(QH * (NSI + GRP))
            t_att_end[gg] = sem_t.n

            def v_attn(v):
                if gg >= 2:
                    v.wait_ge(sem_gs[gg % 2].h, g_sw_done[gg - 2])
                for h in range(QH):
                    for si in range(NSI):
                        k = h * 36 + si if False else h * 32 + si
                        v.wait_ge(sem_t.h, t0 + h * 36 + si + 1)
                        v.tensor_scalar(out=psA[si % 2][:], in0=psA[si % 2][:],
                                        scalar1=bk_sb[:, si * 4 + h:si * 4 + h + 1],
                                        scalar2=0.0, op0=OP.add, op1=OP.max
                                        ).then_inc(sem_v.h, 1)
                    for tc in range(GRP):
                        v.wait_ge(sem_t.h, t0 + h * 36 + 32 + tc + 1)
                        xsl = ((gg * GRP + tc) % XN_RING) * C + h * 256
                        v.scalar_tensor_tensor(
                            out=s_stage[:, (sp * GRP + tc) * C + h * 256:
                                        (sp * GRP + tc) * C + h * 256 + 256],
                            in0=psS2[tc % 2][:],
                            scalar=0.0, in1=xnhi[:, xsl:xsl + 256],
                            op0=OP.add, op1=OP.add).then_inc(sem_v.h, 1)
            vec(v_attn)
            sem_v.inc(QH * 36)
            v_s2_done[gg] = sem_v.n

            def s_attn(s):
                for h in range(QH):
                    for si in range(NSI):
                        s.wait_ge(sem_v.h, v0 + h * 36 + si + 1)
                        s.activation(out=pT[:, si * 512:(si + 1) * 512],
                                     in_=psA[si % 2][:], func=AF.Square
                                     ).then_inc(sem_s.h, 1)
            sca(s_attn)
            sem_s.inc(QH * NSI)

            def g_sw(g):
                for tc in range(GRP):
                    g.wait_ge(sem_v.h, v0 + 3 * 36 + 32 + tc + 1)
                    if sem_tr.n:
                        g.wait_ge(sem_tr.h, sem_tr.n)
                    tg = gg * GRP + tc
                    g.dma_start(out=s_d[tg * 128:(tg + 1) * 128, :],
                                in_=s_stage[:, (sp * GRP + tc) * C:
                                            (sp * GRP + tc + 1) * C]
                                ).then_inc(sem_gs[gg % 2].h, 16)
            block.gpsimd(g_sw)
            sem_gs[gg % 2].inc(16 * GRP)
            g_sw_done[gg] = sem_gs[gg % 2].n

        def emit_endb(b):
            def v_scan(v):
                if b >= 1:
                    v.wait_ge(sem_gsc.h, scat_g_done[b - 1])
                v.engine_nop()
                v.tensor_tensor_scan(out=cex, data0=hT, data1=hT,
                                     initial=0.0, op0=OP.add, op1=OP.max)
                v.engine_nop()
                v.tensor_sub(out=cex, in0=cex, in1=hT).then_inc(sem_v.h, 1)
            vec(v_scan)
            sem_v.inc(1)
            vscan = sem_v.n
            vch = [0] * TPB

            for tt in range(TPB):
                t = b * TPB + tt

                def t_bt(t_, tt=tt):
                    t_.wait_ge(sem_v.h, vscan if tt == 0 else sem_v.n)
                    t_.transpose(psTr[:, 0:8], routb[0:8, T_SEQ + tt * 128:T_SEQ + (tt + 1) * 128],
                                 ident_f[0:8, 0:8]).then_inc(sem_t.h, 1)
                block.tensor(t_bt)
                sem_t.inc(1)
                bt_t = sem_t.n

                def v_chain(v, tt=tt, t=t, b=b, bt_t=bt_t):
                    v.wait_ge(sem_t.h, bt_t)
                    v.tensor_mul(out=scr8[:], in0=psTr[:, 0:8],
                                 in1=is1_b[:, tt * 8:tt * 8 + 8])
                    v.reduce_sum(out=pos12[:, 0:1], in_=scr8[:],
                                 axis=mybir.AxisListType.X)
                    v.tensor_mul(out=scr8[:], in0=psTr[:, 0:8],
                                 in1=is2_b[:, tt * 8:tt * 8 + 8])
                    v.reduce_sum(out=pos12[:, 1:2], in_=scr8[:],
                                 axis=mybir.AxisListType.X)
                    v.tensor_scalar(out=keep12[:], in0=pos12[:], scalar1=float(CAP),
                                    scalar2=None, op0=OP.is_lt)
                    v.tensor_mul(out=c12_all[:, t * 2:t * 2 + 2],
                                 in0=p12_b[:, tt * 2:tt * 2 + 2], in1=keep12[:])
                    v.tensor_scalar(out=tokf[:], in0=iotap[:], scalar1=1.0,
                                    scalar2=float(t * 128), op0=OP.mult, op1=OP.add)
                    v.tensor_copy(out=payload[:, tt * 2:tt * 2 + 1], in_=tokf[:])
                    v.tensor_copy(out=payload[:, tt * 2 + 1:tt * 2 + 2], in_=tokf[:])
                    for k in range(2):
                        ek = e12_b[:, tt * 2 + k:tt * 2 + k + 1]
                        pk = pos12[:, k:k + 1]
                        kk = keep12[:, k:k + 1]
                        v.tensor_scalar(out=slotf[:], in0=ek,
                                        scalar1=float(nb * CAP),
                                        scalar2=float(b * CAP - ZROW),
                                        op0=OP.mult, op1=OP.add)
                        v.tensor_add(out=slotf[:], in0=slotf[:], in1=pk)
                        v.tensor_mul(out=slotf[:], in0=slotf[:], in1=kk)
                        v.tensor_scalar(out=slotf[:], in0=slotf[:],
                                        scalar1=float(ZROW), scalar2=None,
                                        op0=OP.add)
                        v.tensor_copy(out=yidx_all[:, t * 2 + k:t * 2 + k + 1],
                                      in_=slotf[:])
                        v.tensor_scalar(out=slotf[:], in0=ek,
                                        scalar1=float(nb * TSTRIDE),
                                        scalar2=float(b * TSTRIDE - 1048576),
                                        op0=OP.mult, op1=OP.add)
                        v.tensor_add(out=slotf[:], in0=slotf[:], in1=pk)
                        v.tensor_mul(out=slotf[:], in0=slotf[:], in1=kk)
                        v.tensor_scalar(out=slotf[:], in0=slotf[:],
                                        scalar1=1048576.0, scalar2=None, op0=OP.add)
                        v.tensor_copy(out=idxa_u[:, tt * 2 + k:tt * 2 + k + 1],
                                      in_=slotf[:])
                    v.engine_nop().then_inc(sem_v.h, 1)
                vec(v_chain)
                sem_v.inc(1)
                vch[tt] = sem_v.n

            def g_scat(g, b=b):
                g.wait_ge(sem_v.h, vch[TPB - 1])
                if sem_tr.n:
                    g.wait_ge(sem_tr.h, sem_tr.n)
                with g.register("bnd%d" % b) as breg:
                    g.reg_mov(breg, TBL_ROWS - 1)
                    bval = g.snap(breg)
                    for tt in range(TPB):
                        for k in range(2):
                            g.indirect_dma_start(
                                out=tbl_d[:, :],
                                out_offset=IndirectOffsetOnAxis(
                                    idxa_u[:, tt * 2 + k:tt * 2 + k + 1], 0),
                                in_=payload[:, tt * 2:tt * 2 + 2],
                                in_offset=None,
                                bounds_check=bval, oob_is_err=False
                            ).then_inc(sem_gsc.h, 16)
            block.gpsimd(g_scat)
            sem_gsc.inc(32 * TPB)
            endb_v_done[b] = vch[TPB - 1]
            scat_g_done[b] = sem_gsc.n

        for gg in range(NGRP):
            for tig in range(GRP):
                emit_tile(gg * GRP + tig)
            if gg >= 1:
                emit_attn_group(gg - 1)
            if (gg + 1) % GPB == 0:
                emit_endb(gg // GPB)
        emit_attn_group(NGRP - 1)

        _dma_drain()
        _phase_barrier()
        es_p1.close()

        es_p2 = ExitStack()
        w1c = sb(es_p2, "w1c", [128, 2 * H], BF16)
        w2c = sb(es_p2, "w2c", [128, 2 * H], BF16)
        bufT = sb(es_p2, "bufT", [128, 2 * E * NCC * TSTRIDE], BF16)
        tokg = sb(es_p2, "tokg", [128, 2 * C], BF16)
        idxg = sb(es_p2, "idxg", [128, 2 * E * 3], U32)
        hbf = sb(es_p2, "hbf", [128, 2 * CAP], BF16)
        y_stage = sb(es_p2, "y_stage", [128, 3 * 3 * C], BF16)
        y0t = sb(es_p2, "y0t", [128, 2 * C], BF16)
        y1t = sb(es_p2, "y1t", [128, 2 * C], BF16)
        sin_t = sb(es_p2, "sin_t", [128, 2 * C], BF16)
        xr_t = sb(es_p2, "xr_t", [128, 2 * C], F32)
        accf = sb(es_p2, "accf", [128, C], F32)
        gsil = accf
        outst = sb(es_p2, "outst", [128, 2 * C], F32)

        b2_bf = sb(es_p2, "b2_bf", [1, E * C], BF16)
        psH = [ps(es_p2, f"psH{i}", [128, CAP], F32) for i in range(2)]
        psY = [ps(es_p2, "psY0", [128, C], F32), ps(es_p2, "psY1", [128, C], F32),
               ps(es_p2, "psY2", [64, C], F32)]
        NSZ = (128, 128, 64)
        NOF = (0, 128, 256)

        def g_b2ld(g):
            g.dma_start(out=b2_bf[0:1, :], in_=b2_d[:, :],
                        max_dma_last_dim=8192).then_inc(sem_b2.h, 16)
        block.gpsimd(g_b2ld)
        sem_b2.inc(16)
        b2_ready = sem_b2.n

        buf_ready = {}
        te_be = {}
        gctr = [0]

        def emit_gather_e(b, e, prev_marker):
            bp = b % 2

            def g_gath(g):
                g.wait_ge(sem_gsc.h, scat_g_done[b])
                g.wait_ge(sem_gxn.h, sem_gxn.n)
                if prev_marker is not None:
                    g.wait_ge(sem_t.h, prev_marker)
                if sem_tr.n:
                    g.wait_ge(sem_tr.h, sem_tr.n)
                r0 = (e * nb + b) * TSTRIDE
                g.dma_start(out=idxg[:, (bp * E + e) * 3:(bp * E + e) * 3 + 3],
                            in_=tbl_d[r0:r0 + TSTRIDE, 0:1].rearrange(
                                "(a p) c -> p (a c)", p=128),
                            allow_slow_non_contiguous=True
                            ).then_inc(sem_gix.h, 16)
                g.wait_ge(sem_gix.h, sem_gix.n + 16)
                for a in range(3):
                    j = gctr[0] + a
                    if j >= 2:
                        g.wait_ge(sem_tr.h, tr_base_p2[0] + 16 * (j - 1))
                    g.indirect_dma_start(
                        out=tokg[:, (j % 2) * C:(j % 2) * C + C],
                        out_offset=None, in_=xn_d[:, :],
                        in_offset=IndirectOffsetOnAxis(
                            idxg[:, (bp * E + e) * 3 + a:
                                 (bp * E + e) * 3 + a + 1], 0)
                    ).then_inc(sem_gg[j % 2].h, 16)
            block.gpsimd(g_gath)
            sem_gix.inc(16)
            gg0 = [sem_gg[0].n, sem_gg[1].n]
            for a in range(3):
                sem_gg[(gctr[0] + a) % 2].inc(16)

            def s_btr(sy):
                done = {0: gg0[0], 1: gg0[1]}
                for a in range(3):
                    j = gctr[0] + a
                    par = j % 2
                    done[par] += 16
                    sy.wait_ge(sem_gg[par].h, done[par])
                    sy.wait_ge(sem_tr.h, tr_base_p2[0] + 16 * j)
                    base = (bp * E + e) * NCC * TSTRIDE
                    outap = bufT[:, base:base + NCC * TSTRIDE].rearrange(
                        "p (cc n) -> p cc n", cc=NCC)[:, :, a * 128:(a + 1) * 128]
                    sy.dma_start(out=outap,
                                 in_=tokg[:, (j % 2) * C:(j % 2) * C + C],
                                 transpose=True).then_inc(sem_tr.h, 16)
            block.sync(s_btr)
            sem_tr.inc(16 * 3)
            gctr[0] += 3
            buf_ready[(b, e)] = sem_tr.n

        prev_yv = [0]
        t0_prev_exp = [None]
        tr_base_p2 = [sem_tr.n]

        for e in range(E):
            emit_gather_e(0, e, None)

        for b in range(nb):
            bp = b % 2
            for e in range(E):
                eg = b * E + e
                t0 = sem_t.n
                s0 = sem_s.n
                w0 = [sem_wq[0].n, sem_wq[1].n]
                base_buf = (bp * E + e) * NCC * TSTRIDE
                t0p = t0_prev_exp[0]

                def g_wload(g, e=e, t0=t0, t0p=t0p):
                    for hq in range(8):
                        q = hq % 2
                        if hq >= 2:
                            g.wait_ge(sem_t.h, t0 + 8 * hq - 6)
                        elif t0p is not None:
                            g.wait_ge(sem_t.h, t0p + 8 * hq + 58)
                        if sem_tr.n:
                            g.wait_ge(sem_tr.h, sem_tr.n)
                        g.dma_start(
                            out=w1c[:, q * H:(q + 1) * H],
                            in_=w1t_d[(e * 8 + hq) * 128:(e * 8 + hq + 1) * 128, :]
                            ).then_inc(sem_wq[q].h, 16)
                        g.dma_start(
                            out=w2c[:, q * H:(q + 1) * H].rearrange(
                                "p (hcq c) -> p hcq c", hcq=4),
                            in_=w2t_d[e * H + hq * 512:e * H + (hq + 1) * 512, :]
                            .rearrange("(hcq p) c -> p hcq c", p=128)
                            ).then_inc(sem_wq[q].h, 16)
                block.gpsimd(g_wload)
                sem_wq[0].inc(32 * 4)
                sem_wq[1].inc(32 * 4)

                vprev = prev_yv[0]

                def t_exp(t_, b=b, e=e, t0=t0, s0=s0, w0=w0,
                          base_buf=base_buf, vprev=vprev):
                    t_.wait_ge(sem_tr.h, buf_ready[(b, e)])
                    t_.wait_ge(sem_b2.h, b2_ready)
                    if vprev:
                        t_.wait_ge(sem_v.h, vprev)
                    last = None
                    for ncx in range(3):
                        for half in range(2):
                            last = t_.matmul(
                                psY[ncx][:, half * 512:(half + 1) * 512],
                                ones_row_bf[0:1, 0:NSZ[ncx]],
                                b2_bf[0:1, e * C + half * 512:
                                      e * C + (half + 1) * 512],
                                start=True, stop=False)
                    last.then_inc(sem_t.h, 1)

                    def l1(hc):
                        hq = hc // 4
                        q = hq % 2
                        t_.wait_ge(sem_wq[q].h, w0[q] + 32 * (hq // 2 + 1))
                        if hc >= 2:
                            t_.wait_ge(sem_s.h, s0 + hc - 1)
                        last = None
                        for cc in range(NCC):
                            last = t_.matmul(
                                psH[hc % 2][:],
                                w1c[:, q * H + cc * 512 + (hc % 4) * 128:
                                     q * H + cc * 512 + (hc % 4) * 128 + 128],
                                bufT[:, base_buf + cc * TSTRIDE:
                                     base_buf + cc * TSTRIDE + CAP],
                                start=(cc == 0), stop=(cc == NCC - 1))
                        last.then_inc(sem_t.h, 1)

                    def l2(hc):
                        hq = hc // 4
                        q = hq % 2
                        t_.wait_ge(sem_s.h, s0 + hc + 1)
                        last = None
                        for ncx in range(3):
                            for half in range(2):
                                last = t_.matmul(
                                    psY[ncx][:, half * 512:(half + 1) * 512],
                                    hbf[:, (hc % 2) * CAP + NOF[ncx]:
                                        (hc % 2) * CAP + NOF[ncx] + NSZ[ncx]],
                                    w2c[:, q * H + (hc % 4) * 1024 + half * 512:
                                        q * H + (hc % 4) * 1024 + (half + 1) * 512],
                                    start=False, stop=(hc == NHC - 1))
                        last.then_inc(sem_t.h, 1)

                    l1(0)
                    for hc in range(NHC):
                        if hc + 1 < NHC:
                            l1(hc + 1)
                        l2(hc)
                block.tensor(t_exp)
                sem_t.inc(1 + 2 * NHC)
                te_end = sem_t.n
                te_be[(b, e)] = te_end
                t0_prev_exp[0] = t0

                def s_relu(x, e=e, t0=t0):
                    for hc in range(NHC):
                        if hc == 0:
                            x.wait_ge(sem_t.h, t0 + 2)
                        else:
                            x.wait_ge(sem_t.h, t0 + 2 * hc + 2)
                        if hc >= 2:
                            x.wait_ge(sem_t.h, t0 + 2 * (hc - 2) + 4)
                        x.activation(out=hbf[:, (hc % 2) * CAP:(hc % 2) * CAP + CAP],
                                     in_=psH[hc % 2][:], func=AF.Relu,
                                     bias=b1_sb[:, e * NHC + hc:e * NHC + hc + 1],
                                     scale=1.0).then_inc(sem_s.h, 1)
                sca(s_relu)
                sem_s.inc(NHC)

                def v_yevac(v, e=e, eg=eg, te=te_end):
                    v.wait_ge(sem_t.h, te)
                    if eg >= 2:
                        v.wait_ge(sem_yfree.h, eg)
                    for ncx in range(3):
                        v.tensor_copy(out=y_stage[0:NSZ[ncx],
                                                  (eg % 3) * 3 * C + ncx * C:
                                                  (eg % 3) * 3 * C + (ncx + 1) * C],
                                      in_=psY[ncx][0:NSZ[ncx], :])
                    v.engine_nop().then_inc(sem_v.h, 1)
                vec(v_yevac)
                sem_v.inc(1)
                prev_yv[0] = sem_v.n
                yv = sem_v.n

                def s_ywr(sy, b=b, e=e, eg=eg, yv=yv):
                    if sem_yp[eg % 2].n:
                        sy.wait_ge(sem_yp[eg % 2].h, sem_yp[eg % 2].n)
                    sy.sem_inc(sem_yfree.h, 1)
                    sy.wait_ge(sem_v.h, yv)
                    if sem_tr.n:
                        sy.wait_ge(sem_tr.h, sem_tr.n)
                    r0 = (e * nb + b) * CAP
                    sy.dma_start(out=y_d[r0:r0 + 256, :].rearrange(
                        "(k p) c -> p k c", p=128),
                        in_=y_stage[:, (eg % 3) * 3 * C:(eg % 3) * 3 * C + 2 * C]
                        .rearrange("p (k c) -> p k c", k=2)
                        ).then_inc(sem_yp[eg % 2].h, 16)
                    sy.dma_start(out=y_d[r0 + 256:r0 + CAP, :],
                                 in_=y_stage[0:64, (eg % 3) * 3 * C + 2 * C:
                                             (eg % 3) * 3 * C + 3 * C]
                                 ).then_inc(sem_yp[eg % 2].h, 16)
                block.sync(s_ywr)
                sem_yp[eg % 2].inc(32)
                sem_yfree.inc(1)

                if b + 1 < nb:
                    emit_gather_e(b + 1, e, te_be.get((b - 1, e)))

            def s_ycert(sy):
                sy.wait_ge(sem_yp[0].h, sem_yp[0].n)
                sy.wait_ge(sem_yp[1].h, sem_yp[1].n)
                sy.sem_inc(sem_ydn.h, 1)
            block.sync(s_ycert)
            sem_ydn.inc(1)

            for tt in range(TPB):
                t = b * TPB + tt
                tp = tt % 2

                def g_yg(g, t=t, tp=tp, b=b):
                    g.wait_ge(sem_ydn.h, b + 1)
                    g.wait_ge(sem_v.h, sem_v.n)
                    if sem_tr.n:
                        g.wait_ge(sem_tr.h, sem_tr.n)
                    g.indirect_dma_start(
                        out=y0t[:, tp * C:(tp + 1) * C], out_offset=None,
                        in_=y_d[:, :],
                        in_offset=IndirectOffsetOnAxis(
                            yidx_all[:, t * 2:t * 2 + 1], 0)
                    ).then_inc(sem_gyg[tp].h, 16)
                    g.indirect_dma_start(
                        out=y1t[:, tp * C:(tp + 1) * C], out_offset=None,
                        in_=y_d[:, :],
                        in_offset=IndirectOffsetOnAxis(
                            yidx_all[:, t * 2 + 1:t * 2 + 2], 0)
                    ).then_inc(sem_gyg[tp].h, 16)
                block.gpsimd(g_yg)
                sem_gyg[tp].inc(32)
                ygb = sem_gyg[tp].n

                def s_p3ld(sy, t=t, tp=tp):
                    if sem_op[tp].n:
                        sy.wait_ge(sem_op[tp].h, sem_op[tp].n)
                    sy.sem_inc(sem_ofree.h, 1)
                    if sem_tr.n:
                        sy.wait_ge(sem_tr.h, sem_tr.n)
                    sy.dma_start(out=sin_t[:, tp * C:(tp + 1) * C],
                                 in_=s_d[t * 128:(t + 1) * 128, :]
                                 ).then_inc(sem_p3p[tp].h, 16)
                    sy.dma_start(out=xr_t[:, tp * C:(tp + 1) * C],
                                 in_=x_d[t * 128:(t + 1) * 128, :]
                                 ).then_inc(sem_p3p[tp].h, 16)
                block.sync(s_p3ld)
                sem_p3p[tp].inc(32)
                sem_ofree.inc(1)
                p3l = sem_p3p[tp].n
                ofr = sem_ofree.n

                def v_p3(v, t=t, tp=tp, ygb=ygb, p3l=p3l):
                    v.wait_ge(sem_gyg[tp].h, ygb)
                    v.wait_ge(sem_p3p[tp].h, p3l)
                    v.tensor_scalar(out=accf[:], in0=y0t[:, tp * C:(tp + 1) * C],
                                    scalar1=c12_all[:, t * 2:t * 2 + 1],
                                    scalar2=None, op0=OP.mult)
                    v.scalar_tensor_tensor(out=accf[:],
                                           in0=y1t[:, tp * C:(tp + 1) * C],
                                           scalar=c12_all[:, t * 2 + 1:t * 2 + 2],
                                           in1=accf[:], op0=OP.mult, op1=OP.add
                                           ).then_inc(sem_v.h, 1)
                vec(v_p3)
                sem_v.inc(1)

                def s_p3(x):
                    x.wait_ge(sem_v.h, sem_v.n)
                    x.activation(out=gsil[:], in_=accf[:], func=AF.Silu,
                                 bias=0.0, scale=1.0).then_inc(sem_s.h, 1)
                sca(s_p3)
                sem_s.inc(1)

                def v_p3b(v, t=t, tp=tp, ofr=ofr):
                    v.wait_ge(sem_s.h, sem_s.n)
                    v.wait_ge(sem_ofree.h, ofr)
                    v.tensor_mul(out=gsil[:], in0=gsil[:],
                                 in1=sin_t[:, tp * C:(tp + 1) * C])
                    v.tensor_add(out=outst[:, tp * C:(tp + 1) * C],
                                 in0=xr_t[:, tp * C:(tp + 1) * C], in1=gsil[:]
                                 ).then_inc(sem_v.h, 1)
                vec(v_p3b)
                sem_v.inc(1)
                vp3 = sem_v.n

                def s_owr(sy, t=t, tp=tp, vp3=vp3):
                    sy.wait_ge(sem_v.h, vp3)
                    if sem_tr.n:
                        sy.wait_ge(sem_tr.h, sem_tr.n)
                    sy.dma_start(out=out_d[t * 128:(t + 1) * 128, :],
                                 in_=outst[:, tp * C:(tp + 1) * C]
                                 ).then_inc(sem_op[tp].h, 16)
                block.sync(s_owr)
                sem_op[tp].inc(16)

        def s_final(sy):
            sy.wait_ge(sem_op[0].h, sem_op[0].n)
            sy.wait_ge(sem_op[1].h, sem_op[1].n)
        block.sync(s_final)

        es_p2.close()
    es.close()
    return nc


_NC_CACHE = {}


def kernel(L0_attn_residual, L0_attn_w_fc, L0_attn_w_proj, L0_attn_ln_g,
           L0_attn_ln_b, L0_ffn_Wr, L0_ffn_W1, L0_ffn_b1, L0_ffn_W2, L0_ffn_b2):
    nb = 4
    n_cores = 8
    if "nc" not in _NC_CACHE:
        _NC_CACHE["nc"] = build_program(nb=nb)
    nc = _NC_CACHE["nc"]

    x = np.ascontiguousarray(np.asarray(L0_attn_residual, dtype=np.float32))
    B = x.shape[0]
    wfc = np.ascontiguousarray(np.asarray(L0_attn_w_fc, dtype=np.float32))
    wproj = np.ascontiguousarray(np.asarray(L0_attn_w_proj, dtype=np.float32))
    lng = np.ascontiguousarray(np.asarray(L0_attn_ln_g, dtype=np.float32)
                               .reshape(NCC, 128))
    lnb = np.ascontiguousarray(np.asarray(L0_attn_ln_b, dtype=np.float32)
                               .reshape(NCC, 128))
    wr = np.ascontiguousarray(np.asarray(L0_ffn_Wr, dtype=np.float32))
    w1 = np.ascontiguousarray(np.asarray(L0_ffn_W1, dtype=np.float32)
                              .reshape(E * C, H))
    b1 = np.ascontiguousarray(np.asarray(L0_ffn_b1, dtype=np.float32)
                              .reshape(E * H // 128, 128))
    w2 = np.ascontiguousarray(np.asarray(L0_ffn_W2, dtype=np.float32)
                              .reshape(E * H, C))
    b2 = np.ascontiguousarray(np.asarray(L0_ffn_b2, dtype=np.float32))

    in_maps = []
    for c in range(n_cores):
        xs = np.ascontiguousarray(
            x[c * nb:(c + 1) * nb].reshape(nb * T_SEQ, C))
        in_maps.append({"x": xs, "wfc": wfc, "wproj": wproj, "lng": lng,
                        "lnb": lnb, "wr": wr, "w1": w1, "b1": b1, "w2": w2,
                        "b2": b2})

    res = run_bass_kernel_spmd(nc, in_maps, core_ids=list(range(n_cores)))
    outs = [res.results[c]["out"].reshape(nb, T_SEQ, C) for c in range(n_cores)]
    return np.ascontiguousarray(np.concatenate(outs, axis=0))


# version 04
# speedup vs baseline: 1.1589x; 1.1589x over previous
import numpy as np
from contextlib import ExitStack

import concourse.bass as bass
import concourse.mybir as mybir
from concourse.bass_core_rust import IndirectOffsetOnAxis
from concourse.bass_utils import run_bass_kernel_spmd

F32 = mybir.dt.float32
BF16 = mybir.dt.bfloat16
U32 = mybir.dt.uint32
I16 = mybir.dt.int16
AF = mybir.ActivationFunctionType
OP = mybir.AluOpType

T_SEQ = 2048
C = 1024
E = 8
CAP = 320
H = 4096
QH = 4
NCC = C // 128
NSI = 32
GRP = 4
NHC = H // 128
EPS = 1e-5
TSTRIDE = 384
XN_RING = 9


class Sem:
    def __init__(self, es, nc, name):
        self.h = es.enter_context(nc.semaphore(name))
        self.n = 0

    def inc(self, k):
        self.n += k
        return self.n


_CHAIN_SKIP = {"wait_ge", "dma_start", "indirect_dma_start", "dma_start_transpose",
               "register", "reg_mov", "snap", "sem_inc", "wait_op", "nop"}


class _InstProxy:

    def __init__(self, eng, ch):
        self._eng = eng
        self._ch = ch

    def then_inc(self, sem, k):
        m = self._eng.sem_inc(sem, k)
        m._wait_ge(self._ch.h, self._ch.n)
        return self


class ChainEng:

    def __init__(self, eng, ch):
        object.__setattr__(self, "_e", eng)
        object.__setattr__(self, "_c", ch)

    def __getattr__(self, name):
        attr = getattr(self._e, name)
        if (not callable(attr)) or name in _CHAIN_SKIP or name.startswith("_"):
            return attr
        ch = self._c
        eng = self._e

        def wrap(*a, **k):
            inst = attr(*a, **k)
            if inst is not None and hasattr(inst, "then_inc"):
                if ch.n > 0:
                    inst._wait_ge(ch.h, ch.n)
                inst.then_inc(ch.h, 1)
                ch.n += 1
                return _InstProxy(eng, ch)
            return inst
        return wrap


def build_program(nb=4):
    NTOK = nb * T_SEQ
    NTIL = NTOK // 128
    TPB = T_SEQ // 128
    NGRP = NTIL // GRP
    GPB = TPB // GRP
    NSLOT = E * nb * CAP
    ZROW = NSLOT
    TBL_ROWS = ((E * nb * TSTRIDE + 127) // 128) * 128

    nc = bass.Bass("TRN2", target_bir_lowering=False)

    x_d = nc.dram_tensor("x", [NTOK, C], F32, kind="ExternalInput").ap()
    wfc_d = nc.dram_tensor("wfc", [C, C], F32, kind="ExternalInput").ap()
    wproj_d = nc.dram_tensor("wproj", [C, C], F32, kind="ExternalInput").ap()
    lng_d = nc.dram_tensor("lng", [NCC, 128], F32, kind="ExternalInput").ap()
    lnb_d = nc.dram_tensor("lnb", [NCC, 128], F32, kind="ExternalInput").ap()
    wr_d = nc.dram_tensor("wr", [C, E], F32, kind="ExternalInput").ap()
    w1_d = nc.dram_tensor("w1", [E * C, H], F32, kind="ExternalInput").ap()
    b1_d = nc.dram_tensor("b1", [E * H // 128, 128], F32, kind="ExternalInput").ap()
    w2_d = nc.dram_tensor("w2", [E * H, C], F32, kind="ExternalInput").ap()
    b2_d = nc.dram_tensor("b2", [E, C], F32, kind="ExternalInput").ap()
    out_d = nc.dram_tensor("out", [NTOK, C], F32, kind="ExternalOutput").ap()

    xn_d = nc.dram_tensor("xn_scr", [NTOK, C], BF16, kind="Internal").ap()
    s_d = nc.dram_tensor("s_scr", [NTOK, C], BF16, kind="Internal").ap()
    y_d = nc.dram_tensor("y_scr", [NSLOT + 1, C], BF16, kind="Internal").ap()
    tbl_d = nc.dram_tensor("tbl_scr", [TBL_ROWS, 2], U32, kind="Internal").ap()
    w1t_d = nc.dram_tensor("w1t_scr", [E * 8 * 128, H], BF16, kind="Internal").ap()
    w2t_d = nc.dram_tensor("w2t_scr", [E * H, C], BF16, kind="Internal").ap()

    es = ExitStack()
    es_p1 = ExitStack()

    def sb(stack, name, shape, dtype=F32):
        return stack.enter_context(nc.sbuf_tensor(name, shape, dtype))

    def ps(stack, name, shape, dtype=F32):
        return stack.enter_context(nc.psum_tensor(name, shape, dtype))

    ident_f = sb(es, "ident_f", [128, 128], F32)
    iota8 = sb(es, "iota8", [128, 8], F32)
    iotap = sb(es, "iotap", [128, 1], F32)
    iotad = sb(es, "iotad", [128, 128], I16)
    eps_ap = sb(es, "eps_ap", [128, 1], F32)
    ones_row_bf = sb(es, "ones_row_bf", [1, 512], BF16)
    ones_col_bf = sb(es, "ones_col_bf", [128, 1], BF16)
    g_sb = sb(es, "g_sb", [128, NCC], F32)
    battn_bf = sb(es, "battn_bf", [128, NCC], BF16)
    wr_hi = sb(es, "wr_hi", [128, NCC * E], BF16)
    wr_lo = sb(es, "wr_lo", [128, NCC * E], BF16)
    wrsum_bc = sb(es, "wrsum_bc", [128, E], F32)
    bk_sb = sb(es, "bk_sb", [128, NSI * QH], F32)
    b1_sb = sb(es, "b1_sb", [128, E * NHC], F32)
    c12_all = sb(es, "c12_all", [128, 2 * NTIL], F32)
    yidx_all = sb(es, "yidx_all", [128, 2 * NTIL], U32)
    zero_bf = sb(es, "zero_bf", [128, C], BF16)

    sem_xp = [Sem(es, nc, "sx0"), Sem(es, nc, "sx1")]
    sem_tr = Sem(es, nc, "s_tr")
    sem_gxn = Sem(es, nc, "s_gxn")
    sem_gxnc = Sem(es, nc, "s_gxnc")
    sem_gs = [Sem(es, nc, "s_gs0"), Sem(es, nc, "s_gs1")]
    sem_gsc = Sem(es, nc, "s_gsc")
    sem_ld = Sem(es, nc, "s_ld")
    sem_yp = [Sem(es, nc, "s_y0"), Sem(es, nc, "s_y1")]
    sem_yfree = Sem(es, nc, "s_yfree")
    sem_ydn = Sem(es, nc, "s_ydn")
    sem_gix = Sem(es, nc, "s_gix")
    sem_b2 = Sem(es, nc, "s_b2")
    sem_wp1 = [Sem(es, nc, "s_wp1a"), Sem(es, nc, "s_wp1b")]
    sem_ws1 = [Sem(es, nc, "s_ws1a"), Sem(es, nc, "s_ws1b")]
    sem_wp2 = [Sem(es, nc, "s_wp2a"), Sem(es, nc, "s_wp2b")]
    sem_ws2 = [Sem(es, nc, "s_ws2a"), Sem(es, nc, "s_ws2b")]
    sem_gg = [Sem(es, nc, "s_gg0"), Sem(es, nc, "s_gg1")]
    sem_gyg = [Sem(es, nc, "s_gyg0"), Sem(es, nc, "s_gyg1")]
    sem_p3p = [Sem(es, nc, "s_p30"), Sem(es, nc, "s_p31")]
    sem_op = [Sem(es, nc, "s_op0"), Sem(es, nc, "s_op1")]
    sem_ofree = Sem(es, nc, "s_ofree")
    sem_out = Sem(es, nc, "s_out")
    sem_t = Sem(es, nc, "s_t")
    sem_v = Sem(es, nc, "s_v")
    sem_s = Sem(es, nc, "s_s")
    sem_gc = Sem(es, nc, "s_gc")
    sem_ph = Sem(es, nc, "s_ph")
    sem_wq = [Sem(es, nc, "s_wq%d" % i) for i in range(4)]
    ch_v = Sem(es, nc, "chv")
    ch_s = Sem(es, nc, "chs")

    with nc.Block() as block:
        def vec(f):
            block.vector(lambda v: f(ChainEng(v, ch_v)))

        def sca(f):
            block.scalar(lambda x: f(ChainEng(x, ch_s)))

        def _phase_barrier():
            tgt = sem_ph.n + 5

            def _arr_t(t_):
                t_.sem_inc(sem_ph.h, 1)
                t_.wait_ge(sem_ph.h, tgt)
            block.tensor(_arr_t)

            def _arr_s(sy):
                sy.sem_inc(sem_ph.h, 1)
                sy.wait_ge(sem_ph.h, tgt)
            block.sync(_arr_s)

            def _arr_g(g):
                g.sem_inc(sem_ph.h, 1)
                g.wait_ge(sem_ph.h, tgt)
            block.gpsimd(_arr_g)

            def _arr_v(v):
                v.wait_ge(ch_v.h, ch_v.n)
                v.sem_inc(sem_ph.h, 1)
                v.wait_ge(sem_ph.h, tgt)
            block.vector(_arr_v)

            def _arr_sc(x):
                x.wait_ge(ch_s.h, ch_s.n)
                x.sem_inc(sem_ph.h, 1)
                x.wait_ge(sem_ph.h, tgt)
            block.scalar(_arr_sc)
            sem_ph.inc(5)

        def _dma_drain():
            def s_drain(sy):
                for sm_ in (sem_tr, sem_ld, sem_xp[0], sem_xp[1], sem_yp[0],
                            sem_yp[1], sem_out, sem_op[0], sem_op[1],
                            sem_p3p[0], sem_p3p[1]):
                    if sm_.n:
                        sy.wait_ge(sm_.h, sm_.n)
            block.sync(s_drain)

            def g_drain(g):
                for sm_ in (sem_gxn, sem_gs[0], sem_gs[1], sem_gsc, sem_gix,
                            sem_gg[0], sem_gg[1], sem_gyg[0], sem_gyg[1],
                            sem_wq[0], sem_wq[1], sem_wq[2], sem_wq[3], sem_b2,
                            sem_wp1[0], sem_wp1[1], sem_ws1[0], sem_ws1[1],
                            sem_wp2[0], sem_wp2[1], sem_ws2[0], sem_ws2[1]):
                    if sm_.n:
                        g.wait_ge(sm_.h, sm_.n)
            block.gpsimd(g_drain)
        st_sb = ExitStack()
        wfcT = sb(es_p1, "wfcT", [128, NCC * C], BF16)
        wprojT = sb(es_p1, "wprojT", [128, NCC * C], BF16)
        wstage = sb(st_sb, "wstage", [128, C], F32)
        wstage_bf = sb(st_sb, "wstage_bf", [128, 2 * C], BF16)
        small_f = sb(st_sb, "small_f", [128, 128], F32)
        small_f2 = sb(st_sb, "small_f2", [128, 128], F32)
        wsum8 = sb(st_sb, "wsum8", [8, 1], F32)
        wsrow_bf = sb(st_sb, "wsrow_bf", [1, 8], BF16)
        ztbl = sb(st_sb, "ztbl", [128, TBL_ROWS * 2 // 128], U32)
        ps_set = ps(st_sb, "ps_set", [128, 512], F32)
        ps_set2 = ps(st_sb, "ps_set2", [128, 8], F32)

        def g_setup(g):
            g.iota(iotad[:], pattern=[[1, 128]], base=0, channel_multiplier=-1)
            g.iota(iota8[:], pattern=[[1, 8]], base=0, channel_multiplier=0,
                   allow_small_or_imprecise_dtypes=True)
            g.iota(iotap[:], pattern=[[0, 1]], base=0, channel_multiplier=1,
                   allow_small_or_imprecise_dtypes=True)
            g.memset(ztbl[:], 0).then_inc(sem_gc.h, 1)
        block.gpsimd(g_setup)
        sem_gc.inc(1)

        def v_setup0(v):
            v.wait_ge(sem_gc.h, sem_gc.n)
            v.memset(eps_ap[:], EPS)
            v.memset(ones_row_bf[:], 1.0)
            v.memset(ones_col_bf[:], 1.0)
            v.memset(zero_bf[:], 0.0)
            v.tensor_scalar(out=ident_f[:], in0=iotad[:], scalar1=0.0, scalar2=None,
                            op0=OP.is_equal).then_inc(sem_v.h, 1)
        vec(v_setup0)
        sem_v.inc(1)

        def g_zero(g):
            g.wait_ge(sem_v.h, sem_v.n)
            g.dma_start(out=tbl_d.rearrange("(p k) c -> p (k c)", p=128),
                        in_=ztbl[:]).then_inc(sem_gxn.h, 16)
            g.dma_start(out=y_d[ZROW:ZROW + 1, :],
                        in_=zero_bf[0:1, :]).then_inc(sem_gxn.h, 16)
        block.gpsimd(g_zero)
        sem_gxn.inc(32)

        def s_loads(sy):
            sy.dma_start(out=small_f[0:NCC, :], in_=lng_d[:, :]).then_inc(sem_ld.h, 16)
            sy.dma_start(out=small_f2[0:NCC, :], in_=lnb_d[:, :]).then_inc(sem_ld.h, 16)
            sy.dma_start(out=wstage[:, 0:NCC * E].rearrange("p (cc e) -> p cc e", e=E),
                         in_=wr_d[:, :].rearrange("(cc p) e -> p cc e", p=128)
                         ).then_inc(sem_ld.h, 16)
        block.sync(s_loads)
        sem_ld.inc(48)

        def t_gb(t):
            t.wait_ge(sem_v.h, sem_v.n)
            t.wait_ge(sem_ld.h, sem_ld.n)
            t.transpose(ps_set[:, 0:8], small_f[0:NCC, :], ident_f[0:NCC, 0:NCC])
            t.transpose(ps_set[:, 8:16], small_f2[0:NCC, :],
                        ident_f[0:NCC, 0:NCC]).then_inc(sem_t.h, 1)
        block.tensor(t_gb)
        sem_t.inc(1)

        def v_gb(v):
            v.wait_ge(sem_t.h, sem_t.n)
            v.tensor_copy(out=g_sb[:], in_=ps_set[:, 0:8])
            v.tensor_copy(out=battn_bf[:], in_=ps_set[:, 8:16])
            v.tensor_copy(out=wr_hi[:], in_=wstage[:, 0:NCC * E])
            v.tensor_sub(out=wr_lo[:], in0=wstage[:, 0:NCC * E], in1=wr_hi[:]
                         ).then_inc(sem_v.h, 1)
        vec(v_gb)
        sem_v.inc(1)

        def s_b1ld(sy):
            sy.wait_ge(sem_t.h, sem_t.n)
            sy.dma_start(out=small_f[:], in_=b1_d[0:128, :]).then_inc(sem_ld.h, 16)
            sy.dma_start(out=small_f2[:], in_=b1_d[128:256, :]).then_inc(sem_ld.h, 16)
        block.sync(s_b1ld)
        sem_ld.inc(32)


        def t_b1(t):
            t.wait_ge(sem_ld.h, sem_ld.n)
            t.wait_ge(sem_v.h, sem_v.n)
            t.transpose(ps_set[:, 0:128], small_f[:], ident_f[:])
            t.transpose(ps_set[:, 128:256], small_f2[:], ident_f[:]
                        ).then_inc(sem_t.h, 1)
        block.tensor(t_b1)
        sem_t.inc(1)

        def v_b1(v):
            v.wait_ge(sem_t.h, sem_t.n)
            v.tensor_copy(out=b1_sb[:], in_=ps_set[:, 0:256]).then_inc(sem_v.h, 1)
        vec(v_b1)
        sem_v.inc(1)

        ntr = 0
        for (src, dst) in ((wfc_d, wfcT), (wproj_d, wprojT)):
            for rc in range(8):
                slot = ntr % 2

                def s_wload(sy, src=src, rc=rc, vw=None):
                    sy.wait_ge(sem_v.h, sem_v.n)
                    sy.dma_start(out=wstage[:], in_=src[rc * 128:(rc + 1) * 128, :]
                                 ).then_inc(sem_ld.h, 16)
                block.sync(s_wload)
                sem_ld.inc(16)

                def v_wcast(v, slot=slot, ntr=ntr):
                    v.wait_ge(sem_ld.h, sem_ld.n)
                    if ntr >= 2:
                        v.wait_ge(sem_tr.h, 16 * (ntr - 1))
                    v.tensor_copy(out=wstage_bf[:, slot * C:(slot + 1) * C],
                                  in_=wstage[:]).then_inc(sem_v.h, 1)
                vec(v_wcast)
                sem_v.inc(1)

                def s_wtr(sy, dst=dst, rc=rc, slot=slot):
                    sy.wait_ge(sem_v.h, sem_v.n)
                    if sem_tr.n:
                        sy.wait_ge(sem_tr.h, sem_tr.n)
                    outap = dst[:, :].rearrange("p (cc r) -> p cc r", cc=NCC)[
                        :, :, rc * 128:(rc + 1) * 128]
                    sy.dma_start(out=outap, in_=wstage_bf[:, slot * C:(slot + 1) * C],
                                 transpose=True).then_inc(sem_tr.h, 16)
                block.sync(s_wtr)
                sem_tr.inc(16)
                ntr += 1

        for si in range(NSI):
            q = si // 8

            def t_bk(t, si=si, q=q):
                if si == 0:
                    t.wait_ge(sem_tr.h, sem_tr.n)
                t.wait_ge(sem_v.h, sem_v.n)
                last = None
                for h in range(QH):
                    for dc in range(2):
                        base = (2 * q + dc) * C + (si % 8) * 128
                        last = t.matmul(ps_set2[:, h:h + 1],
                                        wfcT[:, base:base + 128],
                                        battn_bf[:, 2 * h + dc:2 * h + dc + 1],
                                        start=(dc == 0), stop=(dc == 1))
                last.then_inc(sem_t.h, 1)
            block.tensor(t_bk)
            sem_t.inc(1)

            def v_bk(v, si=si):
                v.wait_ge(sem_t.h, sem_t.n)
                v.tensor_copy(out=bk_sb[:, si * 4:si * 4 + 4],
                              in_=ps_set2[:, 0:4]).then_inc(sem_v.h, 1)
            vec(v_bk)
            sem_v.inc(1)

        def t_ws(t):
            t.wait_ge(sem_v.h, sem_v.n)
            last = None
            for i, w in enumerate((wr_hi, wr_lo)):
                for cc in range(NCC):
                    last = t.matmul(ps_set2[0:8, 0:1], w[:, cc * 8:cc * 8 + 8],
                                    ones_col_bf[:, :], start=(i == 0 and cc == 0),
                                    stop=(i == 1 and cc == NCC - 1))
            last.then_inc(sem_t.h, 1)
        block.tensor(t_ws)
        sem_t.inc(1)

        def v_ws(v):
            v.wait_ge(sem_t.h, sem_t.n)
            v.tensor_copy(out=wsum8[:], in_=ps_set2[0:8, 0:1]).then_inc(sem_v.h, 1)
        vec(v_ws)
        sem_v.inc(1)

        def t_ws2(t):
            t.wait_ge(sem_v.h, sem_v.n)
            t.transpose(ps_set2[0:1, 0:8], wsum8[:], ident_f[0:8, 0:8]
                        ).then_inc(sem_t.h, 1)
        block.tensor(t_ws2)
        sem_t.inc(1)

        def v_ws2(v):
            v.wait_ge(sem_t.h, sem_t.n)
            v.tensor_copy(out=wsrow_bf[:], in_=ps_set2[0:1, 0:8]).then_inc(sem_v.h, 1)
        vec(v_ws2)
        sem_v.inc(1)

        def t_ws3(t):
            t.wait_ge(sem_v.h, sem_v.n)
            t.matmul(ps_set2[:, 0:8], ones_row_bf[0:1, 0:128], wsrow_bf[:],
                     start=True, stop=True).then_inc(sem_t.h, 1)
        block.tensor(t_ws3)
        sem_t.inc(1)

        def v_ws3(v):
            v.wait_ge(sem_t.h, sem_t.n)
            v.tensor_copy(out=wrsum_bc[:], in_=ps_set2[:, 0:8]).then_inc(sem_v.h, 1)
        vec(v_ws3)
        sem_v.inc(1)

        _dma_drain()
        _phase_barrier()
        st_sb.close()

        xb = sb(es_p1, "xb", [128, 2 * C], F32)
        xnf = sb(es_p1, "xnf", [128, 2 * C], F32)
        xnhi = sb(es_p1, "xnhi", [128, XN_RING * C], BF16)
        zf = sb(es_p1, "zf", [128, C], F32)
        zhi = sb(es_p1, "zhi", [128, 2 * C], BF16)
        zlo = sb(es_p1, "zlo", [128, 2 * C], BF16)
        zThi = sb(es_p1, "zThi", [128, 2 * NCC * 512], BF16)
        zTlo = sb(es_p1, "zTlo", [128, 2 * NCC * 512], BF16)
        zgT = sb(es_p1, "zgT", [128, NCC * 512], BF16)
        pT = sb(es_p1, "pT", [128, NSI * 512], BF16)
        s_stage = sb(es_p1, "s_stage", [128, 2 * GRP * C], BF16)
        routb = sb(es_p1, "routb", [8, 2 * T_SEQ], F32)
        hT = routb[0:8, 0:T_SEQ]
        cex = routb[0:8, T_SEQ:2 * T_SEQ]
        is1_b = sb(es_p1, "is1_b", [128, 8 * TPB], F32)
        is2_b = sb(es_p1, "is2_b", [128, 8 * TPB], F32)
        p12_b = sb(es_p1, "p12_b", [128, 2 * TPB], F32)
        e12_b = sb(es_p1, "e12_b", [128, 2 * TPB], F32)
        sd_ring = sb(es_p1, "sd_ring", [128, GRP], F32)
        mu_ring = sb(es_p1, "mu_ring", [128, GRP], F32)
        bns = sb(es_p1, "bns", [128, 12], F32)
        bn2 = sb(es_p1, "bn2", [128, 2], F32)
        bn2i = sb(es_p1, "bn2i", [128, 2], F32)
        lnv = sb(es_p1, "lnv", [128, 1], F32)
        r1 = sb(es_p1, "r1", [128, 1], F32)
        nmr = sb(es_p1, "nmr", [128, 1], F32)
        lg8 = sb(es_p1, "lg8", [128, 8], F32)
        m8 = sb(es_p1, "m8", [128, 8], F32)
        i8u = sb(es_p1, "i8u", [128, 8], U32)
        e8x = sb(es_p1, "e8x", [128, 8], F32)
        zsum = sb(es_p1, "zsum", [128, 1], F32)
        rz = sb(es_p1, "rz", [128, 1], F32)
        negm1 = sb(es_p1, "negm1", [128, 1], F32)
        p12e = sb(es_p1, "p12e", [128, 2], F32)
        hmaskf = sb(es_p1, "hmaskf", [128, 8], F32)
        scr8 = sb(es_p1, "scr8", [128, 8], F32)
        pos12 = sb(es_p1, "pos12", [128, 2], F32)
        keep12 = sb(es_p1, "keep12", [128, 2], F32)
        slotf = sb(es_p1, "slotf", [128, 1], F32)
        tmp1 = sb(es_p1, "tmp1", [128, 1], F32)
        payload = sb(es_p1, "payload", [128, 2 * TPB], U32)
        idxa_u = sb(es_p1, "idxa_u", [128, 2 * TPB], U32)
        tokf = sb(es_p1, "tokf", [128, 1], F32)

        wpre1 = sb(es_p1, "wpre1", [128, 2 * 2048], BF16)
        wpre2 = sb(es_p1, "wpre2", [128, 2 * C], BF16)

        psA = [ps(es_p1, f"psA{i}", [128, 512], F32) for i in range(2)]
        psS2 = [ps(es_p1, "psS2a", [128, 256], F32), ps(es_p1, "psS2b", [128, 256], F32)]
        psR = ps(es_p1, "psR", [128, 8], F32)
        psTr = ps(es_p1, "psTr", [128, 128], F32)

        def v_p1init(v):
            v.memset(routb[:, :], 0.0)
        vec(v_p1init)

        def s_x0(sy):
            sy.dma_start(out=xb[:, 0:C], in_=x_d[0:128, :]).then_inc(sem_xp[0].h, 16)
        block.sync(s_x0)
        sem_xp[0].inc(16)

        vx_done = [0] * NTIL
        vz_done = [0] * NTIL
        tr_done = [0] * NTIL
        vh_done = [0] * NTIL
        vsm1_done = [0] * NTIL
        ht_evac = [0] * NTIL
        t_att_end = [0] * NGRP
        v_s2_done = [0] * NGRP
        g_sw_done = [0] * NGRP
        endb_v_done = [0] * nb
        scat_g_done = [0] * nb

        def emit_tile(t):
            b = t // TPB
            tt = t % TPB
            gg = t // GRP
            tig = t % GRP
            gp = gg % 2
            xslot = t % 2
            xnslot = t % XN_RING

            def v_ln1(v):
                v.wait_ge(sem_xp[t % 2].h, 16 * (t // 2 + 1))
                if t >= XN_RING:
                    v.wait_ge(sem_v.h, v_s2_done[(t - XN_RING) // GRP])
                    v.wait_ge(sem_gxnc.h, t - XN_RING + 2)
                v.engine_nop()
                v.bn_stats(out=bns[:, 0:6], in_=xb[:, xslot * C:xslot * C + 512])
                v.bn_stats(out=bns[:, 6:12],
                           in_=xb[:, xslot * C + 512:xslot * C + 1024])
                v.bn_aggr(out=bn2[:], in_=bns[:])
                v.engine_nop().then_inc(sem_v.h, 1)
            vec(v_ln1)
            sem_v.inc(1)

            def s_rs1(s):
                s.wait_ge(sem_v.h, sem_v.n)
                s.activation(out=lnv[:], in_=bn2[:, 1:2], func=AF.Ln,
                             bias=eps_ap[:], scale=1.0)
                s.activation(out=r1[:], in_=lnv[:], func=AF.Exp, bias=0.0,
                             scale=-0.5).then_inc(sem_s.h, 1)
            sca(s_rs1)
            sem_s.inc(1)

            def v_ln2(v):
                v.wait_ge(sem_s.h, sem_s.n)
                v.tensor_scalar(out=nmr[:], in0=bn2[:, 0:1], scalar1=r1[:],
                                scalar2=-1.0, op0=OP.mult, op1=OP.mult)
                v.tensor_scalar(out=xnf[:, xslot * C:(xslot + 1) * C],
                                in0=xb[:, xslot * C:(xslot + 1) * C],
                                scalar1=r1[:], scalar2=nmr[:],
                                op0=OP.mult, op1=OP.add).then_inc(sem_v.h, 1)
                v.tensor_copy(out=xnhi[:, xnslot * C:(xnslot + 1) * C],
                              in_=xnf[:, xslot * C:(xslot + 1) * C])
                v.engine_nop()
                v.bn_stats(out=bns[:, 0:6], in_=xnf[:, xslot * C:xslot * C + 512])
                v.bn_stats(out=bns[:, 6:12],
                           in_=xnf[:, xslot * C + 512:xslot * C + 1024])
                v.bn_aggr(out=bn2i[:], in_=bns[:])
                v.engine_nop().then_inc(sem_v.h, 1)
            vec(v_ln2)
            sem_v.inc(1)
            vx_done[t] = sem_v.n
            sem_v.inc(1)
            ln2_done = sem_v.n

            def s_rs2(s):
                s.wait_ge(sem_v.h, sem_v.n)
                s.activation(out=lnv[:], in_=bn2i[:, 1:2], func=AF.Ln,
                             bias=eps_ap[:], scale=1.0)
                s.activation(out=r1[:], in_=lnv[:], func=AF.Exp, bias=0.0,
                             scale=-0.5)
                s.activation(out=sd_ring[:, tig:tig + 1], in_=lnv[:], func=AF.Exp,
                             bias=0.0, scale=0.5).then_inc(sem_s.h, 1)
            sca(s_rs2)
            sem_s.inc(1)

            def v_ln3(v):
                v.wait_ge(sem_s.h, sem_s.n)
                if t >= 2:
                    v.wait_ge(sem_tr.h, tr_done[t - 2])
                v.tensor_copy(out=mu_ring[:, tig:tig + 1], in_=bn2i[:, 0:1])
                v.tensor_scalar(out=nmr[:], in0=bn2i[:, 0:1], scalar1=r1[:],
                                scalar2=-1.0, op0=OP.mult, op1=OP.mult)
                v.tensor_scalar(out=zf[:], in0=xnf[:, xslot * C:(xslot + 1) * C],
                                scalar1=r1[:], scalar2=nmr[:], op0=OP.mult,
                                op1=OP.add)
                v.tensor_copy(out=zhi[:, xslot * C:(xslot + 1) * C], in_=zf[:])
                v.tensor_sub(out=zlo[:, xslot * C:(xslot + 1) * C], in0=zf[:],
                             in1=zhi[:, xslot * C:(xslot + 1) * C]
                             ).then_inc(sem_v.h, 1)
            vec(v_ln3)
            sem_v.inc(1)
            vz_done[t] = sem_v.n

            def g_xn(g, ln2_done=None):
                g.wait_ge(sem_v.h, ln2_done)
                if sem_gxn.n:
                    g.wait_ge(sem_gxn.h, sem_gxn.n)
                g.sem_inc(sem_gxnc.h, 1)
                g.dma_start(out=xn_d[t * 128:(t + 1) * 128, :],
                            in_=xnhi[:, xnslot * C:(xnslot + 1) * C]
                            ).then_inc(sem_gxn.h, 16)
            block.gpsimd(lambda g, l=ln2_done: g_xn(g, l))
            sem_gxn.inc(16)
            sem_gxnc.inc(1)

            def g_wpre(g):
                def w1_load(u):
                    e1, cc, hf = u // 16, (u // 2) % 8, u % 2
                    sl = u % 2
                    if u >= 2:
                        g.wait_ge(sem_ws1[sl].h, sem_ws1[sl].n)
                    g.dma_start(out=wpre1[:, sl * 2048:(sl + 1) * 2048],
                                in_=w1_d[e1 * C + cc * 128:e1 * C + (cc + 1) * 128,
                                         hf * 2048:(hf + 1) * 2048],
                                max_dma_last_dim=8192).then_inc(sem_wp1[sl].h, 16)
                    sem_wp1[sl].inc(16)

                def w1_store(u):
                    e1, cc, hf = u // 16, (u // 2) % 8, u % 2
                    sl = u % 2
                    g.wait_ge(sem_wp1[sl].h, sem_wp1[sl].n)
                    outap = w1t_d[(e1 * 8 + hf * 4) * 128:
                                  (e1 * 8 + hf * 4 + 4) * 128,
                                  cc * 512:(cc + 1) * 512].rearrange(
                        "(hq p) jj -> p hq jj", p=128)
                    g.dma_start(out=outap,
                                in_=wpre1[:, sl * 2048:(sl + 1) * 2048].rearrange(
                                    "p (hq jj) -> p hq jj", hq=4)
                                ).then_inc(sem_ws1[sl].h, 16)
                    sem_ws1[sl].inc(16)

                def w2_load(u2):
                    s2 = u2 % 2
                    e2, k2 = u2 // 32, u2 % 32
                    if u2 >= 2:
                        g.wait_ge(sem_ws2[s2].h, sem_ws2[s2].n)
                    g.dma_start(out=wpre2[:, s2 * C:(s2 + 1) * C],
                                in_=w2_d[e2 * H + k2 * 128:
                                         e2 * H + (k2 + 1) * 128, :],
                                max_dma_last_dim=8192).then_inc(sem_wp2[s2].h, 16)
                    sem_wp2[s2].inc(16)

                def w2_store(u2):
                    s2 = u2 % 2
                    e2, k2 = u2 // 32, u2 % 32
                    g.wait_ge(sem_wp2[s2].h, sem_wp2[s2].n)
                    g.dma_start(out=w2t_d[e2 * H + k2 * 128:
                                          e2 * H + (k2 + 1) * 128, :],
                                in_=wpre2[:, s2 * C:(s2 + 1) * C]
                                ).then_inc(sem_ws2[s2].h, 16)
                    sem_ws2[s2].inc(16)

                n1 = (128 + NTIL - 1) // NTIL
                n2 = (256 + NTIL - 1) // NTIL
                for k in range(n1):
                    u = n1 * t + k
                    if u < 128:
                        w1_load(u)
                        if u >= 1:
                            w1_store(u - 1)
                        if u == 127:
                            w1_store(u)
                for k in range(n2):
                    u2 = n2 * t + k
                    if u2 < 256:
                        w2_load(u2)
                        if u2 >= 1:
                            w2_store(u2 - 1)
                        if u2 == 255:
                            w2_store(u2)
            block.gpsimd(g_wpre)

            def s_tile(sy):
                if t + 1 < NTIL:
                    nxt = (t + 1) % 2
                    if t >= 1:
                        sy.wait_ge(sem_v.h, vx_done[t])
                    sy.dma_start(out=xb[:, nxt * C:(nxt + 1) * C],
                                 in_=x_d[(t + 1) * 128:(t + 2) * 128, :]
                                 ).then_inc(sem_xp[(t + 1) % 2].h, 16)
                sy.wait_ge(sem_v.h, vz_done[t])
                if sem_tr.n:
                    sy.wait_ge(sem_tr.h, sem_tr.n)
                if gg >= 2:
                    sy.wait_ge(sem_t.h, t_att_end[gg - 2])
                outhi = zThi[:, gp * NCC * 512:(gp + 1) * NCC * 512].rearrange(
                    "p (cc q) -> p cc q", cc=NCC)[:, :, tig * 128:(tig + 1) * 128]
                sy.dma_start(out=outhi, in_=zhi[:, xslot * C:(xslot + 1) * C],
                             transpose=True).then_inc(sem_tr.h, 16)
                sy.wait_ge(sem_tr.h, sem_tr.n + 16)
                outlo = zTlo[:, gp * NCC * 512:(gp + 1) * NCC * 512].rearrange(
                    "p (cc q) -> p cc q", cc=NCC)[:, :, tig * 128:(tig + 1) * 128]
                sy.dma_start(out=outlo, in_=zlo[:, xslot * C:(xslot + 1) * C],
                             transpose=True).then_inc(sem_tr.h, 16)
            block.sync(s_tile)
            if t + 1 < NTIL:
                sem_xp[(t + 1) % 2].inc(16)
            sem_tr.inc(32)
            tr_done[t] = sem_tr.n

            def t_router(t_):
                t_.wait_ge(sem_tr.h, tr_done[t])
                if t >= 1:
                    t_.wait_ge(sem_v.h, vsm1_done[t - 1])
                last = None
                n = 0
                for cc in range(NCC):
                    base = gp * NCC * 512 + cc * 512 + tig * 128
                    for lhs, rhs in ((zThi, wr_hi), (zThi, wr_lo), (zTlo, wr_hi)):
                        n += 1
                        last = t_.matmul(psR[:], lhs[:, base:base + 128],
                                         rhs[:, cc * 8:cc * 8 + 8],
                                         start=(n == 1), stop=(n == 3 * NCC))
                last.then_inc(sem_t.h, 1)
            block.tensor(t_router)
            sem_t.inc(1)
            rd_t = sem_t.n

            def v_sm1(v):
                v.wait_ge(sem_t.h, rd_t)
                v.tensor_scalar(out=lg8[:], in0=psR[:],
                                scalar1=sd_ring[:, tig:tig + 1], scalar2=None,
                                op0=OP.mult)
                v.scalar_tensor_tensor(out=lg8[:], in0=wrsum_bc[:],
                                       scalar=mu_ring[:, tig:tig + 1], in1=lg8[:],
                                       op0=OP.mult, op1=OP.add)
                v.engine_nop()
                v.max(out=m8[:], in_=lg8[:])
                v.engine_nop()
                v.max_index(out=i8u[:], in_max=m8[:], in_values=lg8[:])
                v.engine_nop()
                v.tensor_scalar(out=negm1[:], in0=m8[:, 0:1], scalar1=-1.0,
                                scalar2=None, op0=OP.mult).then_inc(sem_v.h, 1)
            vec(v_sm1)
            sem_v.inc(1)
            vsm1_done[t] = sem_v.n

            def s_exp(s):
                s.wait_ge(sem_v.h, sem_v.n)
                s.activation(out=e8x[:], in_=lg8[:], func=AF.Exp, bias=negm1[:],
                             scale=1.0, accum_out=zsum[:])
                s.activation(out=p12e[:], in_=m8[:, 0:2], func=AF.Exp, bias=negm1[:],
                             scale=1.0).then_inc(sem_s.h, 1)
            sca(s_exp)
            sem_s.inc(1)

            def v_sm2(v):
                v.wait_ge(sem_s.h, sem_s.n)
                v.engine_nop()
                v.reciprocal(out=rz[:], in_=zsum[:])
                v.engine_nop()
                v.tensor_scalar(out=p12_b[:, tt * 2:tt * 2 + 2], in0=p12e[:],
                                scalar1=rz[:], scalar2=None, op0=OP.mult)
                v.tensor_copy(out=e12_b[:, tt * 2:tt * 2 + 1], in_=i8u[:, 0:1])
                v.tensor_copy(out=e12_b[:, tt * 2 + 1:tt * 2 + 2], in_=i8u[:, 1:2])
                v.tensor_scalar(out=is1_b[:, tt * 8:tt * 8 + 8], in0=iota8[:],
                                scalar1=e12_b[:, tt * 2:tt * 2 + 1], scalar2=None,
                                op0=OP.is_equal)
                v.tensor_scalar(out=is2_b[:, tt * 8:tt * 8 + 8], in0=iota8[:],
                                scalar1=e12_b[:, tt * 2 + 1:tt * 2 + 2],
                                scalar2=None, op0=OP.is_equal)
                v.tensor_add(out=hmaskf[:], in0=is1_b[:, tt * 8:tt * 8 + 8],
                             in1=is2_b[:, tt * 8:tt * 8 + 8]).then_inc(sem_v.h, 1)
            vec(v_sm2)
            sem_v.inc(1)
            vh_done[t] = sem_v.n

            def t_hm(t_):
                t_.wait_ge(sem_v.h, vh_done[t])
                if tt:
                    t_.wait_ge(sem_v.h, ht_evac[t - 1])
                elif b:
                    t_.wait_ge(sem_v.h, endb_v_done[b - 1])
                t_.transpose(psTr[0:8, 0:128], hmaskf[:], ident_f[:]
                             ).then_inc(sem_t.h, 1)
            block.tensor(t_hm)
            sem_t.inc(1)
            hm_t = sem_t.n

            def v_hm(v):
                v.wait_ge(sem_t.h, hm_t)
                v.tensor_copy(out=routb[0:8, tt * 128:(tt + 1) * 128],
                              in_=psTr[0:8, 0:128]).then_inc(sem_v.h, 1)
            vec(v_hm)
            sem_v.inc(1)
            ht_evac[t] = sem_v.n

        def emit_attn_group(gg):
            gp = gg % 2
            sp = gg % 2

            def v_zg(v):
                v.wait_ge(sem_tr.h, tr_done[gg * GRP + GRP - 1])
                if gg >= 1:
                    v.wait_ge(sem_t.h, t_att_end[gg - 1])
                for cc in range(NCC):
                    v.tensor_scalar(
                        out=zgT[:, cc * 512:(cc + 1) * 512],
                        in0=zThi[:, gp * NCC * 512 + cc * 512:
                                 gp * NCC * 512 + (cc + 1) * 512],
                        scalar1=g_sb[:, cc:cc + 1], scalar2=None, op0=OP.mult)
                v.engine_nop().then_inc(sem_v.h, 1)
            vec(v_zg)
            sem_v.inc(1)
            vzg = sem_v.n

            t0 = sem_t.n
            v0 = sem_v.n
            s0 = sem_s.n

            def t_attn(t_):
                t_.wait_ge(sem_v.h, vzg)
                t_.wait_ge(sem_s.h, s0)
                if gg >= 1:
                    t_.wait_ge(sem_v.h, v_s2_done[gg - 1])
                for h in range(QH):
                    for si in range(NSI):
                        q = si // 8
                        k = h * 32 + si
                        if k >= 2:
                            t_.wait_ge(sem_s.h, s0 + k - 1)
                        last = None
                        for dc in range(2):
                            kb = (2 * q + dc) * C + (si % 8) * 128
                            last = t_.matmul(
                                psA[si % 2][:], wfcT[:, kb:kb + 128],
                                zgT[:, (2 * h + dc) * 512:(2 * h + dc + 1) * 512],
                                start=(dc == 0), stop=(dc == 1))
                        last.then_inc(sem_t.h, 1)
                    for tc in range(GRP):
                        if h == 0 and tc < 2:
                            pass
                        elif tc < 2:
                            t_.wait_ge(sem_v.h, v0 + h * 36 + tc - 1)
                        else:
                            t_.wait_ge(sem_v.h, v0 + h * 36 + 31 + tc)
                        t_.wait_ge(sem_s.h, s0 + h * 32 + 32)
                        last = None
                        for si in range(NSI):
                            q = si // 8
                            last = t_.matmul(
                                psS2[tc % 2][:],
                                pT[:, si * 512 + tc * 128:si * 512 + tc * 128 + 128],
                                wprojT[:, (si % 8) * C + q * 256:
                                       (si % 8) * C + q * 256 + 256],
                                start=(si == 0), stop=(si == NSI - 1))
                        last.then_inc(sem_t.h, 1)
            block.tensor(t_attn)
            sem_t.inc(QH * (NSI + GRP))
            t_att_end[gg] = sem_t.n

            def v_attn(v):
                if gg >= 2:
                    v.wait_ge(sem_gs[gg % 2].h, g_sw_done[gg - 2])
                for h in range(QH):
                    for si in range(NSI):
                        k = h * 36 + si if False else h * 32 + si
                        v.wait_ge(sem_t.h, t0 + h * 36 + si + 1)
                        v.tensor_scalar(out=psA[si % 2][:], in0=psA[si % 2][:],
                                        scalar1=bk_sb[:, si * 4 + h:si * 4 + h + 1],
                                        scalar2=0.0, op0=OP.add, op1=OP.max
                                        ).then_inc(sem_v.h, 1)
                    for tc in range(GRP):
                        v.wait_ge(sem_t.h, t0 + h * 36 + 32 + tc + 1)
                        xsl = ((gg * GRP + tc) % XN_RING) * C + h * 256
                        v.scalar_tensor_tensor(
                            out=s_stage[:, (sp * GRP + tc) * C + h * 256:
                                        (sp * GRP + tc) * C + h * 256 + 256],
                            in0=psS2[tc % 2][:],
                            scalar=0.0, in1=xnhi[:, xsl:xsl + 256],
                            op0=OP.add, op1=OP.add).then_inc(sem_v.h, 1)
            vec(v_attn)
            sem_v.inc(QH * 36)
            v_s2_done[gg] = sem_v.n

            def s_attn(s):
                for h in range(QH):
                    for si in range(NSI):
                        s.wait_ge(sem_v.h, v0 + h * 36 + si + 1)
                        s.activation(out=pT[:, si * 512:(si + 1) * 512],
                                     in_=psA[si % 2][:], func=AF.Square
                                     ).then_inc(sem_s.h, 1)
            sca(s_attn)
            sem_s.inc(QH * NSI)

            def g_sw(g):
                for tc in range(GRP):
                    g.wait_ge(sem_v.h, v0 + 3 * 36 + 32 + tc + 1)
                    tg = gg * GRP + tc
                    g.dma_start(out=s_d[tg * 128:(tg + 1) * 128, :],
                                in_=s_stage[:, (sp * GRP + tc) * C:
                                            (sp * GRP + tc + 1) * C]
                                ).then_inc(sem_gs[gg % 2].h, 16)
            block.gpsimd(g_sw)
            sem_gs[gg % 2].inc(16 * GRP)
            g_sw_done[gg] = sem_gs[gg % 2].n

        def emit_endb(b):
            def v_scan(v):
                if b >= 1:
                    v.wait_ge(sem_gsc.h, scat_g_done[b - 1])
                v.engine_nop()
                v.tensor_tensor_scan(out=cex, data0=hT, data1=hT,
                                     initial=0.0, op0=OP.add, op1=OP.max)
                v.engine_nop()
                v.tensor_sub(out=cex, in0=cex, in1=hT).then_inc(sem_v.h, 1)
            vec(v_scan)
            sem_v.inc(1)
            vscan = sem_v.n
            vch = [0] * TPB

            for tt in range(TPB):
                t = b * TPB + tt

                def t_bt(t_, tt=tt):
                    t_.wait_ge(sem_v.h, vscan if tt == 0 else sem_v.n)
                    t_.transpose(psTr[:, 0:8], routb[0:8, T_SEQ + tt * 128:T_SEQ + (tt + 1) * 128],
                                 ident_f[0:8, 0:8]).then_inc(sem_t.h, 1)
                block.tensor(t_bt)
                sem_t.inc(1)
                bt_t = sem_t.n

                def v_chain(v, tt=tt, t=t, b=b, bt_t=bt_t):
                    v.wait_ge(sem_t.h, bt_t)
                    v.tensor_mul(out=scr8[:], in0=psTr[:, 0:8],
                                 in1=is1_b[:, tt * 8:tt * 8 + 8])
                    v.reduce_sum(out=pos12[:, 0:1], in_=scr8[:],
                                 axis=mybir.AxisListType.X)
                    v.tensor_mul(out=scr8[:], in0=psTr[:, 0:8],
                                 in1=is2_b[:, tt * 8:tt * 8 + 8])
                    v.reduce_sum(out=pos12[:, 1:2], in_=scr8[:],
                                 axis=mybir.AxisListType.X)
                    v.tensor_scalar(out=keep12[:], in0=pos12[:], scalar1=float(CAP),
                                    scalar2=None, op0=OP.is_lt)
                    v.tensor_mul(out=c12_all[:, t * 2:t * 2 + 2],
                                 in0=p12_b[:, tt * 2:tt * 2 + 2], in1=keep12[:])
                    v.tensor_scalar(out=tokf[:], in0=iotap[:], scalar1=1.0,
                                    scalar2=float(t * 128), op0=OP.mult, op1=OP.add)
                    v.tensor_copy(out=payload[:, tt * 2:tt * 2 + 1], in_=tokf[:])
                    v.tensor_copy(out=payload[:, tt * 2 + 1:tt * 2 + 2], in_=tokf[:])
                    for k in range(2):
                        ek = e12_b[:, tt * 2 + k:tt * 2 + k + 1]
                        pk = pos12[:, k:k + 1]
                        kk = keep12[:, k:k + 1]
                        v.tensor_scalar(out=slotf[:], in0=ek,
                                        scalar1=float(nb * CAP),
                                        scalar2=float(b * CAP - ZROW),
                                        op0=OP.mult, op1=OP.add)
                        v.tensor_add(out=slotf[:], in0=slotf[:], in1=pk)
                        v.tensor_mul(out=slotf[:], in0=slotf[:], in1=kk)
                        v.tensor_scalar(out=slotf[:], in0=slotf[:],
                                        scalar1=float(ZROW), scalar2=None,
                                        op0=OP.add)
                        v.tensor_copy(out=yidx_all[:, t * 2 + k:t * 2 + k + 1],
                                      in_=slotf[:])
                        v.tensor_scalar(out=slotf[:], in0=ek,
                                        scalar1=float(nb * TSTRIDE),
                                        scalar2=float(b * TSTRIDE - 1048576),
                                        op0=OP.mult, op1=OP.add)
                        v.tensor_add(out=slotf[:], in0=slotf[:], in1=pk)
                        v.tensor_mul(out=slotf[:], in0=slotf[:], in1=kk)
                        v.tensor_scalar(out=slotf[:], in0=slotf[:],
                                        scalar1=1048576.0, scalar2=None, op0=OP.add)
                        v.tensor_copy(out=idxa_u[:, tt * 2 + k:tt * 2 + k + 1],
                                      in_=slotf[:])
                    v.engine_nop().then_inc(sem_v.h, 1)
                vec(v_chain)
                sem_v.inc(1)
                vch[tt] = sem_v.n

            def g_scat(g, b=b):
                g.wait_ge(sem_v.h, vch[TPB - 1])
                with g.register("bnd%d" % b) as breg:
                    g.reg_mov(breg, TBL_ROWS - 1)
                    bval = g.snap(breg)
                    for tt in range(TPB):
                        for k in range(2):
                            g.indirect_dma_start(
                                out=tbl_d[:, :],
                                out_offset=IndirectOffsetOnAxis(
                                    idxa_u[:, tt * 2 + k:tt * 2 + k + 1], 0),
                                in_=payload[:, tt * 2:tt * 2 + 2],
                                in_offset=None,
                                bounds_check=bval, oob_is_err=False
                            ).then_inc(sem_gsc.h, 16)
            block.gpsimd(g_scat)
            sem_gsc.inc(32 * TPB)
            endb_v_done[b] = vch[TPB - 1]
            scat_g_done[b] = sem_gsc.n

        for gg in range(NGRP):
            for tig in range(GRP):
                emit_tile(gg * GRP + tig)
            if gg >= 1:
                emit_attn_group(gg - 1)
            if (gg + 1) % GPB == 0:
                emit_endb(gg // GPB)
        emit_attn_group(NGRP - 1)

        _dma_drain()
        _phase_barrier()
        es_p1.close()

        es_p2 = ExitStack()
        w1c = sb(es_p2, "w1c", [128, 2 * H], BF16)
        w2c = sb(es_p2, "w2c", [128, 2 * H], BF16)
        bufT = sb(es_p2, "bufT", [128, 2 * E * NCC * TSTRIDE], BF16)
        tokg = sb(es_p2, "tokg", [128, 2 * C], BF16)
        idxg = sb(es_p2, "idxg", [128, 2 * E * 3], U32)
        hbf = sb(es_p2, "hbf", [128, 2 * CAP], BF16)
        y_stage = sb(es_p2, "y_stage", [128, 3 * 3 * C], BF16)
        y0t = sb(es_p2, "y0t", [128, 2 * C], BF16)
        y1t = sb(es_p2, "y1t", [128, 2 * C], BF16)
        sin_t = sb(es_p2, "sin_t", [128, 2 * C], BF16)
        xr_t = sb(es_p2, "xr_t", [128, 2 * C], F32)
        accf = sb(es_p2, "accf", [128, C], F32)
        gsil = accf
        outst = sb(es_p2, "outst", [128, 2 * C], F32)

        b2_bf = sb(es_p2, "b2_bf", [1, E * C], BF16)
        psH = [ps(es_p2, f"psH{i}", [128, CAP], F32) for i in range(2)]
        psY = [ps(es_p2, "psY0", [128, C], F32), ps(es_p2, "psY1", [128, C], F32),
               ps(es_p2, "psY2", [64, C], F32)]
        NSZ = (128, 128, 64)
        NOF = (0, 128, 256)

        def g_b2ld(g):
            g.dma_start(out=b2_bf[0:1, :], in_=b2_d[:, :],
                        max_dma_last_dim=8192).then_inc(sem_b2.h, 16)
        block.gpsimd(g_b2ld)
        sem_b2.inc(16)
        b2_ready = sem_b2.n

        buf_ready = {}
        te_be = {}
        gctr = [0]

        def emit_gather_e(b, e, prev_marker):
            bp = b % 2

            def g_gath(g):
                g.wait_ge(sem_gsc.h, scat_g_done[b])
                g.wait_ge(sem_gxn.h, sem_gxn.n)
                if prev_marker is not None:
                    g.wait_ge(sem_t.h, prev_marker)
                r0 = (e * nb + b) * TSTRIDE
                g.dma_start(out=idxg[:, (bp * E + e) * 3:(bp * E + e) * 3 + 3],
                            in_=tbl_d[r0:r0 + TSTRIDE, 0:1].rearrange(
                                "(a p) c -> p (a c)", p=128),
                            allow_slow_non_contiguous=True
                            ).then_inc(sem_gix.h, 16)
                g.wait_ge(sem_gix.h, sem_gix.n + 16)
                for a in range(3):
                    j = gctr[0] + a
                    if j >= 2:
                        g.wait_ge(sem_tr.h, tr_base_p2[0] + 16 * (j - 1))
                    g.indirect_dma_start(
                        out=tokg[:, (j % 2) * C:(j % 2) * C + C],
                        out_offset=None, in_=xn_d[:, :],
                        in_offset=IndirectOffsetOnAxis(
                            idxg[:, (bp * E + e) * 3 + a:
                                 (bp * E + e) * 3 + a + 1], 0)
                    ).then_inc(sem_gg[j % 2].h, 16)
            block.gpsimd(g_gath)
            sem_gix.inc(16)
            gg0 = [sem_gg[0].n, sem_gg[1].n]
            for a in range(3):
                sem_gg[(gctr[0] + a) % 2].inc(16)

            def s_btr(sy):
                done = {0: gg0[0], 1: gg0[1]}
                for a in range(3):
                    j = gctr[0] + a
                    par = j % 2
                    done[par] += 16
                    sy.wait_ge(sem_gg[par].h, done[par])
                    sy.wait_ge(sem_tr.h, tr_base_p2[0] + 16 * j)
                    base = (bp * E + e) * NCC * TSTRIDE
                    outap = bufT[:, base:base + NCC * TSTRIDE].rearrange(
                        "p (cc n) -> p cc n", cc=NCC)[:, :, a * 128:(a + 1) * 128]
                    sy.dma_start(out=outap,
                                 in_=tokg[:, (j % 2) * C:(j % 2) * C + C],
                                 transpose=True).then_inc(sem_tr.h, 16)
            block.sync(s_btr)
            sem_tr.inc(16 * 3)
            gctr[0] += 3
            buf_ready[(b, e)] = sem_tr.n

        prev_yv = [0]
        t0_prev_exp = [None]
        tr_base_p2 = [sem_tr.n]

        for e in range(E):
            emit_gather_e(0, e, None)

        for b in range(nb):
            bp = b % 2
            for e in range(E):
                eg = b * E + e
                t0 = sem_t.n
                s0 = sem_s.n
                w0 = [sem_wq[0].n, sem_wq[1].n]
                base_buf = (bp * E + e) * NCC * TSTRIDE
                t0p = t0_prev_exp[0]

                def g_wload(g, e=e, t0=t0, t0p=t0p):
                    for hq in range(8):
                        q = hq % 2
                        if hq >= 2:
                            g.wait_ge(sem_t.h, t0 + 8 * hq - 6)
                        elif t0p is not None:
                            g.wait_ge(sem_t.h, t0p + 8 * hq + 58)
                        g.dma_start(
                            out=w1c[:, q * H:(q + 1) * H],
                            in_=w1t_d[(e * 8 + hq) * 128:(e * 8 + hq + 1) * 128, :]
                            ).then_inc(sem_wq[q].h, 16)
                        g.dma_start(
                            out=w2c[:, q * H:(q + 1) * H].rearrange(
                                "p (hcq c) -> p hcq c", hcq=4),
                            in_=w2t_d[e * H + hq * 512:e * H + (hq + 1) * 512, :]
                            .rearrange("(hcq p) c -> p hcq c", p=128)
                            ).then_inc(sem_wq[q].h, 16)
                block.gpsimd(g_wload)
                sem_wq[0].inc(32 * 4)
                sem_wq[1].inc(32 * 4)

                vprev = prev_yv[0]

                def t_exp(t_, b=b, e=e, t0=t0, s0=s0, w0=w0,
                          base_buf=base_buf, vprev=vprev):
                    t_.wait_ge(sem_tr.h, buf_ready[(b, e)])
                    t_.wait_ge(sem_b2.h, b2_ready)
                    if vprev:
                        t_.wait_ge(sem_v.h, vprev)
                    last = None
                    for ncx in range(3):
                        for half in range(2):
                            last = t_.matmul(
                                psY[ncx][:, half * 512:(half + 1) * 512],
                                ones_row_bf[0:1, 0:NSZ[ncx]],
                                b2_bf[0:1, e * C + half * 512:
                                      e * C + (half + 1) * 512],
                                start=True, stop=False)
                    last.then_inc(sem_t.h, 1)

                    def l1(hc):
                        hq = hc // 4
                        q = hq % 2
                        t_.wait_ge(sem_wq[q].h, w0[q] + 32 * (hq // 2 + 1))
                        if hc >= 2:
                            t_.wait_ge(sem_s.h, s0 + hc - 1)
                        last = None
                        for cc in range(NCC):
                            last = t_.matmul(
                                psH[hc % 2][:],
                                w1c[:, q * H + cc * 512 + (hc % 4) * 128:
                                     q * H + cc * 512 + (hc % 4) * 128 + 128],
                                bufT[:, base_buf + cc * TSTRIDE:
                                     base_buf + cc * TSTRIDE + CAP],
                                start=(cc == 0), stop=(cc == NCC - 1))
                        last.then_inc(sem_t.h, 1)

                    def l2(hc):
                        hq = hc // 4
                        q = hq % 2
                        t_.wait_ge(sem_s.h, s0 + hc + 1)
                        last = None
                        for ncx in range(3):
                            for half in range(2):
                                last = t_.matmul(
                                    psY[ncx][:, half * 512:(half + 1) * 512],
                                    hbf[:, (hc % 2) * CAP + NOF[ncx]:
                                        (hc % 2) * CAP + NOF[ncx] + NSZ[ncx]],
                                    w2c[:, q * H + (hc % 4) * 1024 + half * 512:
                                        q * H + (hc % 4) * 1024 + (half + 1) * 512],
                                    start=False, stop=(hc == NHC - 1))
                        last.then_inc(sem_t.h, 1)

                    l1(0)
                    for hc in range(NHC):
                        if hc + 1 < NHC:
                            l1(hc + 1)
                        l2(hc)
                block.tensor(t_exp)
                sem_t.inc(1 + 2 * NHC)
                te_end = sem_t.n
                te_be[(b, e)] = te_end
                t0_prev_exp[0] = t0

                def s_relu(x, e=e, t0=t0):
                    for hc in range(NHC):
                        if hc == 0:
                            x.wait_ge(sem_t.h, t0 + 2)
                        else:
                            x.wait_ge(sem_t.h, t0 + 2 * hc + 2)
                        if hc >= 2:
                            x.wait_ge(sem_t.h, t0 + 2 * (hc - 2) + 4)
                        x.activation(out=hbf[:, (hc % 2) * CAP:(hc % 2) * CAP + CAP],
                                     in_=psH[hc % 2][:], func=AF.Relu,
                                     bias=b1_sb[:, e * NHC + hc:e * NHC + hc + 1],
                                     scale=1.0).then_inc(sem_s.h, 1)
                sca(s_relu)
                sem_s.inc(NHC)

                def v_yevac(v, e=e, eg=eg, te=te_end):
                    v.wait_ge(sem_t.h, te)
                    if eg >= 2:
                        v.wait_ge(sem_yfree.h, eg)
                    for ncx in range(3):
                        v.tensor_copy(out=y_stage[0:NSZ[ncx],
                                                  (eg % 3) * 3 * C + ncx * C:
                                                  (eg % 3) * 3 * C + (ncx + 1) * C],
                                      in_=psY[ncx][0:NSZ[ncx], :])
                    v.engine_nop().then_inc(sem_v.h, 1)
                vec(v_yevac)
                sem_v.inc(1)
                prev_yv[0] = sem_v.n
                yv = sem_v.n

                def s_ywr(sy, b=b, e=e, eg=eg, yv=yv):
                    if sem_yp[eg % 2].n:
                        sy.wait_ge(sem_yp[eg % 2].h, sem_yp[eg % 2].n)
                    sy.sem_inc(sem_yfree.h, 1)
                    sy.wait_ge(sem_v.h, yv)
                    r0 = (e * nb + b) * CAP
                    sy.dma_start(out=y_d[r0:r0 + 256, :].rearrange(
                        "(k p) c -> p k c", p=128),
                        in_=y_stage[:, (eg % 3) * 3 * C:(eg % 3) * 3 * C + 2 * C]
                        .rearrange("p (k c) -> p k c", k=2)
                        ).then_inc(sem_yp[eg % 2].h, 16)
                    sy.dma_start(out=y_d[r0 + 256:r0 + CAP, :],
                                 in_=y_stage[0:64, (eg % 3) * 3 * C + 2 * C:
                                             (eg % 3) * 3 * C + 3 * C]
                                 ).then_inc(sem_yp[eg % 2].h, 16)
                block.sync(s_ywr)
                sem_yp[eg % 2].inc(32)
                sem_yfree.inc(1)

                if b + 1 < nb:
                    emit_gather_e(b + 1, e, te_be.get((b - 1, e)))

            def s_ycert(sy):
                sy.wait_ge(sem_yp[0].h, sem_yp[0].n)
                sy.wait_ge(sem_yp[1].h, sem_yp[1].n)
                sy.sem_inc(sem_ydn.h, 1)
            block.sync(s_ycert)
            sem_ydn.inc(1)

            for tt in range(TPB):
                t = b * TPB + tt
                tp = tt % 2

                def g_yg(g, t=t, tp=tp, b=b):
                    g.wait_ge(sem_ydn.h, b + 1)
                    g.wait_ge(sem_v.h, sem_v.n)
                    g.indirect_dma_start(
                        out=y0t[:, tp * C:(tp + 1) * C], out_offset=None,
                        in_=y_d[:, :],
                        in_offset=IndirectOffsetOnAxis(
                            yidx_all[:, t * 2:t * 2 + 1], 0)
                    ).then_inc(sem_gyg[tp].h, 16)
                    g.indirect_dma_start(
                        out=y1t[:, tp * C:(tp + 1) * C], out_offset=None,
                        in_=y_d[:, :],
                        in_offset=IndirectOffsetOnAxis(
                            yidx_all[:, t * 2 + 1:t * 2 + 2], 0)
                    ).then_inc(sem_gyg[tp].h, 16)
                block.gpsimd(g_yg)
                sem_gyg[tp].inc(32)
                ygb = sem_gyg[tp].n

                def s_p3ld(sy, t=t, tp=tp):
                    if sem_op[tp].n:
                        sy.wait_ge(sem_op[tp].h, sem_op[tp].n)
                    sy.sem_inc(sem_ofree.h, 1)
                    sy.dma_start(out=sin_t[:, tp * C:(tp + 1) * C],
                                 in_=s_d[t * 128:(t + 1) * 128, :]
                                 ).then_inc(sem_p3p[tp].h, 16)
                    sy.dma_start(out=xr_t[:, tp * C:(tp + 1) * C],
                                 in_=x_d[t * 128:(t + 1) * 128, :]
                                 ).then_inc(sem_p3p[tp].h, 16)
                block.sync(s_p3ld)
                sem_p3p[tp].inc(32)
                sem_ofree.inc(1)
                p3l = sem_p3p[tp].n
                ofr = sem_ofree.n

                def v_p3(v, t=t, tp=tp, ygb=ygb, p3l=p3l):
                    v.wait_ge(sem_gyg[tp].h, ygb)
                    v.wait_ge(sem_p3p[tp].h, p3l)
                    v.tensor_scalar(out=accf[:], in0=y0t[:, tp * C:(tp + 1) * C],
                                    scalar1=c12_all[:, t * 2:t * 2 + 1],
                                    scalar2=None, op0=OP.mult)
                    v.scalar_tensor_tensor(out=accf[:],
                                           in0=y1t[:, tp * C:(tp + 1) * C],
                                           scalar=c12_all[:, t * 2 + 1:t * 2 + 2],
                                           in1=accf[:], op0=OP.mult, op1=OP.add
                                           ).then_inc(sem_v.h, 1)
                vec(v_p3)
                sem_v.inc(1)

                def s_p3(x):
                    x.wait_ge(sem_v.h, sem_v.n)
                    x.activation(out=gsil[:], in_=accf[:], func=AF.Silu,
                                 bias=0.0, scale=1.0).then_inc(sem_s.h, 1)
                sca(s_p3)
                sem_s.inc(1)

                def v_p3b(v, t=t, tp=tp, ofr=ofr):
                    v.wait_ge(sem_s.h, sem_s.n)
                    v.wait_ge(sem_ofree.h, ofr)
                    v.tensor_mul(out=gsil[:], in0=gsil[:],
                                 in1=sin_t[:, tp * C:(tp + 1) * C])
                    v.tensor_add(out=outst[:, tp * C:(tp + 1) * C],
                                 in0=xr_t[:, tp * C:(tp + 1) * C], in1=gsil[:]
                                 ).then_inc(sem_v.h, 1)
                vec(v_p3b)
                sem_v.inc(1)
                vp3 = sem_v.n

                def s_owr(sy, t=t, tp=tp, vp3=vp3):
                    sy.wait_ge(sem_v.h, vp3)
                    sy.dma_start(out=out_d[t * 128:(t + 1) * 128, :],
                                 in_=outst[:, tp * C:(tp + 1) * C]
                                 ).then_inc(sem_op[tp].h, 16)
                block.sync(s_owr)
                sem_op[tp].inc(16)

        def s_final(sy):
            sy.wait_ge(sem_op[0].h, sem_op[0].n)
            sy.wait_ge(sem_op[1].h, sem_op[1].n)
        block.sync(s_final)

        es_p2.close()
    es.close()
    return nc


_NC_CACHE = {}


def kernel(L0_attn_residual, L0_attn_w_fc, L0_attn_w_proj, L0_attn_ln_g,
           L0_attn_ln_b, L0_ffn_Wr, L0_ffn_W1, L0_ffn_b1, L0_ffn_W2, L0_ffn_b2):
    nb = 4
    n_cores = 8
    if "nc" not in _NC_CACHE:
        _NC_CACHE["nc"] = build_program(nb=nb)
    nc = _NC_CACHE["nc"]

    x = np.ascontiguousarray(np.asarray(L0_attn_residual, dtype=np.float32))
    B = x.shape[0]
    wfc = np.ascontiguousarray(np.asarray(L0_attn_w_fc, dtype=np.float32))
    wproj = np.ascontiguousarray(np.asarray(L0_attn_w_proj, dtype=np.float32))
    lng = np.ascontiguousarray(np.asarray(L0_attn_ln_g, dtype=np.float32)
                               .reshape(NCC, 128))
    lnb = np.ascontiguousarray(np.asarray(L0_attn_ln_b, dtype=np.float32)
                               .reshape(NCC, 128))
    wr = np.ascontiguousarray(np.asarray(L0_ffn_Wr, dtype=np.float32))
    w1 = np.ascontiguousarray(np.asarray(L0_ffn_W1, dtype=np.float32)
                              .reshape(E * C, H))
    b1 = np.ascontiguousarray(np.asarray(L0_ffn_b1, dtype=np.float32)
                              .reshape(E * H // 128, 128))
    w2 = np.ascontiguousarray(np.asarray(L0_ffn_W2, dtype=np.float32)
                              .reshape(E * H, C))
    b2 = np.ascontiguousarray(np.asarray(L0_ffn_b2, dtype=np.float32))

    in_maps = []
    for c in range(n_cores):
        xs = np.ascontiguousarray(
            x[c * nb:(c + 1) * nb].reshape(nb * T_SEQ, C))
        in_maps.append({"x": xs, "wfc": wfc, "wproj": wproj, "lng": lng,
                        "lnb": lnb, "wr": wr, "w1": w1, "b1": b1, "w2": w2,
                        "b2": b2})

    res = run_bass_kernel_spmd(nc, in_maps, core_ids=list(range(n_cores)))
    outs = [res.results[c]["out"].reshape(nb, T_SEQ, C) for c in range(n_cores)]
    return np.ascontiguousarray(np.concatenate(outs, axis=0))


# version 05
# speedup vs baseline: 1.1736x; 1.0127x over previous
import numpy as np
from contextlib import ExitStack

import concourse.bass as bass
import concourse.mybir as mybir
from concourse.bass_core_rust import IndirectOffsetOnAxis
from concourse.bass_utils import run_bass_kernel_spmd

F32 = mybir.dt.float32
BF16 = mybir.dt.bfloat16
U32 = mybir.dt.uint32
I16 = mybir.dt.int16
AF = mybir.ActivationFunctionType
OP = mybir.AluOpType

T_SEQ = 2048
C = 1024
E = 8
CAP = 320
H = 4096
QH = 4
NCC = C // 128
NSI = 32
GRP = 4
NHC = H // 128
EPS = 1e-5
TSTRIDE = 384
XN_RING = 9


class Sem:
    def __init__(self, es, nc, name):
        self.h = es.enter_context(nc.semaphore(name))
        self.n = 0

    def inc(self, k):
        self.n += k
        return self.n


_CHAIN_SKIP = {"wait_ge", "dma_start", "indirect_dma_start", "dma_start_transpose",
               "register", "reg_mov", "snap", "sem_inc", "wait_op", "nop"}


class _InstProxy:

    def __init__(self, eng, ch):
        self._eng = eng
        self._ch = ch

    def then_inc(self, sem, k):
        m = self._eng.sem_inc(sem, k)
        m._wait_ge(self._ch.h, self._ch.n)
        return self


class ChainEng:

    def __init__(self, eng, ch):
        object.__setattr__(self, "_e", eng)
        object.__setattr__(self, "_c", ch)

    def __getattr__(self, name):
        attr = getattr(self._e, name)
        if (not callable(attr)) or name in _CHAIN_SKIP or name.startswith("_"):
            return attr
        ch = self._c
        eng = self._e

        def wrap(*a, **k):
            inst = attr(*a, **k)
            if inst is not None and hasattr(inst, "then_inc"):
                if ch.n > 0:
                    inst._wait_ge(ch.h, ch.n)
                inst.then_inc(ch.h, 1)
                ch.n += 1
                return _InstProxy(eng, ch)
            return inst
        return wrap


def build_program(nb=4):
    NTOK = nb * T_SEQ
    NTIL = NTOK // 128
    TPB = T_SEQ // 128
    NGRP = NTIL // GRP
    GPB = TPB // GRP
    NSLOT = E * nb * CAP
    ZROW = NSLOT
    TBL_ROWS = ((E * nb * TSTRIDE + 127) // 128) * 128

    nc = bass.Bass("TRN2", target_bir_lowering=False)

    x_d = nc.dram_tensor("x", [NTOK, C], F32, kind="ExternalInput").ap()
    wfc_d = nc.dram_tensor("wfc", [C, C], F32, kind="ExternalInput").ap()
    wproj_d = nc.dram_tensor("wproj", [C, C], F32, kind="ExternalInput").ap()
    lng_d = nc.dram_tensor("lng", [NCC, 128], F32, kind="ExternalInput").ap()
    lnb_d = nc.dram_tensor("lnb", [NCC, 128], F32, kind="ExternalInput").ap()
    wr_d = nc.dram_tensor("wr", [C, E], F32, kind="ExternalInput").ap()
    w1_d = nc.dram_tensor("w1", [E * C, H], F32, kind="ExternalInput").ap()
    b1_d = nc.dram_tensor("b1", [E * H // 128, 128], F32, kind="ExternalInput").ap()
    w2_d = nc.dram_tensor("w2", [E * H, C], F32, kind="ExternalInput").ap()
    b2_d = nc.dram_tensor("b2", [E, C], F32, kind="ExternalInput").ap()
    out_d = nc.dram_tensor("out", [NTOK, C], F32, kind="ExternalOutput").ap()

    xn_d = nc.dram_tensor("xn_scr", [NTOK, C], BF16, kind="Internal").ap()
    s_d = nc.dram_tensor("s_scr", [NTOK, C], BF16, kind="Internal").ap()
    y_d = nc.dram_tensor("y_scr", [NSLOT + 1, C], BF16, kind="Internal").ap()
    tbl_d = nc.dram_tensor("tbl_scr", [TBL_ROWS, 2], U32, kind="Internal").ap()
    w1t_d = nc.dram_tensor("w1t_scr", [E * 8 * 128, H], BF16, kind="Internal").ap()
    w2t_d = nc.dram_tensor("w2t_scr", [E * H, C], BF16, kind="Internal").ap()

    es = ExitStack()
    es_p1 = ExitStack()

    def sb(stack, name, shape, dtype=F32):
        return stack.enter_context(nc.sbuf_tensor(name, shape, dtype))

    def ps(stack, name, shape, dtype=F32):
        return stack.enter_context(nc.psum_tensor(name, shape, dtype))

    ident_f = sb(es, "ident_f", [128, 128], F32)
    iota8 = sb(es, "iota8", [128, 8], F32)
    iotap = sb(es, "iotap", [128, 1], F32)
    iotad = sb(es, "iotad", [128, 128], I16)
    eps_ap = sb(es, "eps_ap", [128, 1], F32)
    ones_row_bf = sb(es, "ones_row_bf", [1, 512], BF16)
    ones_col_bf = sb(es, "ones_col_bf", [128, 1], BF16)
    g_sb = sb(es, "g_sb", [128, NCC], F32)
    battn_bf = sb(es, "battn_bf", [128, NCC], BF16)
    wr_hi = sb(es, "wr_hi", [128, NCC * E], BF16)
    wr_lo = sb(es, "wr_lo", [128, NCC * E], BF16)
    wrsum_bc = sb(es, "wrsum_bc", [128, E], F32)
    bk_sb = sb(es, "bk_sb", [128, NSI * QH], F32)
    b1_sb = sb(es, "b1_sb", [128, E * NHC], F32)
    c12_all = sb(es, "c12_all", [128, 2 * NTIL], F32)
    yidx_all = sb(es, "yidx_all", [128, 2 * NTIL], U32)
    zero_bf = sb(es, "zero_bf", [128, C], BF16)

    sem_xp = [Sem(es, nc, "sx0"), Sem(es, nc, "sx1")]
    sem_tr = Sem(es, nc, "s_tr")
    sem_gxn = Sem(es, nc, "s_gxn")
    sem_gxnc = Sem(es, nc, "s_gxnc")
    sem_gs = [Sem(es, nc, "s_gs0"), Sem(es, nc, "s_gs1")]
    sem_gsc = Sem(es, nc, "s_gsc")
    sem_ld = Sem(es, nc, "s_ld")
    sem_yp = [Sem(es, nc, "s_y0"), Sem(es, nc, "s_y1")]
    sem_yfree = Sem(es, nc, "s_yfree")
    sem_ydn = Sem(es, nc, "s_ydn")
    sem_gix = Sem(es, nc, "s_gix")
    sem_b2 = Sem(es, nc, "s_b2")
    sem_wp1 = [Sem(es, nc, "s_wp1a"), Sem(es, nc, "s_wp1b")]
    sem_ws1 = [Sem(es, nc, "s_ws1a"), Sem(es, nc, "s_ws1b")]
    sem_wp2 = [Sem(es, nc, "s_wp2a"), Sem(es, nc, "s_wp2b")]
    sem_ws2 = [Sem(es, nc, "s_ws2a"), Sem(es, nc, "s_ws2b")]
    sem_gg = [Sem(es, nc, "s_gg0"), Sem(es, nc, "s_gg1")]
    sem_gyg = [Sem(es, nc, "s_gyg0"), Sem(es, nc, "s_gyg1")]
    sem_p3p = [Sem(es, nc, "s_p30"), Sem(es, nc, "s_p31")]
    sem_op = [Sem(es, nc, "s_op0"), Sem(es, nc, "s_op1")]
    sem_ofree = Sem(es, nc, "s_ofree")
    sem_out = Sem(es, nc, "s_out")
    sem_t = Sem(es, nc, "s_t")
    sem_v = Sem(es, nc, "s_v")
    sem_s = Sem(es, nc, "s_s")
    sem_gc = Sem(es, nc, "s_gc")
    sem_ph = Sem(es, nc, "s_ph")
    sem_wq = [Sem(es, nc, "s_wq%d" % i) for i in range(4)]
    ch_v = Sem(es, nc, "chv")
    ch_s = Sem(es, nc, "chs")

    with nc.Block() as block:
        def vec(f):
            block.vector(lambda v: f(ChainEng(v, ch_v)))

        def sca(f):
            block.scalar(lambda x: f(ChainEng(x, ch_s)))

        def _phase_barrier():
            tgt = sem_ph.n + 5

            def _arr_t(t_):
                t_.sem_inc(sem_ph.h, 1)
                t_.wait_ge(sem_ph.h, tgt)
            block.tensor(_arr_t)

            def _arr_s(sy):
                sy.sem_inc(sem_ph.h, 1)
                sy.wait_ge(sem_ph.h, tgt)
            block.sync(_arr_s)

            def _arr_g(g):
                g.sem_inc(sem_ph.h, 1)
                g.wait_ge(sem_ph.h, tgt)
            block.gpsimd(_arr_g)

            def _arr_v(v):
                v.wait_ge(ch_v.h, ch_v.n)
                v.sem_inc(sem_ph.h, 1)
                v.wait_ge(sem_ph.h, tgt)
            block.vector(_arr_v)

            def _arr_sc(x):
                x.wait_ge(ch_s.h, ch_s.n)
                x.sem_inc(sem_ph.h, 1)
                x.wait_ge(sem_ph.h, tgt)
            block.scalar(_arr_sc)
            sem_ph.inc(5)

        def _dma_drain():
            def s_drain(sy):
                for sm_ in (sem_tr, sem_ld, sem_xp[0], sem_xp[1], sem_yp[0],
                            sem_yp[1], sem_out, sem_op[0], sem_op[1],
                            sem_p3p[0], sem_p3p[1]):
                    if sm_.n:
                        sy.wait_ge(sm_.h, sm_.n)
            block.sync(s_drain)

            def g_drain(g):
                for sm_ in (sem_gxn, sem_gs[0], sem_gs[1], sem_gsc, sem_gix,
                            sem_gg[0], sem_gg[1], sem_gyg[0], sem_gyg[1],
                            sem_wq[0], sem_wq[1], sem_wq[2], sem_wq[3], sem_b2,
                            sem_wp1[0], sem_wp1[1], sem_ws1[0], sem_ws1[1],
                            sem_wp2[0], sem_wp2[1], sem_ws2[0], sem_ws2[1]):
                    if sm_.n:
                        g.wait_ge(sm_.h, sm_.n)
            block.gpsimd(g_drain)
        st_sb = ExitStack()
        wfcT = sb(es_p1, "wfcT", [128, NCC * C], BF16)
        wprojT = sb(es_p1, "wprojT", [128, NCC * C], BF16)
        wstage = sb(st_sb, "wstage", [128, C], F32)
        wstage_bf = sb(st_sb, "wstage_bf", [128, 2 * C], BF16)
        small_f = sb(st_sb, "small_f", [128, 128], F32)
        small_f2 = sb(st_sb, "small_f2", [128, 128], F32)
        wsum8 = sb(st_sb, "wsum8", [8, 1], F32)
        wsrow_bf = sb(st_sb, "wsrow_bf", [1, 8], BF16)
        ztbl = sb(st_sb, "ztbl", [128, TBL_ROWS * 2 // 128], U32)
        ps_set = ps(st_sb, "ps_set", [128, 512], F32)
        ps_set2 = ps(st_sb, "ps_set2", [128, 8], F32)

        def g_setup(g):
            g.iota(iotad[:], pattern=[[1, 128]], base=0, channel_multiplier=-1)
            g.iota(iota8[:], pattern=[[1, 8]], base=0, channel_multiplier=0,
                   allow_small_or_imprecise_dtypes=True)
            g.iota(iotap[:], pattern=[[0, 1]], base=0, channel_multiplier=1,
                   allow_small_or_imprecise_dtypes=True)
            g.memset(ztbl[:], 0).then_inc(sem_gc.h, 1)
        block.gpsimd(g_setup)
        sem_gc.inc(1)

        def v_setup0(v):
            v.wait_ge(sem_gc.h, sem_gc.n)
            v.memset(eps_ap[:], EPS)
            v.memset(ones_row_bf[:], 1.0)
            v.memset(ones_col_bf[:], 1.0)
            v.memset(zero_bf[:], 0.0)
            v.tensor_scalar(out=ident_f[:], in0=iotad[:], scalar1=0.0, scalar2=None,
                            op0=OP.is_equal).then_inc(sem_v.h, 1)
        vec(v_setup0)
        sem_v.inc(1)

        def g_zero(g):
            g.wait_ge(sem_v.h, sem_v.n)
            g.dma_start(out=tbl_d.rearrange("(p k) c -> p (k c)", p=128),
                        in_=ztbl[:]).then_inc(sem_gxn.h, 16)
            g.dma_start(out=y_d[ZROW:ZROW + 1, :],
                        in_=zero_bf[0:1, :]).then_inc(sem_gxn.h, 16)
        block.gpsimd(g_zero)
        sem_gxn.inc(32)

        def s_loads(sy):
            sy.dma_start(out=small_f[0:NCC, :], in_=lng_d[:, :]).then_inc(sem_ld.h, 16)
            sy.dma_start(out=small_f2[0:NCC, :], in_=lnb_d[:, :]).then_inc(sem_ld.h, 16)
            sy.dma_start(out=wstage[:, 0:NCC * E].rearrange("p (cc e) -> p cc e", e=E),
                         in_=wr_d[:, :].rearrange("(cc p) e -> p cc e", p=128)
                         ).then_inc(sem_ld.h, 16)
        block.sync(s_loads)
        sem_ld.inc(48)

        def t_gb(t):
            t.wait_ge(sem_v.h, sem_v.n)
            t.wait_ge(sem_ld.h, sem_ld.n)
            t.transpose(ps_set[:, 0:8], small_f[0:NCC, :], ident_f[0:NCC, 0:NCC])
            t.transpose(ps_set[:, 8:16], small_f2[0:NCC, :],
                        ident_f[0:NCC, 0:NCC]).then_inc(sem_t.h, 1)
        block.tensor(t_gb)
        sem_t.inc(1)

        def v_gb(v):
            v.wait_ge(sem_t.h, sem_t.n)
            v.tensor_copy(out=g_sb[:], in_=ps_set[:, 0:8])
            v.tensor_copy(out=battn_bf[:], in_=ps_set[:, 8:16])
            v.tensor_copy(out=wr_hi[:], in_=wstage[:, 0:NCC * E])
            v.tensor_sub(out=wr_lo[:], in0=wstage[:, 0:NCC * E], in1=wr_hi[:]
                         ).then_inc(sem_v.h, 1)
        vec(v_gb)
        sem_v.inc(1)

        def s_b1ld(sy):
            sy.wait_ge(sem_t.h, sem_t.n)
            sy.dma_start(out=small_f[:], in_=b1_d[0:128, :]).then_inc(sem_ld.h, 16)
            sy.dma_start(out=small_f2[:], in_=b1_d[128:256, :]).then_inc(sem_ld.h, 16)
        block.sync(s_b1ld)
        sem_ld.inc(32)


        def t_b1(t):
            t.wait_ge(sem_ld.h, sem_ld.n)
            t.wait_ge(sem_v.h, sem_v.n)
            t.transpose(ps_set[:, 0:128], small_f[:], ident_f[:])
            t.transpose(ps_set[:, 128:256], small_f2[:], ident_f[:]
                        ).then_inc(sem_t.h, 1)
        block.tensor(t_b1)
        sem_t.inc(1)

        def v_b1(v):
            v.wait_ge(sem_t.h, sem_t.n)
            v.tensor_copy(out=b1_sb[:], in_=ps_set[:, 0:256]).then_inc(sem_v.h, 1)
        vec(v_b1)
        sem_v.inc(1)

        ntr = 0
        for (src, dst) in ((wfc_d, wfcT), (wproj_d, wprojT)):
            for rc in range(8):
                slot = ntr % 2

                def s_wload(sy, src=src, rc=rc, vw=None):
                    sy.wait_ge(sem_v.h, sem_v.n)
                    sy.dma_start(out=wstage[:], in_=src[rc * 128:(rc + 1) * 128, :]
                                 ).then_inc(sem_ld.h, 16)
                block.sync(s_wload)
                sem_ld.inc(16)

                def v_wcast(v, slot=slot, ntr=ntr):
                    v.wait_ge(sem_ld.h, sem_ld.n)
                    if ntr >= 2:
                        v.wait_ge(sem_tr.h, 16 * (ntr - 1))
                    v.tensor_copy(out=wstage_bf[:, slot * C:(slot + 1) * C],
                                  in_=wstage[:]).then_inc(sem_v.h, 1)
                vec(v_wcast)
                sem_v.inc(1)

                def s_wtr(sy, dst=dst, rc=rc, slot=slot):
                    sy.wait_ge(sem_v.h, sem_v.n)
                    if sem_tr.n:
                        sy.wait_ge(sem_tr.h, sem_tr.n)
                    outap = dst[:, :].rearrange("p (cc r) -> p cc r", cc=NCC)[
                        :, :, rc * 128:(rc + 1) * 128]
                    sy.dma_start(out=outap, in_=wstage_bf[:, slot * C:(slot + 1) * C],
                                 transpose=True).then_inc(sem_tr.h, 16)
                block.sync(s_wtr)
                sem_tr.inc(16)
                ntr += 1

        for si in range(NSI):
            q = si // 8

            def t_bk(t, si=si, q=q):
                if si == 0:
                    t.wait_ge(sem_tr.h, sem_tr.n)
                t.wait_ge(sem_v.h, sem_v.n)
                last = None
                for h in range(QH):
                    for dc in range(2):
                        base = (2 * q + dc) * C + (si % 8) * 128
                        last = t.matmul(ps_set2[:, h:h + 1],
                                        wfcT[:, base:base + 128],
                                        battn_bf[:, 2 * h + dc:2 * h + dc + 1],
                                        start=(dc == 0), stop=(dc == 1))
                last.then_inc(sem_t.h, 1)
            block.tensor(t_bk)
            sem_t.inc(1)

            def v_bk(v, si=si):
                v.wait_ge(sem_t.h, sem_t.n)
                v.tensor_copy(out=bk_sb[:, si * 4:si * 4 + 4],
                              in_=ps_set2[:, 0:4]).then_inc(sem_v.h, 1)
            vec(v_bk)
            sem_v.inc(1)

        def t_ws(t):
            t.wait_ge(sem_v.h, sem_v.n)
            last = None
            for i, w in enumerate((wr_hi, wr_lo)):
                for cc in range(NCC):
                    last = t.matmul(ps_set2[0:8, 0:1], w[:, cc * 8:cc * 8 + 8],
                                    ones_col_bf[:, :], start=(i == 0 and cc == 0),
                                    stop=(i == 1 and cc == NCC - 1))
            last.then_inc(sem_t.h, 1)
        block.tensor(t_ws)
        sem_t.inc(1)

        def v_ws(v):
            v.wait_ge(sem_t.h, sem_t.n)
            v.tensor_copy(out=wsum8[:], in_=ps_set2[0:8, 0:1]).then_inc(sem_v.h, 1)
        vec(v_ws)
        sem_v.inc(1)

        def t_ws2(t):
            t.wait_ge(sem_v.h, sem_v.n)
            t.transpose(ps_set2[0:1, 0:8], wsum8[:], ident_f[0:8, 0:8]
                        ).then_inc(sem_t.h, 1)
        block.tensor(t_ws2)
        sem_t.inc(1)

        def v_ws2(v):
            v.wait_ge(sem_t.h, sem_t.n)
            v.tensor_copy(out=wsrow_bf[:], in_=ps_set2[0:1, 0:8]).then_inc(sem_v.h, 1)
        vec(v_ws2)
        sem_v.inc(1)

        def t_ws3(t):
            t.wait_ge(sem_v.h, sem_v.n)
            t.matmul(ps_set2[:, 0:8], ones_row_bf[0:1, 0:128], wsrow_bf[:],
                     start=True, stop=True).then_inc(sem_t.h, 1)
        block.tensor(t_ws3)
        sem_t.inc(1)

        def v_ws3(v):
            v.wait_ge(sem_t.h, sem_t.n)
            v.tensor_copy(out=wrsum_bc[:], in_=ps_set2[:, 0:8]).then_inc(sem_v.h, 1)
        vec(v_ws3)
        sem_v.inc(1)

        _dma_drain()
        _phase_barrier()
        st_sb.close()

        xb = sb(es_p1, "xb", [128, 2 * C], F32)
        xnf = sb(es_p1, "xnf", [128, 2 * C], F32)
        xnhi = sb(es_p1, "xnhi", [128, XN_RING * C], BF16)
        zf = sb(es_p1, "zf", [128, C], F32)
        zhi = sb(es_p1, "zhi", [128, 2 * C], BF16)
        zlo = sb(es_p1, "zlo", [128, 2 * C], BF16)
        zThi = sb(es_p1, "zThi", [128, 2 * NCC * 512], BF16)
        zTlo = sb(es_p1, "zTlo", [128, 2 * NCC * 512], BF16)
        zgT = sb(es_p1, "zgT", [128, NCC * 512], BF16)
        pT = sb(es_p1, "pT", [128, NSI * 512], BF16)
        s_stage = sb(es_p1, "s_stage", [128, 2 * GRP * C], BF16)
        routb = sb(es_p1, "routb", [8, 2 * T_SEQ], F32)
        hT = routb[0:8, 0:T_SEQ]
        cex = routb[0:8, T_SEQ:2 * T_SEQ]
        is1_b = sb(es_p1, "is1_b", [128, 8 * TPB], F32)
        is2_b = sb(es_p1, "is2_b", [128, 8 * TPB], F32)
        p12_b = sb(es_p1, "p12_b", [128, 2 * TPB], F32)
        e12_b = sb(es_p1, "e12_b", [128, 2 * TPB], F32)
        sd_ring = sb(es_p1, "sd_ring", [128, GRP], F32)
        mu_ring = sb(es_p1, "mu_ring", [128, GRP], F32)
        bns = sb(es_p1, "bns", [128, 12], F32)
        bn2 = sb(es_p1, "bn2", [128, 2], F32)
        bn2i = sb(es_p1, "bn2i", [128, 2], F32)
        lnv = sb(es_p1, "lnv", [128, 1], F32)
        r2 = sb(es_p1, "r2", [128, 1], F32)
        vprime = sb(es_p1, "vprime", [128, 1], F32)
        r1 = sb(es_p1, "r1", [128, 1], F32)
        nmr = sb(es_p1, "nmr", [128, 1], F32)
        lg8 = sb(es_p1, "lg8", [128, 8], F32)
        m8 = sb(es_p1, "m8", [128, 8], F32)
        i8u = sb(es_p1, "i8u", [128, 8], U32)
        e8x = sb(es_p1, "e8x", [128, 8], F32)
        zsum = sb(es_p1, "zsum", [128, 1], F32)
        rz = sb(es_p1, "rz", [128, 1], F32)
        negm1 = sb(es_p1, "negm1", [128, 1], F32)
        p12e = sb(es_p1, "p12e", [128, 2], F32)
        hmaskf = sb(es_p1, "hmaskf", [128, 8], F32)
        scr8 = sb(es_p1, "scr8", [128, 8], F32)
        pos12 = sb(es_p1, "pos12", [128, 2], F32)
        keep12 = sb(es_p1, "keep12", [128, 2], F32)
        slotf = sb(es_p1, "slotf", [128, 1], F32)
        tmp1 = sb(es_p1, "tmp1", [128, 1], F32)
        payload = sb(es_p1, "payload", [128, 2 * TPB], U32)
        idxa_u = sb(es_p1, "idxa_u", [128, 2 * TPB], U32)
        tokf = sb(es_p1, "tokf", [128, 1], F32)

        wpre1 = sb(es_p1, "wpre1", [128, 2 * 2048], BF16)
        wpre2 = sb(es_p1, "wpre2", [128, 2 * C], BF16)

        psA = [ps(es_p1, f"psA{i}", [128, 512], F32) for i in range(2)]
        psS2 = [ps(es_p1, "psS2a", [128, 256], F32), ps(es_p1, "psS2b", [128, 256], F32)]
        psR = ps(es_p1, "psR", [128, 8], F32)
        psTr = ps(es_p1, "psTr", [128, 128], F32)

        def v_p1init(v):
            v.memset(routb[:, :], 0.0)
        vec(v_p1init)

        def s_x0(sy):
            sy.dma_start(out=xb[:, 0:C], in_=x_d[0:128, :]).then_inc(sem_xp[0].h, 16)
        block.sync(s_x0)
        sem_xp[0].inc(16)

        vx_done = [0] * NTIL
        vz_done = [0] * NTIL
        tr_done = [0] * NTIL
        vh_done = [0] * NTIL
        vsm1_done = [0] * NTIL
        ht_evac = [0] * NTIL
        t_att_end = [0] * NGRP
        v_s2_done = [0] * NGRP
        g_sw_done = [0] * NGRP
        endb_v_done = [0] * nb
        scat_g_done = [0] * nb

        def emit_tile(t):
            b = t // TPB
            tt = t % TPB
            gg = t // GRP
            tig = t % GRP
            gp = gg % 2
            xslot = t % 2
            xnslot = t % XN_RING

            def v_ln1(v):
                v.wait_ge(sem_xp[t % 2].h, 16 * (t // 2 + 1))
                if t >= XN_RING:
                    v.wait_ge(sem_v.h, v_s2_done[(t - XN_RING) // GRP])
                    v.wait_ge(sem_gxnc.h, t - XN_RING + 2)
                v.bn_stats(out=bns[:, 0:6], in_=xb[:, xslot * C:xslot * C + 512])
                v.bn_stats(out=bns[:, 6:12],
                           in_=xb[:, xslot * C + 512:xslot * C + 1024])
                v.bn_aggr(out=bn2[:], in_=bns[:]).then_inc(sem_v.h, 1)
            vec(v_ln1)
            sem_v.inc(1)

            def s_rs1(x):
                x.wait_ge(sem_v.h, sem_v.n)
                x.activation(out=lnv[:], in_=bn2[:, 1:2], func=AF.Ln,
                             bias=eps_ap[:], scale=1.0)
                x.activation(out=r1[:], in_=lnv[:], func=AF.Exp, bias=0.0,
                             scale=-0.5)
                x.activation(out=r2[:], in_=lnv[:], func=AF.Exp, bias=0.0,
                             scale=-1.0).then_inc(sem_s.h, 1)
            sca(s_rs1)
            sem_s.inc(1)

            def v_ln2(v):
                v.wait_ge(sem_s.h, sem_s.n)
                v.tensor_scalar(out=nmr[:], in0=bn2[:, 0:1], scalar1=r1[:],
                                scalar2=-1.0, op0=OP.mult, op1=OP.mult)
                v.tensor_scalar(out=xnf[:, xslot * C:(xslot + 1) * C],
                                in0=xb[:, xslot * C:(xslot + 1) * C],
                                scalar1=r1[:], scalar2=nmr[:],
                                op0=OP.mult, op1=OP.add).then_inc(sem_v.h, 1)
                v.tensor_copy(out=xnhi[:, xnslot * C:(xnslot + 1) * C],
                              in_=xnf[:, xslot * C:(xslot + 1) * C])
                v.tensor_mul(out=vprime[:], in0=bn2[:, 1:2], in1=r2[:]
                             ).then_inc(sem_v.h, 1)
            vec(v_ln2)
            sem_v.inc(1)
            vx_done[t] = sem_v.n
            sem_v.inc(1)
            ln2_done = sem_v.n

            def s_rs2(x):
                x.wait_ge(sem_v.h, sem_v.n)
                x.activation(out=lnv[:], in_=vprime[:], func=AF.Ln,
                             bias=eps_ap[:], scale=1.0)
                x.activation(out=r1[:], in_=lnv[:], func=AF.Exp, bias=0.0,
                             scale=-0.5)
                x.activation(out=sd_ring[:, tig:tig + 1], in_=lnv[:], func=AF.Exp,
                             bias=0.0, scale=0.5).then_inc(sem_s.h, 1)
            sca(s_rs2)
            sem_s.inc(1)

            def v_ln3(v):
                v.wait_ge(sem_s.h, sem_s.n)
                if t >= 2:
                    v.wait_ge(sem_tr.h, tr_done[t - 2])
                v.tensor_scalar(out=zf[:], in0=xnf[:, xslot * C:(xslot + 1) * C],
                                scalar1=r1[:], scalar2=None, op0=OP.mult)
                v.tensor_copy(out=zhi[:, xslot * C:(xslot + 1) * C], in_=zf[:])
                v.tensor_sub(out=zlo[:, xslot * C:(xslot + 1) * C], in0=zf[:],
                             in1=zhi[:, xslot * C:(xslot + 1) * C]
                             ).then_inc(sem_v.h, 1)
            vec(v_ln3)
            sem_v.inc(1)
            vz_done[t] = sem_v.n

            def g_xn(g, ln2_done=None):
                g.wait_ge(sem_v.h, ln2_done)
                if sem_gxn.n:
                    g.wait_ge(sem_gxn.h, sem_gxn.n)
                g.sem_inc(sem_gxnc.h, 1)
                g.dma_start(out=xn_d[t * 128:(t + 1) * 128, :],
                            in_=xnhi[:, xnslot * C:(xnslot + 1) * C]
                            ).then_inc(sem_gxn.h, 16)
            block.gpsimd(lambda g, l=ln2_done: g_xn(g, l))
            sem_gxn.inc(16)
            sem_gxnc.inc(1)

            def g_wpre(g):
                def w1_load(u):
                    e1, cc, hf = u // 16, (u // 2) % 8, u % 2
                    sl = u % 2
                    if u >= 2:
                        g.wait_ge(sem_ws1[sl].h, sem_ws1[sl].n)
                    g.dma_start(out=wpre1[:, sl * 2048:(sl + 1) * 2048],
                                in_=w1_d[e1 * C + cc * 128:e1 * C + (cc + 1) * 128,
                                         hf * 2048:(hf + 1) * 2048],
                                max_dma_last_dim=8192).then_inc(sem_wp1[sl].h, 16)
                    sem_wp1[sl].inc(16)

                def w1_store(u):
                    e1, cc, hf = u // 16, (u // 2) % 8, u % 2
                    sl = u % 2
                    g.wait_ge(sem_wp1[sl].h, sem_wp1[sl].n)
                    outap = w1t_d[(e1 * 8 + hf * 4) * 128:
                                  (e1 * 8 + hf * 4 + 4) * 128,
                                  cc * 512:(cc + 1) * 512].rearrange(
                        "(hq p) jj -> p hq jj", p=128)
                    g.dma_start(out=outap,
                                in_=wpre1[:, sl * 2048:(sl + 1) * 2048].rearrange(
                                    "p (hq jj) -> p hq jj", hq=4)
                                ).then_inc(sem_ws1[sl].h, 16)
                    sem_ws1[sl].inc(16)

                def w2_load(u2):
                    s2 = u2 % 2
                    e2, k2 = u2 // 32, u2 % 32
                    if u2 >= 2:
                        g.wait_ge(sem_ws2[s2].h, sem_ws2[s2].n)
                    g.dma_start(out=wpre2[:, s2 * C:(s2 + 1) * C],
                                in_=w2_d[e2 * H + k2 * 128:
                                         e2 * H + (k2 + 1) * 128, :],
                                max_dma_last_dim=8192).then_inc(sem_wp2[s2].h, 16)
                    sem_wp2[s2].inc(16)

                def w2_store(u2):
                    s2 = u2 % 2
                    e2, k2 = u2 // 32, u2 % 32
                    g.wait_ge(sem_wp2[s2].h, sem_wp2[s2].n)
                    g.dma_start(out=w2t_d[e2 * H + k2 * 128:
                                          e2 * H + (k2 + 1) * 128, :],
                                in_=wpre2[:, s2 * C:(s2 + 1) * C]
                                ).then_inc(sem_ws2[s2].h, 16)
                    sem_ws2[s2].inc(16)

                n1 = (128 + NTIL - 1) // NTIL
                n2 = (256 + NTIL - 1) // NTIL
                for k in range(n1):
                    u = n1 * t + k
                    if u < 128:
                        w1_load(u)
                        if u >= 1:
                            w1_store(u - 1)
                        if u == 127:
                            w1_store(u)
                for k in range(n2):
                    u2 = n2 * t + k
                    if u2 < 256:
                        w2_load(u2)
                        if u2 >= 1:
                            w2_store(u2 - 1)
                        if u2 == 255:
                            w2_store(u2)
            block.gpsimd(g_wpre)

            def s_tile(sy):
                if t + 1 < NTIL:
                    nxt = (t + 1) % 2
                    if t >= 1:
                        sy.wait_ge(sem_v.h, vx_done[t])
                    sy.dma_start(out=xb[:, nxt * C:(nxt + 1) * C],
                                 in_=x_d[(t + 1) * 128:(t + 2) * 128, :]
                                 ).then_inc(sem_xp[(t + 1) % 2].h, 16)
                sy.wait_ge(sem_v.h, vz_done[t])
                if sem_tr.n:
                    sy.wait_ge(sem_tr.h, sem_tr.n)
                if gg >= 2:
                    sy.wait_ge(sem_t.h, t_att_end[gg - 2])
                outhi = zThi[:, gp * NCC * 512:(gp + 1) * NCC * 512].rearrange(
                    "p (cc q) -> p cc q", cc=NCC)[:, :, tig * 128:(tig + 1) * 128]
                sy.dma_start(out=outhi, in_=zhi[:, xslot * C:(xslot + 1) * C],
                             transpose=True).then_inc(sem_tr.h, 16)
                sy.wait_ge(sem_tr.h, sem_tr.n + 16)
                outlo = zTlo[:, gp * NCC * 512:(gp + 1) * NCC * 512].rearrange(
                    "p (cc q) -> p cc q", cc=NCC)[:, :, tig * 128:(tig + 1) * 128]
                sy.dma_start(out=outlo, in_=zlo[:, xslot * C:(xslot + 1) * C],
                             transpose=True).then_inc(sem_tr.h, 16)
            block.sync(s_tile)
            if t + 1 < NTIL:
                sem_xp[(t + 1) % 2].inc(16)
            sem_tr.inc(32)
            tr_done[t] = sem_tr.n

            def t_router(t_):
                t_.wait_ge(sem_tr.h, tr_done[t])
                if t >= 1:
                    t_.wait_ge(sem_v.h, vsm1_done[t - 1])
                last = None
                n = 0
                for cc in range(NCC):
                    base = gp * NCC * 512 + cc * 512 + tig * 128
                    for lhs, rhs in ((zThi, wr_hi), (zThi, wr_lo), (zTlo, wr_hi)):
                        n += 1
                        last = t_.matmul(psR[:], lhs[:, base:base + 128],
                                         rhs[:, cc * 8:cc * 8 + 8],
                                         start=(n == 1), stop=(n == 3 * NCC))
                last.then_inc(sem_t.h, 1)
            block.tensor(t_router)
            sem_t.inc(1)
            rd_t = sem_t.n

            def v_sm1(v):
                v.wait_ge(sem_t.h, rd_t)
                v.tensor_scalar(out=lg8[:], in0=psR[:],
                                scalar1=sd_ring[:, tig:tig + 1], scalar2=None,
                                op0=OP.mult)
                v.max(out=m8[:], in_=lg8[:])
                v.max_index(out=i8u[:], in_max=m8[:], in_values=lg8[:])
                v.tensor_scalar(out=negm1[:], in0=m8[:, 0:1], scalar1=-1.0,
                                scalar2=None, op0=OP.mult).then_inc(sem_v.h, 1)
            vec(v_sm1)
            sem_v.inc(1)
            vsm1_done[t] = sem_v.n

            def s_exp(s):
                s.wait_ge(sem_v.h, sem_v.n)
                s.activation(out=e8x[:], in_=lg8[:], func=AF.Exp, bias=negm1[:],
                             scale=1.0, accum_out=zsum[:])
                s.activation(out=p12e[:], in_=m8[:, 0:2], func=AF.Exp, bias=negm1[:],
                             scale=1.0).then_inc(sem_s.h, 1)
            sca(s_exp)
            sem_s.inc(1)

            def v_sm2(v):
                v.wait_ge(sem_s.h, sem_s.n)
                v.reciprocal(out=rz[:], in_=zsum[:])
                v.tensor_scalar(out=p12_b[:, tt * 2:tt * 2 + 2], in0=p12e[:],
                                scalar1=rz[:], scalar2=None, op0=OP.mult)
                v.tensor_copy(out=e12_b[:, tt * 2:tt * 2 + 1], in_=i8u[:, 0:1])
                v.tensor_copy(out=e12_b[:, tt * 2 + 1:tt * 2 + 2], in_=i8u[:, 1:2])
                v.tensor_scalar(out=is1_b[:, tt * 8:tt * 8 + 8], in0=iota8[:],
                                scalar1=e12_b[:, tt * 2:tt * 2 + 1], scalar2=None,
                                op0=OP.is_equal)
                v.tensor_scalar(out=is2_b[:, tt * 8:tt * 8 + 8], in0=iota8[:],
                                scalar1=e12_b[:, tt * 2 + 1:tt * 2 + 2],
                                scalar2=None, op0=OP.is_equal)
                v.tensor_add(out=hmaskf[:], in0=is1_b[:, tt * 8:tt * 8 + 8],
                             in1=is2_b[:, tt * 8:tt * 8 + 8]).then_inc(sem_v.h, 1)
            vec(v_sm2)
            sem_v.inc(1)
            vh_done[t] = sem_v.n

            def t_hm(t_):
                t_.wait_ge(sem_v.h, vh_done[t])
                if tt:
                    t_.wait_ge(sem_v.h, ht_evac[t - 1])
                elif b:
                    t_.wait_ge(sem_v.h, endb_v_done[b - 1])
                t_.transpose(psTr[0:8, 0:128], hmaskf[:], ident_f[:]
                             ).then_inc(sem_t.h, 1)
            block.tensor(t_hm)
            sem_t.inc(1)
            hm_t = sem_t.n

            def v_hm(v):
                v.wait_ge(sem_t.h, hm_t)
                v.tensor_copy(out=routb[0:8, tt * 128:(tt + 1) * 128],
                              in_=psTr[0:8, 0:128]).then_inc(sem_v.h, 1)
            vec(v_hm)
            sem_v.inc(1)
            ht_evac[t] = sem_v.n

        def emit_attn_group(gg):
            gp = gg % 2
            sp = gg % 2

            def v_zg(v):
                v.wait_ge(sem_tr.h, tr_done[gg * GRP + GRP - 1])
                if gg >= 1:
                    v.wait_ge(sem_t.h, t_att_end[gg - 1])
                for cc in range(NCC):
                    ins = v.tensor_scalar(
                        out=zgT[:, cc * 512:(cc + 1) * 512],
                        in0=zThi[:, gp * NCC * 512 + cc * 512:
                                 gp * NCC * 512 + (cc + 1) * 512],
                        scalar1=g_sb[:, cc:cc + 1], scalar2=None, op0=OP.mult)
                ins.then_inc(sem_v.h, 1)
            vec(v_zg)
            sem_v.inc(1)
            vzg = sem_v.n

            t0 = sem_t.n
            v0 = sem_v.n
            s0 = sem_s.n

            def t_attn(t_):
                t_.wait_ge(sem_v.h, vzg)
                t_.wait_ge(sem_s.h, s0)
                if gg >= 1:
                    t_.wait_ge(sem_v.h, v_s2_done[gg - 1])
                for h in range(QH):
                    for si in range(NSI):
                        q = si // 8
                        k = h * 32 + si
                        if k >= 2:
                            t_.wait_ge(sem_s.h, s0 + k - 1)
                        last = None
                        for dc in range(2):
                            kb = (2 * q + dc) * C + (si % 8) * 128
                            last = t_.matmul(
                                psA[si % 2][:], wfcT[:, kb:kb + 128],
                                zgT[:, (2 * h + dc) * 512:(2 * h + dc + 1) * 512],
                                start=(dc == 0), stop=(dc == 1))
                        last.then_inc(sem_t.h, 1)
                    for tc in range(GRP):
                        if not (h == 0 and tc < 2):
                            t_.wait_ge(sem_v.h, v0 + h * 4 + tc - 1)
                        t_.wait_ge(sem_s.h, s0 + h * 32 + 32)
                        last = None
                        for si in range(NSI):
                            q = si // 8
                            last = t_.matmul(
                                psS2[tc % 2][:],
                                pT[:, si * 512 + tc * 128:si * 512 + tc * 128 + 128],
                                wprojT[:, (si % 8) * C + q * 256:
                                       (si % 8) * C + q * 256 + 256],
                                start=(si == 0), stop=(si == NSI - 1))
                        last.then_inc(sem_t.h, 1)
            block.tensor(t_attn)
            sem_t.inc(QH * (NSI + GRP))
            t_att_end[gg] = sem_t.n

            def s_attn(x, t0=t0, s0=s0):
                for h in range(QH):
                    for si in range(NSI):
                        x.wait_ge(sem_t.h, t0 + h * 36 + si + 1)
                        x.activation(out=psA[si % 2][:], in_=psA[si % 2][:],
                                     func=AF.Relu,
                                     bias=bk_sb[:, si * 4 + h:si * 4 + h + 1],
                                     scale=1.0)
                        x.activation(out=pT[:, si * 512:(si + 1) * 512],
                                     in_=psA[si % 2][:], func=AF.Square
                                     ).then_inc(sem_s.h, 1)
            sca(s_attn)
            sem_s.inc(QH * NSI)

            def v_attn(v):
                if gg >= 2:
                    v.wait_ge(sem_gs[gg % 2].h, g_sw_done[gg - 2])
                for h in range(QH):
                    for tc in range(GRP):
                        v.wait_ge(sem_t.h, t0 + h * 36 + 32 + tc + 1)
                        xsl = ((gg * GRP + tc) % XN_RING) * C + h * 256
                        v.scalar_tensor_tensor(
                            out=s_stage[:, (sp * GRP + tc) * C + h * 256:
                                        (sp * GRP + tc) * C + h * 256 + 256],
                            in0=psS2[tc % 2][:],
                            scalar=0.0, in1=xnhi[:, xsl:xsl + 256],
                            op0=OP.add, op1=OP.add).then_inc(sem_v.h, 1)
            vec(v_attn)
            sem_v.inc(QH * GRP)
            v_s2_done[gg] = sem_v.n

            def g_sw(g):
                for tc in range(GRP):
                    g.wait_ge(sem_v.h, v0 + 3 * 4 + tc + 1)
                    tg = gg * GRP + tc
                    g.dma_start(out=s_d[tg * 128:(tg + 1) * 128, :],
                                in_=s_stage[:, (sp * GRP + tc) * C:
                                            (sp * GRP + tc + 1) * C]
                                ).then_inc(sem_gs[gg % 2].h, 16)
            block.gpsimd(g_sw)
            sem_gs[gg % 2].inc(16 * GRP)
            g_sw_done[gg] = sem_gs[gg % 2].n

        def emit_endb(b):
            def v_scan(v):
                if b >= 1:
                    v.wait_ge(sem_gsc.h, scat_g_done[b - 1])
                v.tensor_tensor_scan(out=cex, data0=hT, data1=hT,
                                     initial=0.0, op0=OP.add, op1=OP.max)
                v.tensor_sub(out=cex, in0=cex, in1=hT).then_inc(sem_v.h, 1)
            vec(v_scan)
            sem_v.inc(1)
            vscan = sem_v.n
            vch = [0] * TPB

            for tt in range(TPB):
                t = b * TPB + tt

                def t_bt(t_, tt=tt):
                    t_.wait_ge(sem_v.h, vscan if tt == 0 else sem_v.n)
                    t_.transpose(psTr[:, 0:8], routb[0:8, T_SEQ + tt * 128:T_SEQ + (tt + 1) * 128],
                                 ident_f[0:8, 0:8]).then_inc(sem_t.h, 1)
                block.tensor(t_bt)
                sem_t.inc(1)
                bt_t = sem_t.n

                def v_chain(v, tt=tt, t=t, b=b, bt_t=bt_t):
                    v.wait_ge(sem_t.h, bt_t)
                    v.tensor_mul(out=scr8[:], in0=psTr[:, 0:8],
                                 in1=is1_b[:, tt * 8:tt * 8 + 8])
                    v.reduce_sum(out=pos12[:, 0:1], in_=scr8[:],
                                 axis=mybir.AxisListType.X)
                    v.tensor_mul(out=scr8[:], in0=psTr[:, 0:8],
                                 in1=is2_b[:, tt * 8:tt * 8 + 8])
                    v.reduce_sum(out=pos12[:, 1:2], in_=scr8[:],
                                 axis=mybir.AxisListType.X)
                    v.tensor_scalar(out=keep12[:], in0=pos12[:], scalar1=float(CAP),
                                    scalar2=None, op0=OP.is_lt)
                    v.tensor_mul(out=c12_all[:, t * 2:t * 2 + 2],
                                 in0=p12_b[:, tt * 2:tt * 2 + 2], in1=keep12[:])
                    v.tensor_scalar(out=tokf[:], in0=iotap[:], scalar1=1.0,
                                    scalar2=float(t * 128), op0=OP.mult, op1=OP.add)
                    v.tensor_copy(out=payload[:, tt * 2:tt * 2 + 1], in_=tokf[:])
                    v.tensor_copy(out=payload[:, tt * 2 + 1:tt * 2 + 2], in_=tokf[:])
                    for k in range(2):
                        ek = e12_b[:, tt * 2 + k:tt * 2 + k + 1]
                        pk = pos12[:, k:k + 1]
                        kk = keep12[:, k:k + 1]
                        v.tensor_scalar(out=slotf[:], in0=ek,
                                        scalar1=float(nb * CAP),
                                        scalar2=float(b * CAP - ZROW),
                                        op0=OP.mult, op1=OP.add)
                        v.tensor_add(out=slotf[:], in0=slotf[:], in1=pk)
                        v.tensor_mul(out=slotf[:], in0=slotf[:], in1=kk)
                        v.tensor_scalar(out=slotf[:], in0=slotf[:],
                                        scalar1=float(ZROW), scalar2=None,
                                        op0=OP.add)
                        v.tensor_copy(out=yidx_all[:, t * 2 + k:t * 2 + k + 1],
                                      in_=slotf[:])
                        v.tensor_scalar(out=slotf[:], in0=ek,
                                        scalar1=float(nb * TSTRIDE),
                                        scalar2=float(b * TSTRIDE - 1048576),
                                        op0=OP.mult, op1=OP.add)
                        v.tensor_add(out=slotf[:], in0=slotf[:], in1=pk)
                        v.tensor_mul(out=slotf[:], in0=slotf[:], in1=kk)
                        v.tensor_scalar(out=slotf[:], in0=slotf[:],
                                        scalar1=1048576.0, scalar2=None, op0=OP.add)
                        v.tensor_copy(out=idxa_u[:, tt * 2 + k:tt * 2 + k + 1],
                                      in_=slotf[:])
                    v.sem_inc(sem_v.h, 1)._wait_ge(ch_v.h, ch_v.n)
                vec(v_chain)
                sem_v.inc(1)
                vch[tt] = sem_v.n

            def g_scat(g, b=b):
                g.wait_ge(sem_v.h, vch[TPB - 1])
                with g.register("bnd%d" % b) as breg:
                    g.reg_mov(breg, TBL_ROWS - 1)
                    bval = g.snap(breg)
                    for tt in range(TPB):
                        for k in range(2):
                            g.indirect_dma_start(
                                out=tbl_d[:, :],
                                out_offset=IndirectOffsetOnAxis(
                                    idxa_u[:, tt * 2 + k:tt * 2 + k + 1], 0),
                                in_=payload[:, tt * 2:tt * 2 + 2],
                                in_offset=None,
                                bounds_check=bval, oob_is_err=False
                            ).then_inc(sem_gsc.h, 16)
            block.gpsimd(g_scat)
            sem_gsc.inc(32 * TPB)
            endb_v_done[b] = vch[TPB - 1]
            scat_g_done[b] = sem_gsc.n

        for gg in range(NGRP):
            for tig in range(GRP):
                emit_tile(gg * GRP + tig)
            if gg >= 1:
                emit_attn_group(gg - 1)
            if (gg + 1) % GPB == 0:
                emit_endb(gg // GPB)
        emit_attn_group(NGRP - 1)

        _dma_drain()
        _phase_barrier()
        es_p1.close()

        es_p2 = ExitStack()
        w1c = sb(es_p2, "w1c", [128, 2 * H], BF16)
        w2c = sb(es_p2, "w2c", [128, 2 * H], BF16)
        bufT = sb(es_p2, "bufT", [128, 2 * E * NCC * TSTRIDE], BF16)
        tokg = sb(es_p2, "tokg", [128, 2 * C], BF16)
        idxg = sb(es_p2, "idxg", [128, 2 * E * 3], U32)
        hbf = sb(es_p2, "hbf", [128, 2 * CAP], BF16)
        y_stage = sb(es_p2, "y_stage", [128, 3 * 3 * C], BF16)
        y0t = sb(es_p2, "y0t", [128, 2 * C], BF16)
        y1t = sb(es_p2, "y1t", [128, 2 * C], BF16)
        sin_t = sb(es_p2, "sin_t", [128, 2 * C], BF16)
        xr_t = sb(es_p2, "xr_t", [128, 2 * C], F32)
        accf = sb(es_p2, "accf", [128, C], F32)
        gsil = accf
        outst = sb(es_p2, "outst", [128, 2 * C], F32)

        b2_bf = sb(es_p2, "b2_bf", [1, E * C], BF16)
        psH = [ps(es_p2, f"psH{i}", [128, CAP], F32) for i in range(2)]
        psY = [ps(es_p2, "psY0", [128, C], F32), ps(es_p2, "psY1", [128, C], F32),
               ps(es_p2, "psY2", [64, C], F32)]
        NSZ = (128, 128, 64)
        NOF = (0, 128, 256)

        def g_b2ld(g):
            g.dma_start(out=b2_bf[0:1, :], in_=b2_d[:, :],
                        max_dma_last_dim=8192).then_inc(sem_b2.h, 16)
        block.gpsimd(g_b2ld)
        sem_b2.inc(16)
        b2_ready = sem_b2.n

        buf_ready = {}
        te_be = {}
        gctr = [0]

        def emit_gather_e(b, e, prev_marker):
            bp = b % 2

            def g_gath(g):
                g.wait_ge(sem_gsc.h, scat_g_done[b])
                g.wait_ge(sem_gxn.h, sem_gxn.n)
                if prev_marker is not None:
                    g.wait_ge(sem_t.h, prev_marker)
                r0 = (e * nb + b) * TSTRIDE
                g.dma_start(out=idxg[:, (bp * E + e) * 3:(bp * E + e) * 3 + 3],
                            in_=tbl_d[r0:r0 + TSTRIDE, 0:1].rearrange(
                                "(a p) c -> p (a c)", p=128),
                            allow_slow_non_contiguous=True
                            ).then_inc(sem_gix.h, 16)
                g.wait_ge(sem_gix.h, sem_gix.n + 16)
                for a in range(3):
                    j = gctr[0] + a
                    if j >= 2:
                        g.wait_ge(sem_tr.h, tr_base_p2[0] + 16 * (j - 1))
                    g.indirect_dma_start(
                        out=tokg[:, (j % 2) * C:(j % 2) * C + C],
                        out_offset=None, in_=xn_d[:, :],
                        in_offset=IndirectOffsetOnAxis(
                            idxg[:, (bp * E + e) * 3 + a:
                                 (bp * E + e) * 3 + a + 1], 0)
                    ).then_inc(sem_gg[j % 2].h, 16)
            block.gpsimd(g_gath)
            sem_gix.inc(16)
            gg0 = [sem_gg[0].n, sem_gg[1].n]
            for a in range(3):
                sem_gg[(gctr[0] + a) % 2].inc(16)

            def s_btr(sy):
                done = {0: gg0[0], 1: gg0[1]}
                for a in range(3):
                    j = gctr[0] + a
                    par = j % 2
                    done[par] += 16
                    sy.wait_ge(sem_gg[par].h, done[par])
                    sy.wait_ge(sem_tr.h, tr_base_p2[0] + 16 * j)
                    base = (bp * E + e) * NCC * TSTRIDE
                    outap = bufT[:, base:base + NCC * TSTRIDE].rearrange(
                        "p (cc n) -> p cc n", cc=NCC)[:, :, a * 128:(a + 1) * 128]
                    sy.dma_start(out=outap,
                                 in_=tokg[:, (j % 2) * C:(j % 2) * C + C],
                                 transpose=True).then_inc(sem_tr.h, 16)
            block.sync(s_btr)
            sem_tr.inc(16 * 3)
            gctr[0] += 3
            buf_ready[(b, e)] = sem_tr.n

        def emit_p3_tile(t):
            b = t // TPB
            tt = t % TPB
            tp = tt % 2

            def g_yg(g, t=t, tp=tp, b=b):
                g.wait_ge(sem_ydn.h, b + 1)
                g.wait_ge(sem_v.h, sem_v.n)
                g.indirect_dma_start(
                    out=y0t[:, tp * C:(tp + 1) * C], out_offset=None,
                    in_=y_d[:, :],
                    in_offset=IndirectOffsetOnAxis(
                        yidx_all[:, t * 2:t * 2 + 1], 0)
                ).then_inc(sem_gyg[tp].h, 16)
                g.indirect_dma_start(
                    out=y1t[:, tp * C:(tp + 1) * C], out_offset=None,
                    in_=y_d[:, :],
                    in_offset=IndirectOffsetOnAxis(
                        yidx_all[:, t * 2 + 1:t * 2 + 2], 0)
                ).then_inc(sem_gyg[tp].h, 16)
            block.gpsimd(g_yg)
            sem_gyg[tp].inc(32)
            ygb = sem_gyg[tp].n

            def s_p3ld(sy, t=t, tp=tp):
                if sem_op[tp].n:
                    sy.wait_ge(sem_op[tp].h, sem_op[tp].n)
                sy.sem_inc(sem_ofree.h, 1)
                sy.dma_start(out=sin_t[:, tp * C:(tp + 1) * C],
                             in_=s_d[t * 128:(t + 1) * 128, :]
                             ).then_inc(sem_p3p[tp].h, 16)
                sy.dma_start(out=xr_t[:, tp * C:(tp + 1) * C],
                             in_=x_d[t * 128:(t + 1) * 128, :]
                             ).then_inc(sem_p3p[tp].h, 16)
            block.sync(s_p3ld)
            sem_p3p[tp].inc(32)
            sem_ofree.inc(1)
            p3l = sem_p3p[tp].n
            ofr = sem_ofree.n

            def v_p3(v, t=t, tp=tp, ygb=ygb, p3l=p3l):
                v.wait_ge(sem_gyg[tp].h, ygb)
                v.wait_ge(sem_p3p[tp].h, p3l)
                v.tensor_scalar(out=accf[:], in0=y0t[:, tp * C:(tp + 1) * C],
                                scalar1=c12_all[:, t * 2:t * 2 + 1],
                                scalar2=None, op0=OP.mult)
                v.scalar_tensor_tensor(out=accf[:],
                                       in0=y1t[:, tp * C:(tp + 1) * C],
                                       scalar=c12_all[:, t * 2 + 1:t * 2 + 2],
                                       in1=accf[:], op0=OP.mult, op1=OP.add
                                       ).then_inc(sem_v.h, 1)
            vec(v_p3)
            sem_v.inc(1)

            def s_p3(x):
                x.wait_ge(sem_v.h, sem_v.n)
                x.activation(out=gsil[:], in_=accf[:], func=AF.Silu,
                             bias=0.0, scale=1.0).then_inc(sem_s.h, 1)
            sca(s_p3)
            sem_s.inc(1)

            def v_p3b(v, t=t, tp=tp, ofr=ofr):
                v.wait_ge(sem_s.h, sem_s.n)
                v.wait_ge(sem_ofree.h, ofr)
                v.tensor_mul(out=gsil[:], in0=gsil[:],
                             in1=sin_t[:, tp * C:(tp + 1) * C])
                v.tensor_add(out=outst[:, tp * C:(tp + 1) * C],
                             in0=xr_t[:, tp * C:(tp + 1) * C], in1=gsil[:]
                             ).then_inc(sem_v.h, 1)
            vec(v_p3b)
            sem_v.inc(1)
            vp3 = sem_v.n

            def s_owr(sy, t=t, tp=tp, vp3=vp3):
                sy.wait_ge(sem_v.h, vp3)
                sy.dma_start(out=out_d[t * 128:(t + 1) * 128, :],
                             in_=outst[:, tp * C:(tp + 1) * C]
                             ).then_inc(sem_op[tp].h, 16)
            block.sync(s_owr)
            sem_op[tp].inc(16)

        prev_yv = [0]
        t0_prev_exp = [None]
        tr_base_p2 = [sem_tr.n]

        for e in range(E):
            emit_gather_e(0, e, None)

        for b in range(nb):
            bp = b % 2
            for e in range(E):
                eg = b * E + e
                t0 = sem_t.n
                s0 = sem_s.n
                w0 = [sem_wq[0].n, sem_wq[1].n]
                base_buf = (bp * E + e) * NCC * TSTRIDE
                t0p = t0_prev_exp[0]

                def g_wload(g, e=e, t0=t0, t0p=t0p):
                    for hq in range(8):
                        q = hq % 2
                        if hq >= 2:
                            g.wait_ge(sem_t.h, t0 + 8 * hq - 6)
                        elif t0p is not None:
                            g.wait_ge(sem_t.h, t0p + 8 * hq + 58)
                        g.dma_start(
                            out=w1c[:, q * H:(q + 1) * H],
                            in_=w1t_d[(e * 8 + hq) * 128:(e * 8 + hq + 1) * 128, :]
                            ).then_inc(sem_wq[q].h, 16)
                        g.dma_start(
                            out=w2c[:, q * H:(q + 1) * H].rearrange(
                                "p (hcq c) -> p hcq c", hcq=4),
                            in_=w2t_d[e * H + hq * 512:e * H + (hq + 1) * 512, :]
                            .rearrange("(hcq p) c -> p hcq c", p=128)
                            ).then_inc(sem_wq[q].h, 16)
                block.gpsimd(g_wload)
                sem_wq[0].inc(32 * 4)
                sem_wq[1].inc(32 * 4)

                vprev = prev_yv[0]

                def t_exp(t_, b=b, e=e, t0=t0, s0=s0, w0=w0,
                          base_buf=base_buf, vprev=vprev):
                    t_.wait_ge(sem_tr.h, buf_ready[(b, e)])
                    t_.wait_ge(sem_b2.h, b2_ready)
                    if vprev:
                        t_.wait_ge(sem_v.h, vprev)
                    last = None
                    for ncx in range(3):
                        for half in range(2):
                            last = t_.matmul(
                                psY[ncx][:, half * 512:(half + 1) * 512],
                                ones_row_bf[0:1, 0:NSZ[ncx]],
                                b2_bf[0:1, e * C + half * 512:
                                      e * C + (half + 1) * 512],
                                start=True, stop=False)
                    last.then_inc(sem_t.h, 1)

                    def l1(hc):
                        hq = hc // 4
                        q = hq % 2
                        t_.wait_ge(sem_wq[q].h, w0[q] + 32 * (hq // 2 + 1))
                        if hc >= 2:
                            t_.wait_ge(sem_s.h, s0 + hc - 1)
                        last = None
                        for cc in range(NCC):
                            last = t_.matmul(
                                psH[hc % 2][:],
                                w1c[:, q * H + cc * 512 + (hc % 4) * 128:
                                     q * H + cc * 512 + (hc % 4) * 128 + 128],
                                bufT[:, base_buf + cc * TSTRIDE:
                                     base_buf + cc * TSTRIDE + CAP],
                                start=(cc == 0), stop=(cc == NCC - 1))
                        last.then_inc(sem_t.h, 1)

                    def l2(hc):
                        hq = hc // 4
                        q = hq % 2
                        t_.wait_ge(sem_s.h, s0 + hc + 1)
                        last = None
                        for ncx in range(3):
                            for half in range(2):
                                last = t_.matmul(
                                    psY[ncx][:, half * 512:(half + 1) * 512],
                                    hbf[:, (hc % 2) * CAP + NOF[ncx]:
                                        (hc % 2) * CAP + NOF[ncx] + NSZ[ncx]],
                                    w2c[:, q * H + (hc % 4) * 1024 + half * 512:
                                        q * H + (hc % 4) * 1024 + (half + 1) * 512],
                                    start=False, stop=(hc == NHC - 1))
                        last.then_inc(sem_t.h, 1)

                    l1(0)
                    for hc in range(NHC):
                        if hc + 1 < NHC:
                            l1(hc + 1)
                        l2(hc)
                block.tensor(t_exp)
                sem_t.inc(1 + 2 * NHC)
                te_end = sem_t.n
                te_be[(b, e)] = te_end
                t0_prev_exp[0] = t0

                def s_relu(x, e=e, t0=t0):
                    for hc in range(NHC):
                        if hc == 0:
                            x.wait_ge(sem_t.h, t0 + 2)
                        else:
                            x.wait_ge(sem_t.h, t0 + 2 * hc + 2)
                        if hc >= 2:
                            x.wait_ge(sem_t.h, t0 + 2 * (hc - 2) + 4)
                        x.activation(out=hbf[:, (hc % 2) * CAP:(hc % 2) * CAP + CAP],
                                     in_=psH[hc % 2][:], func=AF.Relu,
                                     bias=b1_sb[:, e * NHC + hc:e * NHC + hc + 1],
                                     scale=1.0).then_inc(sem_s.h, 1)
                sca(s_relu)
                sem_s.inc(NHC)

                def v_yevac(v, e=e, eg=eg, te=te_end):
                    v.wait_ge(sem_t.h, te)
                    if eg >= 2:
                        v.wait_ge(sem_yfree.h, eg)
                    ins = None
                    for ncx in range(3):
                        ins = v.tensor_copy(out=y_stage[0:NSZ[ncx],
                                                  (eg % 3) * 3 * C + ncx * C:
                                                  (eg % 3) * 3 * C + (ncx + 1) * C],
                                      in_=psY[ncx][0:NSZ[ncx], :])
                    ins.then_inc(sem_v.h, 1)
                vec(v_yevac)
                sem_v.inc(1)
                prev_yv[0] = sem_v.n
                yv = sem_v.n

                def s_ywr(sy, b=b, e=e, eg=eg, yv=yv):
                    if sem_yp[eg % 2].n:
                        sy.wait_ge(sem_yp[eg % 2].h, sem_yp[eg % 2].n)
                    sy.sem_inc(sem_yfree.h, 1)
                    sy.wait_ge(sem_v.h, yv)
                    r0 = (e * nb + b) * CAP
                    sy.dma_start(out=y_d[r0:r0 + 256, :].rearrange(
                        "(k p) c -> p k c", p=128),
                        in_=y_stage[:, (eg % 3) * 3 * C:(eg % 3) * 3 * C + 2 * C]
                        .rearrange("p (k c) -> p k c", k=2)
                        ).then_inc(sem_yp[eg % 2].h, 16)
                    sy.dma_start(out=y_d[r0 + 256:r0 + CAP, :],
                                 in_=y_stage[0:64, (eg % 3) * 3 * C + 2 * C:
                                             (eg % 3) * 3 * C + 3 * C]
                                 ).then_inc(sem_yp[eg % 2].h, 16)
                block.sync(s_ywr)
                sem_yp[eg % 2].inc(32)
                sem_yfree.inc(1)

                if b + 1 < nb:
                    emit_gather_e(b + 1, e, te_be.get((b - 1, e)))
                if b >= 1:
                    emit_p3_tile((b - 1) * TPB + 2 * e)
                    emit_p3_tile((b - 1) * TPB + 2 * e + 1)

            def s_ycert(sy):
                sy.wait_ge(sem_yp[0].h, sem_yp[0].n)
                sy.wait_ge(sem_yp[1].h, sem_yp[1].n)
                sy.sem_inc(sem_ydn.h, 1)
            block.sync(s_ycert)
            sem_ydn.inc(1)

        for tt in range(TPB):
            emit_p3_tile((nb - 1) * TPB + tt)

        def s_final(sy):
            sy.wait_ge(sem_op[0].h, sem_op[0].n)
            sy.wait_ge(sem_op[1].h, sem_op[1].n)
        block.sync(s_final)

        es_p2.close()
    es.close()
    return nc


_NC_CACHE = {}


def kernel(L0_attn_residual, L0_attn_w_fc, L0_attn_w_proj, L0_attn_ln_g,
           L0_attn_ln_b, L0_ffn_Wr, L0_ffn_W1, L0_ffn_b1, L0_ffn_W2, L0_ffn_b2):
    nb = 4
    n_cores = 8
    if "nc" not in _NC_CACHE:
        _NC_CACHE["nc"] = build_program(nb=nb)
    nc = _NC_CACHE["nc"]

    x = np.ascontiguousarray(np.asarray(L0_attn_residual, dtype=np.float32))
    B = x.shape[0]
    wfc = np.ascontiguousarray(np.asarray(L0_attn_w_fc, dtype=np.float32))
    wproj = np.ascontiguousarray(np.asarray(L0_attn_w_proj, dtype=np.float32))
    lng = np.ascontiguousarray(np.asarray(L0_attn_ln_g, dtype=np.float32)
                               .reshape(NCC, 128))
    lnb = np.ascontiguousarray(np.asarray(L0_attn_ln_b, dtype=np.float32)
                               .reshape(NCC, 128))
    wr = np.ascontiguousarray(np.asarray(L0_ffn_Wr, dtype=np.float32))
    w1 = np.ascontiguousarray(np.asarray(L0_ffn_W1, dtype=np.float32)
                              .reshape(E * C, H))
    b1 = np.ascontiguousarray(np.asarray(L0_ffn_b1, dtype=np.float32)
                              .reshape(E * H // 128, 128))
    w2 = np.ascontiguousarray(np.asarray(L0_ffn_W2, dtype=np.float32)
                              .reshape(E * H, C))
    b2 = np.ascontiguousarray(np.asarray(L0_ffn_b2, dtype=np.float32))

    in_maps = []
    for c in range(n_cores):
        xs = np.ascontiguousarray(
            x[c * nb:(c + 1) * nb].reshape(nb * T_SEQ, C))
        in_maps.append({"x": xs, "wfc": wfc, "wproj": wproj, "lng": lng,
                        "lnb": lnb, "wr": wr, "w1": w1, "b1": b1, "w2": w2,
                        "b2": b2})

    res = run_bass_kernel_spmd(nc, in_maps, core_ids=list(range(n_cores)))
    outs = [res.results[c]["out"].reshape(nb, T_SEQ, C) for c in range(n_cores)]
    return np.ascontiguousarray(np.concatenate(outs, axis=0))


# version 08
# speedup vs baseline: 1.2967x; 1.1050x over previous
import numpy as np
from contextlib import ExitStack

import concourse.bass as bass
import concourse.mybir as mybir
from concourse.bass_core_rust import IndirectOffsetOnAxis
from concourse.bass_utils import run_bass_kernel_spmd

F32 = mybir.dt.float32
BF16 = mybir.dt.bfloat16
U32 = mybir.dt.uint32
I16 = mybir.dt.int16
AF = mybir.ActivationFunctionType
OP = mybir.AluOpType

T_SEQ = 2048
C = 1024
E = 8
CAP = 320
H = 4096
QH = 4
NCC = C // 128
NSI = 32
GRP = 4
NHC = H // 128
EPS = 1e-5
TSTRIDE = 384
XN_RING = 9


class Sem:
    def __init__(self, es, nc, name):
        self.h = es.enter_context(nc.semaphore(name))
        self.n = 0

    def inc(self, k):
        self.n += k
        return self.n


_CHAIN_SKIP = {"wait_ge", "dma_start", "indirect_dma_start", "dma_start_transpose",
               "register", "reg_mov", "snap", "sem_inc", "wait_op", "nop"}


class _InstProxy:

    def __init__(self, eng, ch):
        self._eng = eng
        self._ch = ch

    def then_inc(self, sem, k):
        m = self._eng.sem_inc(sem, k)
        m._wait_ge(self._ch.h, self._ch.n)
        return self


class ChainEng:

    def __init__(self, eng, ch):
        object.__setattr__(self, "_e", eng)
        object.__setattr__(self, "_c", ch)

    def __getattr__(self, name):
        attr = getattr(self._e, name)
        if (not callable(attr)) or name in _CHAIN_SKIP or name.startswith("_"):
            return attr
        ch = self._c
        eng = self._e

        def wrap(*a, **k):
            inst = attr(*a, **k)
            if inst is not None and hasattr(inst, "then_inc"):
                if ch.n > 0:
                    inst._wait_ge(ch.h, ch.n)
                inst.then_inc(ch.h, 1)
                ch.n += 1
                return _InstProxy(eng, ch)
            return inst
        return wrap


def build_program(nb=4):
    NTOK = nb * T_SEQ
    NTIL = NTOK // 128
    TPB = T_SEQ // 128
    NGRP = NTIL // GRP
    GPB = TPB // GRP
    NSLOT = E * nb * CAP
    ZROW = NSLOT
    TBL_ROWS = ((E * nb * TSTRIDE + 127) // 128) * 128

    nc = bass.Bass("TRN2", target_bir_lowering=False)

    x_d = nc.dram_tensor("x", [NTOK, C], F32, kind="ExternalInput").ap()
    wfc_d = nc.dram_tensor("wfc", [C, C], F32, kind="ExternalInput").ap()
    wproj_d = nc.dram_tensor("wproj", [C, C], F32, kind="ExternalInput").ap()
    lng_d = nc.dram_tensor("lng", [NCC, 128], F32, kind="ExternalInput").ap()
    lnb_d = nc.dram_tensor("lnb", [NCC, 128], F32, kind="ExternalInput").ap()
    wr_d = nc.dram_tensor("wr", [C, E], F32, kind="ExternalInput").ap()
    w1_d = nc.dram_tensor("w1", [E * C, H], F32, kind="ExternalInput").ap()
    b1_d = nc.dram_tensor("b1", [E * H // 128, 128], F32, kind="ExternalInput").ap()
    w2_d = nc.dram_tensor("w2", [E * H, C], F32, kind="ExternalInput").ap()
    b2_d = nc.dram_tensor("b2", [E, C], F32, kind="ExternalInput").ap()
    out_d = nc.dram_tensor("out", [NTOK, C], F32, kind="ExternalOutput").ap()

    xn_d = nc.dram_tensor("xn_scr", [NTOK, C], BF16, kind="Internal").ap()
    s_d = nc.dram_tensor("s_scr", [NTOK, C], BF16, kind="Internal").ap()
    y_d = nc.dram_tensor("y_scr", [NSLOT + 1, C], BF16, kind="Internal").ap()
    tbl_d = nc.dram_tensor("tbl_scr", [TBL_ROWS, 2], U32, kind="Internal").ap()

    es = ExitStack()
    es_p1 = ExitStack()

    def sb(stack, name, shape, dtype=F32):
        return stack.enter_context(nc.sbuf_tensor(name, shape, dtype))

    def ps(stack, name, shape, dtype=F32):
        return stack.enter_context(nc.psum_tensor(name, shape, dtype))

    ident_f = sb(es, "ident_f", [128, 128], F32)
    iota8 = sb(es, "iota8", [128, 8], F32)
    iotap = sb(es, "iotap", [128, 1], F32)
    iotad = sb(es, "iotad", [128, 128], I16)
    eps_ap = sb(es, "eps_ap", [128, 1], F32)
    ones_row_bf = sb(es, "ones_row_bf", [1, 512], BF16)
    ones_col_bf = sb(es, "ones_col_bf", [128, 1], BF16)
    g_sb = sb(es, "g_sb", [128, NCC], F32)
    battn_bf = sb(es, "battn_bf", [128, NCC], BF16)
    wr_hi = sb(es, "wr_hi", [128, NCC * E], BF16)
    wr_lo = sb(es, "wr_lo", [128, NCC * E], BF16)
    wrsum_bc = sb(es, "wrsum_bc", [128, E], F32)
    bk_sb = sb(es, "bk_sb", [128, NSI * QH], F32)
    b1_sb = sb(es, "b1_sb", [128, E * NHC], F32)
    c12_all = sb(es, "c12_all", [128, 2 * NTIL], F32)
    yidx_all = sb(es, "yidx_all", [128, 2 * NTIL], U32)
    zero_bf = sb(es, "zero_bf", [128, C], BF16)

    sem_xp = [Sem(es, nc, "sx0"), Sem(es, nc, "sx1")]
    sem_tr = Sem(es, nc, "s_tr")
    sem_gxn = Sem(es, nc, "s_gxn")
    sem_gxnc = Sem(es, nc, "s_gxnc")
    sem_gs = [Sem(es, nc, "s_gs0"), Sem(es, nc, "s_gs1")]
    sem_gsc = Sem(es, nc, "s_gsc")
    sem_ld = Sem(es, nc, "s_ld")
    sem_yp = [Sem(es, nc, "s_y0"), Sem(es, nc, "s_y1")]
    sem_yfree = Sem(es, nc, "s_yfree")
    sem_ydn = Sem(es, nc, "s_ydn")
    sem_gix = Sem(es, nc, "s_gix")
    sem_b2 = Sem(es, nc, "s_b2")
    sem_wp1 = [Sem(es, nc, "s_wp1a"), Sem(es, nc, "s_wp1b")]
    sem_ws1 = [Sem(es, nc, "s_ws1a"), Sem(es, nc, "s_ws1b")]
    sem_wp2 = [Sem(es, nc, "s_wp2a"), Sem(es, nc, "s_wp2b")]
    sem_ws2 = [Sem(es, nc, "s_ws2a"), Sem(es, nc, "s_ws2b")]
    sem_gg = [Sem(es, nc, "s_gg0"), Sem(es, nc, "s_gg1")]
    sem_gyg = [Sem(es, nc, "s_gyg0"), Sem(es, nc, "s_gyg1")]
    sem_p3p = [Sem(es, nc, "s_p30"), Sem(es, nc, "s_p31")]
    sem_op = [Sem(es, nc, "s_op0"), Sem(es, nc, "s_op1")]
    sem_ofree = Sem(es, nc, "s_ofree")
    sem_out = Sem(es, nc, "s_out")
    sem_t = Sem(es, nc, "s_t")
    sem_v = Sem(es, nc, "s_v")
    sem_s = Sem(es, nc, "s_s")
    sem_gc = Sem(es, nc, "s_gc")
    sem_ph = Sem(es, nc, "s_ph")
    sem_wq = [Sem(es, nc, "s_wq%d" % i) for i in range(4)]
    ch_v = Sem(es, nc, "chv")
    ch_s = Sem(es, nc, "chs")

    with nc.Block() as block:
        def vec(f):
            block.vector(lambda v: f(ChainEng(v, ch_v)))

        def sca(f):
            block.scalar(lambda x: f(ChainEng(x, ch_s)))

        def _phase_barrier():
            tgt = sem_ph.n + 5

            def _arr_t(t_):
                t_.sem_inc(sem_ph.h, 1)
                t_.wait_ge(sem_ph.h, tgt)
            block.tensor(_arr_t)

            def _arr_s(sy):
                sy.sem_inc(sem_ph.h, 1)
                sy.wait_ge(sem_ph.h, tgt)
            block.sync(_arr_s)

            def _arr_g(g):
                g.sem_inc(sem_ph.h, 1)
                g.wait_ge(sem_ph.h, tgt)
            block.gpsimd(_arr_g)

            def _arr_v(v):
                v.wait_ge(ch_v.h, ch_v.n)
                v.sem_inc(sem_ph.h, 1)
                v.wait_ge(sem_ph.h, tgt)
            block.vector(_arr_v)

            def _arr_sc(x):
                x.wait_ge(ch_s.h, ch_s.n)
                x.sem_inc(sem_ph.h, 1)
                x.wait_ge(sem_ph.h, tgt)
            block.scalar(_arr_sc)
            sem_ph.inc(5)

        def _dma_drain():
            def s_drain(sy):
                for sm_ in (sem_tr, sem_ld, sem_xp[0], sem_xp[1], sem_yp[0],
                            sem_yp[1], sem_out, sem_op[0], sem_op[1],
                            sem_p3p[0], sem_p3p[1]):
                    if sm_.n:
                        sy.wait_ge(sm_.h, sm_.n)
            block.sync(s_drain)

            def g_drain(g):
                for sm_ in (sem_gxn, sem_gs[0], sem_gs[1], sem_gsc, sem_gix,
                            sem_gg[0], sem_gg[1], sem_gyg[0], sem_gyg[1],
                            sem_wq[0], sem_wq[1], sem_wq[2], sem_wq[3], sem_b2,
                            sem_wp1[0], sem_wp1[1], sem_ws1[0], sem_ws1[1],
                            sem_wp2[0], sem_wp2[1], sem_ws2[0], sem_ws2[1]):
                    if sm_.n:
                        g.wait_ge(sm_.h, sm_.n)
            block.gpsimd(g_drain)
        st_sb = ExitStack()
        wfcT = sb(es_p1, "wfcT", [128, NCC * C], BF16)
        wprojT = sb(es_p1, "wprojT", [128, NCC * C], BF16)
        wstage = sb(st_sb, "wstage", [128, C], F32)
        wstage_bf = sb(st_sb, "wstage_bf", [128, 2 * C], BF16)
        small_f = sb(st_sb, "small_f", [128, 128], F32)
        small_f2 = sb(st_sb, "small_f2", [128, 128], F32)
        wsum8 = sb(st_sb, "wsum8", [8, 1], F32)
        wsrow_bf = sb(st_sb, "wsrow_bf", [1, 8], BF16)
        ztbl = sb(st_sb, "ztbl", [128, TBL_ROWS * 2 // 128], U32)
        ps_set = ps(st_sb, "ps_set", [128, 512], F32)
        ps_set2 = ps(st_sb, "ps_set2", [128, 8], F32)

        def g_setup(g):
            g.iota(iotad[:], pattern=[[1, 128]], base=0, channel_multiplier=-1)
            g.iota(iota8[:], pattern=[[1, 8]], base=0, channel_multiplier=0,
                   allow_small_or_imprecise_dtypes=True)
            g.iota(iotap[:], pattern=[[0, 1]], base=0, channel_multiplier=1,
                   allow_small_or_imprecise_dtypes=True)
            g.memset(ztbl[:], 0).then_inc(sem_gc.h, 1)
        block.gpsimd(g_setup)
        sem_gc.inc(1)

        def v_setup0(v):
            v.wait_ge(sem_gc.h, sem_gc.n)
            v.memset(eps_ap[:], EPS)
            v.memset(ones_row_bf[:], 1.0)
            v.memset(ones_col_bf[:], 1.0)
            v.memset(zero_bf[:], 0.0)
            v.tensor_scalar(out=ident_f[:], in0=iotad[:], scalar1=0.0, scalar2=None,
                            op0=OP.is_equal).then_inc(sem_v.h, 1)
        vec(v_setup0)
        sem_v.inc(1)

        def g_zero(g):
            g.wait_ge(sem_v.h, sem_v.n)
            g.dma_start(out=tbl_d.rearrange("(p k) c -> p (k c)", p=128),
                        in_=ztbl[:]).then_inc(sem_gxn.h, 16)
            g.dma_start(out=y_d[ZROW:ZROW + 1, :],
                        in_=zero_bf[0:1, :]).then_inc(sem_gxn.h, 16)
        block.gpsimd(g_zero)
        sem_gxn.inc(32)

        def s_loads(sy):
            sy.dma_start(out=small_f[0:NCC, :], in_=lng_d[:, :]).then_inc(sem_ld.h, 16)
            sy.dma_start(out=small_f2[0:NCC, :], in_=lnb_d[:, :]).then_inc(sem_ld.h, 16)
            sy.dma_start(out=wstage[:, 0:NCC * E].rearrange("p (cc e) -> p cc e", e=E),
                         in_=wr_d[:, :].rearrange("(cc p) e -> p cc e", p=128)
                         ).then_inc(sem_ld.h, 16)
        block.sync(s_loads)
        sem_ld.inc(48)

        def t_gb(t):
            t.wait_ge(sem_v.h, sem_v.n)
            t.wait_ge(sem_ld.h, sem_ld.n)
            t.transpose(ps_set[:, 0:8], small_f[0:NCC, :], ident_f[0:NCC, 0:NCC])
            t.transpose(ps_set[:, 8:16], small_f2[0:NCC, :],
                        ident_f[0:NCC, 0:NCC]).then_inc(sem_t.h, 1)
        block.tensor(t_gb)
        sem_t.inc(1)

        def v_gb(v):
            v.wait_ge(sem_t.h, sem_t.n)
            v.tensor_copy(out=g_sb[:], in_=ps_set[:, 0:8])
            v.tensor_copy(out=battn_bf[:], in_=ps_set[:, 8:16])
            v.tensor_copy(out=wr_hi[:], in_=wstage[:, 0:NCC * E])
            v.tensor_sub(out=wr_lo[:], in0=wstage[:, 0:NCC * E], in1=wr_hi[:]
                         ).then_inc(sem_v.h, 1)
        vec(v_gb)
        sem_v.inc(1)

        def s_b1ld(sy):
            sy.wait_ge(sem_t.h, sem_t.n)
            sy.dma_start(out=small_f[:], in_=b1_d[0:128, :]).then_inc(sem_ld.h, 16)
            sy.dma_start(out=small_f2[:], in_=b1_d[128:256, :]).then_inc(sem_ld.h, 16)
        block.sync(s_b1ld)
        sem_ld.inc(32)


        def t_b1(t):
            t.wait_ge(sem_ld.h, sem_ld.n)
            t.wait_ge(sem_v.h, sem_v.n)
            t.transpose(ps_set[:, 0:128], small_f[:], ident_f[:])
            t.transpose(ps_set[:, 128:256], small_f2[:], ident_f[:]
                        ).then_inc(sem_t.h, 1)
        block.tensor(t_b1)
        sem_t.inc(1)

        def v_b1(v):
            v.wait_ge(sem_t.h, sem_t.n)
            v.tensor_copy(out=b1_sb[:], in_=ps_set[:, 0:256]).then_inc(sem_v.h, 1)
        vec(v_b1)
        sem_v.inc(1)

        ntr = 0
        for (src, dst) in ((wfc_d, wfcT), (wproj_d, wprojT)):
            for rc in range(8):
                slot = ntr % 2

                def s_wload(sy, src=src, rc=rc, vw=None):
                    sy.wait_ge(sem_v.h, sem_v.n)
                    sy.dma_start(out=wstage[:], in_=src[rc * 128:(rc + 1) * 128, :]
                                 ).then_inc(sem_ld.h, 16)
                block.sync(s_wload)
                sem_ld.inc(16)

                def v_wcast(v, slot=slot, ntr=ntr):
                    v.wait_ge(sem_ld.h, sem_ld.n)
                    if ntr >= 2:
                        v.wait_ge(sem_tr.h, 16 * (ntr - 1))
                    v.tensor_copy(out=wstage_bf[:, slot * C:(slot + 1) * C],
                                  in_=wstage[:]).then_inc(sem_v.h, 1)
                vec(v_wcast)
                sem_v.inc(1)

                def s_wtr(sy, dst=dst, rc=rc, slot=slot):
                    sy.wait_ge(sem_v.h, sem_v.n)
                    if sem_tr.n:
                        sy.wait_ge(sem_tr.h, sem_tr.n)
                    outap = dst[:, :].rearrange("p (cc r) -> p cc r", cc=NCC)[
                        :, :, rc * 128:(rc + 1) * 128]
                    sy.dma_start(out=outap, in_=wstage_bf[:, slot * C:(slot + 1) * C],
                                 transpose=True).then_inc(sem_tr.h, 16)
                block.sync(s_wtr)
                sem_tr.inc(16)
                ntr += 1

        for si in range(NSI):
            q = si // 8

            def t_bk(t, si=si, q=q):
                if si == 0:
                    t.wait_ge(sem_tr.h, sem_tr.n)
                t.wait_ge(sem_v.h, sem_v.n)
                last = None
                for h in range(QH):
                    for dc in range(2):
                        base = (2 * q + dc) * C + (si % 8) * 128
                        last = t.matmul(ps_set2[:, h:h + 1],
                                        wfcT[:, base:base + 128],
                                        battn_bf[:, 2 * h + dc:2 * h + dc + 1],
                                        start=(dc == 0), stop=(dc == 1))
                last.then_inc(sem_t.h, 1)
            block.tensor(t_bk)
            sem_t.inc(1)

            def v_bk(v, si=si):
                v.wait_ge(sem_t.h, sem_t.n)
                v.tensor_copy(out=bk_sb[:, si * 4:si * 4 + 4],
                              in_=ps_set2[:, 0:4]).then_inc(sem_v.h, 1)
            vec(v_bk)
            sem_v.inc(1)

        def t_ws(t):
            t.wait_ge(sem_v.h, sem_v.n)
            last = None
            for i, w in enumerate((wr_hi, wr_lo)):
                for cc in range(NCC):
                    last = t.matmul(ps_set2[0:8, 0:1], w[:, cc * 8:cc * 8 + 8],
                                    ones_col_bf[:, :], start=(i == 0 and cc == 0),
                                    stop=(i == 1 and cc == NCC - 1))
            last.then_inc(sem_t.h, 1)
        block.tensor(t_ws)
        sem_t.inc(1)

        def v_ws(v):
            v.wait_ge(sem_t.h, sem_t.n)
            v.tensor_copy(out=wsum8[:], in_=ps_set2[0:8, 0:1]).then_inc(sem_v.h, 1)
        vec(v_ws)
        sem_v.inc(1)

        def t_ws2(t):
            t.wait_ge(sem_v.h, sem_v.n)
            t.transpose(ps_set2[0:1, 0:8], wsum8[:], ident_f[0:8, 0:8]
                        ).then_inc(sem_t.h, 1)
        block.tensor(t_ws2)
        sem_t.inc(1)

        def v_ws2(v):
            v.wait_ge(sem_t.h, sem_t.n)
            v.tensor_copy(out=wsrow_bf[:], in_=ps_set2[0:1, 0:8]).then_inc(sem_v.h, 1)
        vec(v_ws2)
        sem_v.inc(1)

        def t_ws3(t):
            t.wait_ge(sem_v.h, sem_v.n)
            t.matmul(ps_set2[:, 0:8], ones_row_bf[0:1, 0:128], wsrow_bf[:],
                     start=True, stop=True).then_inc(sem_t.h, 1)
        block.tensor(t_ws3)
        sem_t.inc(1)

        def v_ws3(v):
            v.wait_ge(sem_t.h, sem_t.n)
            v.tensor_copy(out=wrsum_bc[:], in_=ps_set2[:, 0:8]).then_inc(sem_v.h, 1)
        vec(v_ws3)
        sem_v.inc(1)

        _dma_drain()
        _phase_barrier()
        st_sb.close()

        xb = sb(es_p1, "xb", [128, 2 * C], F32)
        xnf = sb(es_p1, "xnf", [128, 2 * C], F32)
        xnhi = sb(es_p1, "xnhi", [128, XN_RING * C], BF16)
        zf = sb(es_p1, "zf", [128, C], F32)
        zhi = sb(es_p1, "zhi", [128, 2 * C], BF16)
        zlo = sb(es_p1, "zlo", [128, 2 * C], BF16)
        zThi = sb(es_p1, "zThi", [128, 2 * NCC * 512], BF16)
        zTlo = sb(es_p1, "zTlo", [128, 2 * NCC * 512], BF16)
        zgT = sb(es_p1, "zgT", [128, NCC * 512], BF16)
        pT = sb(es_p1, "pT", [128, NSI * 512], BF16)
        s_stage = sb(es_p1, "s_stage", [128, 2 * GRP * C], BF16)
        routb = sb(es_p1, "routb", [8, 2 * T_SEQ], F32)
        hT = routb[0:8, 0:T_SEQ]
        cex = routb[0:8, T_SEQ:2 * T_SEQ]
        is1_b = sb(es_p1, "is1_b", [128, 8 * TPB], F32)
        is2_b = sb(es_p1, "is2_b", [128, 8 * TPB], F32)
        p12_b = sb(es_p1, "p12_b", [128, 2 * TPB], F32)
        e12_b = sb(es_p1, "e12_b", [128, 2 * TPB], F32)
        sd_ring = sb(es_p1, "sd_ring", [128, GRP], F32)
        mu_ring = sb(es_p1, "mu_ring", [128, GRP], F32)
        bns = sb(es_p1, "bns", [128, 12], F32)
        bn2 = sb(es_p1, "bn2", [128, 2], F32)
        bn2i = sb(es_p1, "bn2i", [128, 2], F32)
        lnv = sb(es_p1, "lnv", [128, 1], F32)
        r2 = sb(es_p1, "r2", [128, 1], F32)
        vprime = sb(es_p1, "vprime", [128, 1], F32)
        r1 = sb(es_p1, "r1", [128, 1], F32)
        nmr = sb(es_p1, "nmr", [128, 1], F32)
        lg8 = sb(es_p1, "lg8", [128, 8], F32)
        m8 = sb(es_p1, "m8", [128, 8], F32)
        i8u = sb(es_p1, "i8u", [128, 8], U32)
        e8x = sb(es_p1, "e8x", [128, 8], F32)
        zsum = sb(es_p1, "zsum", [128, 1], F32)
        rz = sb(es_p1, "rz", [128, 1], F32)
        negm1 = sb(es_p1, "negm1", [128, 1], F32)
        p12e = sb(es_p1, "p12e", [128, 2], F32)
        hmaskf = sb(es_p1, "hmaskf", [128, 8], F32)
        scr8 = sb(es_p1, "scr8", [128, 8], F32)
        pos12 = sb(es_p1, "pos12", [128, 2], F32)
        keep12 = sb(es_p1, "keep12", [128, 2], F32)
        slotf = sb(es_p1, "slotf", [128, 1], F32)
        tmp1 = sb(es_p1, "tmp1", [128, 1], F32)
        payload = sb(es_p1, "payload", [128, 2 * TPB], U32)
        idxa_u = sb(es_p1, "idxa_u", [128, 2 * TPB], U32)
        tokf = sb(es_p1, "tokf", [128, 1], F32)

        psA = [ps(es_p1, f"psA{i}", [128, 512], F32) for i in range(2)]
        psS2 = [ps(es_p1, "psS2a", [128, 256], F32), ps(es_p1, "psS2b", [128, 256], F32)]
        psR = ps(es_p1, "psR", [128, 8], F32)
        psTr = ps(es_p1, "psTr", [128, 128], F32)

        def v_p1init(v):
            v.memset(routb[:, :], 0.0)
        vec(v_p1init)

        def s_x0(sy):
            sy.dma_start(out=xb[:, 0:C], in_=x_d[0:128, :]).then_inc(sem_xp[0].h, 16)
        block.sync(s_x0)
        sem_xp[0].inc(16)

        vx_done = [0] * NTIL
        vz_done = [0] * NTIL
        tr_done = [0] * NTIL
        vh_done = [0] * NTIL
        vsm1_done = [0] * NTIL
        ht_evac = [0] * NTIL
        t_att_end = [0] * NGRP
        v_s2_done = [0] * NGRP
        g_sw_done = [0] * NGRP
        endb_v_done = [0] * nb
        scat_g_done = [0] * nb

        def emit_tile(t):
            b = t // TPB
            tt = t % TPB
            gg = t // GRP
            tig = t % GRP
            gp = gg % 2
            xslot = t % 2
            xnslot = t % XN_RING

            def v_ln1(v):
                v.wait_ge(sem_xp[t % 2].h, 16 * (t // 2 + 1))
                if t >= XN_RING:
                    v.wait_ge(sem_v.h, v_s2_done[(t - XN_RING) // GRP])
                    v.wait_ge(sem_gxnc.h, t - XN_RING + 2)
                v.bn_stats(out=bns[:, 0:6], in_=xb[:, xslot * C:xslot * C + 512])
                v.bn_stats(out=bns[:, 6:12],
                           in_=xb[:, xslot * C + 512:xslot * C + 1024])
                v.bn_aggr(out=bn2[:], in_=bns[:]).then_inc(sem_v.h, 1)
            vec(v_ln1)
            sem_v.inc(1)

            def s_rs1(x):
                x.wait_ge(sem_v.h, sem_v.n)
                x.activation(out=lnv[:], in_=bn2[:, 1:2], func=AF.Ln,
                             bias=eps_ap[:], scale=1.0)
                x.activation(out=r1[:], in_=lnv[:], func=AF.Exp, bias=0.0,
                             scale=-0.5)
                x.activation(out=r2[:], in_=lnv[:], func=AF.Exp, bias=0.0,
                             scale=-1.0).then_inc(sem_s.h, 1)
            sca(s_rs1)
            sem_s.inc(1)

            def v_ln2(v):
                v.wait_ge(sem_s.h, sem_s.n)
                v.tensor_scalar(out=nmr[:], in0=bn2[:, 0:1], scalar1=r1[:],
                                scalar2=-1.0, op0=OP.mult, op1=OP.mult)
                v.tensor_scalar(out=xnf[:, xslot * C:(xslot + 1) * C],
                                in0=xb[:, xslot * C:(xslot + 1) * C],
                                scalar1=r1[:], scalar2=nmr[:],
                                op0=OP.mult, op1=OP.add).then_inc(sem_v.h, 1)
                v.tensor_copy(out=xnhi[:, xnslot * C:(xnslot + 1) * C],
                              in_=xnf[:, xslot * C:(xslot + 1) * C])
                v.tensor_mul(out=vprime[:], in0=bn2[:, 1:2], in1=r2[:]
                             ).then_inc(sem_v.h, 1)
            vec(v_ln2)
            sem_v.inc(1)
            vx_done[t] = sem_v.n
            sem_v.inc(1)
            ln2_done = sem_v.n

            def s_rs2(x):
                x.wait_ge(sem_v.h, sem_v.n)
                x.activation(out=lnv[:], in_=vprime[:], func=AF.Ln,
                             bias=eps_ap[:], scale=1.0)
                x.activation(out=r1[:], in_=lnv[:], func=AF.Exp, bias=0.0,
                             scale=-0.5)
                x.activation(out=sd_ring[:, tig:tig + 1], in_=lnv[:], func=AF.Exp,
                             bias=0.0, scale=0.5).then_inc(sem_s.h, 1)
            sca(s_rs2)
            sem_s.inc(1)

            def v_ln3(v):
                v.wait_ge(sem_s.h, sem_s.n)
                if t >= 2:
                    v.wait_ge(sem_tr.h, tr_done[t - 2])
                v.tensor_scalar(out=zf[:], in0=xnf[:, xslot * C:(xslot + 1) * C],
                                scalar1=r1[:], scalar2=None, op0=OP.mult)
                v.tensor_copy(out=zhi[:, xslot * C:(xslot + 1) * C], in_=zf[:])
                v.tensor_sub(out=zlo[:, xslot * C:(xslot + 1) * C], in0=zf[:],
                             in1=zhi[:, xslot * C:(xslot + 1) * C]
                             ).then_inc(sem_v.h, 1)
            vec(v_ln3)
            sem_v.inc(1)
            vz_done[t] = sem_v.n

            def g_xn(g, ln2_done=None):
                g.wait_ge(sem_v.h, ln2_done)
                if sem_gxn.n:
                    g.wait_ge(sem_gxn.h, sem_gxn.n)
                g.sem_inc(sem_gxnc.h, 1)
                g.dma_start(out=xn_d[t * 128:(t + 1) * 128, :],
                            in_=xnhi[:, xnslot * C:(xnslot + 1) * C]
                            ).then_inc(sem_gxn.h, 16)
            block.gpsimd(lambda g, l=ln2_done: g_xn(g, l))
            sem_gxn.inc(16)
            sem_gxnc.inc(1)


            def s_tile(sy):
                if t + 1 < NTIL:
                    nxt = (t + 1) % 2
                    if t >= 1:
                        sy.wait_ge(sem_v.h, vx_done[t])
                    sy.dma_start(out=xb[:, nxt * C:(nxt + 1) * C],
                                 in_=x_d[(t + 1) * 128:(t + 2) * 128, :]
                                 ).then_inc(sem_xp[(t + 1) % 2].h, 16)
                sy.wait_ge(sem_v.h, vz_done[t])
                if sem_tr.n:
                    sy.wait_ge(sem_tr.h, sem_tr.n)
                if gg >= 2:
                    sy.wait_ge(sem_t.h, t_att_end[gg - 2])
                outhi = zThi[:, gp * NCC * 512:(gp + 1) * NCC * 512].rearrange(
                    "p (cc q) -> p cc q", cc=NCC)[:, :, tig * 128:(tig + 1) * 128]
                sy.dma_start(out=outhi, in_=zhi[:, xslot * C:(xslot + 1) * C],
                             transpose=True).then_inc(sem_tr.h, 16)
                sy.wait_ge(sem_tr.h, sem_tr.n + 16)
                outlo = zTlo[:, gp * NCC * 512:(gp + 1) * NCC * 512].rearrange(
                    "p (cc q) -> p cc q", cc=NCC)[:, :, tig * 128:(tig + 1) * 128]
                sy.dma_start(out=outlo, in_=zlo[:, xslot * C:(xslot + 1) * C],
                             transpose=True).then_inc(sem_tr.h, 16)
            block.sync(s_tile)
            if t + 1 < NTIL:
                sem_xp[(t + 1) % 2].inc(16)
            sem_tr.inc(32)
            tr_done[t] = sem_tr.n

            def t_router(t_):
                t_.wait_ge(sem_tr.h, tr_done[t])
                if t >= 1:
                    t_.wait_ge(sem_v.h, vsm1_done[t - 1])
                last = None
                n = 0
                for cc in range(NCC):
                    base = gp * NCC * 512 + cc * 512 + tig * 128
                    for lhs, rhs in ((zThi, wr_hi), (zThi, wr_lo), (zTlo, wr_hi)):
                        n += 1
                        last = t_.matmul(psR[:], lhs[:, base:base + 128],
                                         rhs[:, cc * 8:cc * 8 + 8],
                                         start=(n == 1), stop=(n == 3 * NCC))
                last.then_inc(sem_t.h, 1)
            block.tensor(t_router)
            sem_t.inc(1)
            rd_t = sem_t.n

            def v_sm1(v):
                v.wait_ge(sem_t.h, rd_t)
                v.tensor_scalar(out=lg8[:], in0=psR[:],
                                scalar1=sd_ring[:, tig:tig + 1], scalar2=None,
                                op0=OP.mult)
                v.max(out=m8[:], in_=lg8[:])
                v.max_index(out=i8u[:], in_max=m8[:], in_values=lg8[:])
                v.tensor_scalar(out=negm1[:], in0=m8[:, 0:1], scalar1=-1.0,
                                scalar2=None, op0=OP.mult).then_inc(sem_v.h, 1)
            vec(v_sm1)
            sem_v.inc(1)
            vsm1_done[t] = sem_v.n

            def s_exp(s):
                s.wait_ge(sem_v.h, sem_v.n)
                s.activation(out=e8x[:], in_=lg8[:], func=AF.Exp, bias=negm1[:],
                             scale=1.0, accum_out=zsum[:])
                s.activation(out=p12e[:], in_=m8[:, 0:2], func=AF.Exp, bias=negm1[:],
                             scale=1.0).then_inc(sem_s.h, 1)
            sca(s_exp)
            sem_s.inc(1)

            def v_sm2(v):
                v.wait_ge(sem_s.h, sem_s.n)
                v.reciprocal(out=rz[:], in_=zsum[:])
                v.tensor_scalar(out=p12_b[:, tt * 2:tt * 2 + 2], in0=p12e[:],
                                scalar1=rz[:], scalar2=None, op0=OP.mult)
                v.tensor_copy(out=e12_b[:, tt * 2:tt * 2 + 1], in_=i8u[:, 0:1])
                v.tensor_copy(out=e12_b[:, tt * 2 + 1:tt * 2 + 2], in_=i8u[:, 1:2])
                v.tensor_scalar(out=is1_b[:, tt * 8:tt * 8 + 8], in0=iota8[:],
                                scalar1=e12_b[:, tt * 2:tt * 2 + 1], scalar2=None,
                                op0=OP.is_equal)
                v.tensor_scalar(out=is2_b[:, tt * 8:tt * 8 + 8], in0=iota8[:],
                                scalar1=e12_b[:, tt * 2 + 1:tt * 2 + 2],
                                scalar2=None, op0=OP.is_equal)
                v.tensor_add(out=hmaskf[:], in0=is1_b[:, tt * 8:tt * 8 + 8],
                             in1=is2_b[:, tt * 8:tt * 8 + 8]).then_inc(sem_v.h, 1)
            vec(v_sm2)
            sem_v.inc(1)
            vh_done[t] = sem_v.n

            def t_hm(t_):
                t_.wait_ge(sem_v.h, vh_done[t])
                if tt:
                    t_.wait_ge(sem_v.h, ht_evac[t - 1])
                elif b:
                    t_.wait_ge(sem_v.h, endb_v_done[b - 1])
                t_.transpose(psTr[0:8, 0:128], hmaskf[:], ident_f[:]
                             ).then_inc(sem_t.h, 1)
            block.tensor(t_hm)
            sem_t.inc(1)
            hm_t = sem_t.n

            def v_hm(v):
                v.wait_ge(sem_t.h, hm_t)
                v.tensor_copy(out=routb[0:8, tt * 128:(tt + 1) * 128],
                              in_=psTr[0:8, 0:128]).then_inc(sem_v.h, 1)
            vec(v_hm)
            sem_v.inc(1)
            ht_evac[t] = sem_v.n

        def emit_attn_group(gg):
            gp = gg % 2
            sp = gg % 2

            def v_zg(v):
                v.wait_ge(sem_tr.h, tr_done[gg * GRP + GRP - 1])
                if gg >= 1:
                    v.wait_ge(sem_t.h, t_att_end[gg - 1])
                for cc in range(NCC):
                    ins = v.tensor_scalar(
                        out=zgT[:, cc * 512:(cc + 1) * 512],
                        in0=zThi[:, gp * NCC * 512 + cc * 512:
                                 gp * NCC * 512 + (cc + 1) * 512],
                        scalar1=g_sb[:, cc:cc + 1], scalar2=None, op0=OP.mult)
                ins.then_inc(sem_v.h, 1)
            vec(v_zg)
            sem_v.inc(1)
            vzg = sem_v.n

            t0 = sem_t.n
            v0 = sem_v.n
            s0 = sem_s.n

            def t_attn(t_):
                t_.wait_ge(sem_v.h, vzg)
                t_.wait_ge(sem_s.h, s0)
                if gg >= 1:
                    t_.wait_ge(sem_v.h, v_s2_done[gg - 1])
                for h in range(QH):
                    for si in range(NSI):
                        q = si // 8
                        k = h * 32 + si
                        if k >= 2:
                            t_.wait_ge(sem_s.h, s0 + k - 1)
                        last = None
                        for dc in range(2):
                            kb = (2 * q + dc) * C + (si % 8) * 128
                            last = t_.matmul(
                                psA[si % 2][:], wfcT[:, kb:kb + 128],
                                zgT[:, (2 * h + dc) * 512:(2 * h + dc + 1) * 512],
                                start=(dc == 0), stop=(dc == 1))
                        last.then_inc(sem_t.h, 1)
                    for tc in range(GRP):
                        if h >= 1 and tc < 2:
                            t_.wait_ge(sem_v.h, v0 + h * 36 + tc - 1)
                        elif tc >= 2:
                            t_.wait_ge(sem_v.h, v0 + h * 36 + 31 + tc)
                        t_.wait_ge(sem_s.h, s0 + h * 32 + 32)
                        last = None
                        for si in range(NSI):
                            q = si // 8
                            last = t_.matmul(
                                psS2[tc % 2][:],
                                pT[:, si * 512 + tc * 128:si * 512 + tc * 128 + 128],
                                wprojT[:, (si % 8) * C + q * 256:
                                       (si % 8) * C + q * 256 + 256],
                                start=(si == 0), stop=(si == NSI - 1))
                        last.then_inc(sem_t.h, 1)
            block.tensor(t_attn)
            sem_t.inc(QH * (NSI + GRP))
            t_att_end[gg] = sem_t.n

            def s_attn(x, s0=s0, v0=v0):
                for h in range(QH):
                    for si in range(NSI):
                        x.wait_ge(sem_v.h, v0 + h * 36 + si + 1)
                        x.activation(out=pT[:, si * 512:(si + 1) * 512],
                                     in_=psA[si % 2][:], func=AF.Square
                                     ).then_inc(sem_s.h, 1)
            sca(s_attn)
            sem_s.inc(QH * NSI)

            def v_attn(v):
                if gg >= 2:
                    v.wait_ge(sem_gs[gg % 2].h, g_sw_done[gg - 2])
                for h in range(QH):
                    for si in range(NSI):
                        v.wait_ge(sem_t.h, t0 + h * 36 + si + 1)
                        v.tensor_scalar(out=psA[si % 2][:], in0=psA[si % 2][:],
                                        scalar1=bk_sb[:, si * 4 + h:si * 4 + h + 1],
                                        scalar2=0.0, op0=OP.add, op1=OP.max
                                        ).then_inc(sem_v.h, 1)
                    for tc in range(GRP):
                        v.wait_ge(sem_t.h, t0 + h * 36 + 32 + tc + 1)
                        xsl = ((gg * GRP + tc) % XN_RING) * C + h * 256
                        v.scalar_tensor_tensor(
                            out=s_stage[:, (sp * GRP + tc) * C + h * 256:
                                        (sp * GRP + tc) * C + h * 256 + 256],
                            in0=psS2[tc % 2][:],
                            scalar=0.0, in1=xnhi[:, xsl:xsl + 256],
                            op0=OP.add, op1=OP.add).then_inc(sem_v.h, 1)
            vec(v_attn)
            sem_v.inc(QH * (NSI + GRP))
            v_s2_done[gg] = sem_v.n

            def g_sw(g):
                for tc in range(GRP):
                    g.wait_ge(sem_v.h, v0 + 3 * 36 + 32 + tc + 1)
                    tg = gg * GRP + tc
                    g.dma_start(out=s_d[tg * 128:(tg + 1) * 128, :],
                                in_=s_stage[:, (sp * GRP + tc) * C:
                                            (sp * GRP + tc + 1) * C]
                                ).then_inc(sem_gs[gg % 2].h, 16)
            block.gpsimd(g_sw)
            sem_gs[gg % 2].inc(16 * GRP)
            g_sw_done[gg] = sem_gs[gg % 2].n

        def emit_endb(b):
            def v_scan(v):
                if b >= 1:
                    v.wait_ge(sem_gsc.h, scat_g_done[b - 1])
                v.tensor_tensor_scan(out=cex, data0=hT, data1=hT,
                                     initial=0.0, op0=OP.add, op1=OP.max)
                v.tensor_sub(out=cex, in0=cex, in1=hT).then_inc(sem_v.h, 1)
            vec(v_scan)
            sem_v.inc(1)
            vscan = sem_v.n
            vch = [0] * TPB

            for tt in range(TPB):
                t = b * TPB + tt

                def t_bt(t_, tt=tt):
                    t_.wait_ge(sem_v.h, vscan if tt == 0 else sem_v.n)
                    t_.transpose(psTr[:, 0:8], routb[0:8, T_SEQ + tt * 128:T_SEQ + (tt + 1) * 128],
                                 ident_f[0:8, 0:8]).then_inc(sem_t.h, 1)
                block.tensor(t_bt)
                sem_t.inc(1)
                bt_t = sem_t.n

                def v_chain(v, tt=tt, t=t, b=b, bt_t=bt_t):
                    v.wait_ge(sem_t.h, bt_t)
                    v.tensor_mul(out=scr8[:], in0=psTr[:, 0:8],
                                 in1=is1_b[:, tt * 8:tt * 8 + 8])
                    v.reduce_sum(out=pos12[:, 0:1], in_=scr8[:],
                                 axis=mybir.AxisListType.X)
                    v.tensor_mul(out=scr8[:], in0=psTr[:, 0:8],
                                 in1=is2_b[:, tt * 8:tt * 8 + 8])
                    v.reduce_sum(out=pos12[:, 1:2], in_=scr8[:],
                                 axis=mybir.AxisListType.X)
                    v.tensor_scalar(out=keep12[:], in0=pos12[:], scalar1=float(CAP),
                                    scalar2=None, op0=OP.is_lt)
                    v.tensor_mul(out=c12_all[:, t * 2:t * 2 + 2],
                                 in0=p12_b[:, tt * 2:tt * 2 + 2], in1=keep12[:])
                    v.tensor_scalar(out=tokf[:], in0=iotap[:], scalar1=1.0,
                                    scalar2=float(t * 128), op0=OP.mult, op1=OP.add)
                    v.tensor_copy(out=payload[:, tt * 2:tt * 2 + 1], in_=tokf[:])
                    v.tensor_copy(out=payload[:, tt * 2 + 1:tt * 2 + 2], in_=tokf[:])
                    for k in range(2):
                        ek = e12_b[:, tt * 2 + k:tt * 2 + k + 1]
                        pk = pos12[:, k:k + 1]
                        kk = keep12[:, k:k + 1]
                        v.tensor_scalar(out=slotf[:], in0=ek,
                                        scalar1=float(nb * CAP),
                                        scalar2=float(b * CAP - ZROW),
                                        op0=OP.mult, op1=OP.add)
                        v.tensor_add(out=slotf[:], in0=slotf[:], in1=pk)
                        v.tensor_mul(out=slotf[:], in0=slotf[:], in1=kk)
                        v.tensor_scalar(out=slotf[:], in0=slotf[:],
                                        scalar1=float(ZROW), scalar2=None,
                                        op0=OP.add)
                        v.tensor_copy(out=yidx_all[:, t * 2 + k:t * 2 + k + 1],
                                      in_=slotf[:])
                        v.tensor_scalar(out=slotf[:], in0=ek,
                                        scalar1=float(nb * TSTRIDE),
                                        scalar2=float(b * TSTRIDE - 1048576),
                                        op0=OP.mult, op1=OP.add)
                        v.tensor_add(out=slotf[:], in0=slotf[:], in1=pk)
                        v.tensor_mul(out=slotf[:], in0=slotf[:], in1=kk)
                        v.tensor_scalar(out=slotf[:], in0=slotf[:],
                                        scalar1=1048576.0, scalar2=None, op0=OP.add)
                        v.tensor_copy(out=idxa_u[:, tt * 2 + k:tt * 2 + k + 1],
                                      in_=slotf[:])
                    v.sem_inc(sem_v.h, 1)._wait_ge(ch_v.h, ch_v.n)
                vec(v_chain)
                sem_v.inc(1)
                vch[tt] = sem_v.n

            def g_scat(g, b=b):
                g.wait_ge(sem_v.h, vch[TPB - 1])
                with g.register("bnd%d" % b) as breg:
                    g.reg_mov(breg, TBL_ROWS - 1)
                    bval = g.snap(breg)
                    for tt in range(TPB):
                        for k in range(2):
                            g.indirect_dma_start(
                                out=tbl_d[:, :],
                                out_offset=IndirectOffsetOnAxis(
                                    idxa_u[:, tt * 2 + k:tt * 2 + k + 1], 0),
                                in_=payload[:, tt * 2:tt * 2 + 2],
                                in_offset=None,
                                bounds_check=bval, oob_is_err=False
                            ).then_inc(sem_gsc.h, 16)
            block.gpsimd(g_scat)
            sem_gsc.inc(32 * TPB)
            endb_v_done[b] = vch[TPB - 1]
            scat_g_done[b] = sem_gsc.n

        for gg in range(NGRP):
            for tig in range(GRP):
                emit_tile(gg * GRP + tig)
            if gg >= 1:
                emit_attn_group(gg - 1)
            if (gg + 1) % GPB == 0:
                emit_endb(gg // GPB)
        emit_attn_group(NGRP - 1)

        _dma_drain()
        _phase_barrier()
        es_p1.close()

        es_p2 = ExitStack()
        w1s = sb(es_p2, "w1s", [128, 2 * NCC * 1024], BF16)
        w2s = sb(es_p2, "w2s", [128, 2 * NCC * 1024], BF16)
        h_sb = sb(es_p2, "h_sb", [128, nb * 8 * CAP], BF16)
        NSY = nb if nb >= 2 else 3
        y_sb = sb(es_p2, "y_sb", [128, NSY * 3 * C], BF16)
        bufT = sb(es_p2, "bufT", [128, 2 * nb * NCC * CAP], BF16)
        tokg = sb(es_p2, "tokg", [128, 2 * C], BF16)
        idxg = sb(es_p2, "idxg", [128, 2 * nb * 3], U32)
        y0t = sb(es_p2, "y0t", [128, 2 * C], BF16)
        y1t = sb(es_p2, "y1t", [128, 2 * C], BF16)
        sin_t = sb(es_p2, "sin_t", [128, 2 * C], BF16)
        xr_t = sb(es_p2, "xr_t", [128, 2 * C], F32)
        accf = sb(es_p2, "accf", [128, C], F32)
        gsil = accf
        outst = sb(es_p2, "outst", [128, 2 * C], F32)
        b2_bf = sb(es_p2, "b2_bf", [1, E * C], BF16)

        psH = [ps(es_p2, f"psH{i}", [128, CAP], F32) for i in range(2)]
        psY = [ps(es_p2, "psY0", [128, C], F32), ps(es_p2, "psY1", [128, C], F32),
               ps(es_p2, "psY2", [64, C], F32)]
        NSZ = (128, 128, 64)
        NOF = (0, 128, 256)

        def g_b2ld(g):
            g.dma_start(out=b2_bf[0:1, :], in_=b2_d[:, :],
                        max_dma_last_dim=8192).then_inc(sem_b2.h, 16)
        block.gpsimd(g_b2ld)
        sem_b2.inc(16)
        b2_ready = sem_b2.n

        buf_ready = {}
        te_mark = {}
        gctr = [0]
        tr_base_p2 = [sem_tr.n]

        def emit_gather_e(b, e, prev_marker):
            ep = e % 2

            def g_gath(g):
                g.wait_ge(sem_gsc.h, scat_g_done[b])
                g.wait_ge(sem_gxn.h, sem_gxn.n)
                if prev_marker is not None:
                    g.wait_ge(sem_t.h, prev_marker)
                r0 = (e * nb + b) * TSTRIDE
                g.dma_start(out=idxg[:, (ep * nb + b) * 3:(ep * nb + b) * 3 + 3],
                            in_=tbl_d[r0:r0 + TSTRIDE, 0:1].rearrange(
                                "(a p) c -> p (a c)", p=128),
                            allow_slow_non_contiguous=True
                            ).then_inc(sem_gix.h, 16)
                g.wait_ge(sem_gix.h, sem_gix.n + 16)
                for a in range(3):
                    j = gctr[0] + a
                    nr = 128 if a < 2 else 64
                    if j >= 2:
                        g.wait_ge(sem_tr.h, tr_base_p2[0] + 16 * (j - 1))
                    g.indirect_dma_start(
                        out=tokg[0:nr, (j % 2) * C:(j % 2) * C + C],
                        out_offset=None, in_=xn_d[:, :],
                        in_offset=IndirectOffsetOnAxis(
                            idxg[0:nr, (ep * nb + b) * 3 + a:
                                 (ep * nb + b) * 3 + a + 1], 0)
                    ).then_inc(sem_gg[j % 2].h, 16)
            block.gpsimd(g_gath)
            sem_gix.inc(16)
            gg0 = [sem_gg[0].n, sem_gg[1].n]
            for a in range(3):
                sem_gg[(gctr[0] + a) % 2].inc(16)

            def s_btr(sy):
                done = {0: gg0[0], 1: gg0[1]}
                for a in range(3):
                    j = gctr[0] + a
                    nr = 128 if a < 2 else 64
                    par = j % 2
                    done[par] += 16
                    sy.wait_ge(sem_gg[par].h, done[par])
                    sy.wait_ge(sem_tr.h, tr_base_p2[0] + 16 * j)
                    base = (ep * nb + b) * NCC * CAP
                    outap = bufT[:, base:base + NCC * CAP].rearrange(
                        "p (cc n) -> p cc n", cc=NCC)[:, :, a * 128:a * 128 + nr]
                    sy.dma_start(out=outap,
                                 in_=tokg[0:nr, (j % 2) * C:(j % 2) * C + C],
                                 transpose=True).then_inc(sem_tr.h, 16)
            block.sync(s_btr)
            sem_tr.inc(16 * 3)
            gctr[0] += 3
            buf_ready[(b, e)] = sem_tr.n

        for b in range(nb):
            emit_gather_e(b, 0, None)

        for e in range(E):
            ep = e % 2
            w0 = [sem_wq[0].n, sem_wq[1].n]
            t_e0 = sem_t.n
            for S in range(4):
                par = S % 2
                t0 = sem_t.n
                s0 = sem_s.n
                v0 = sem_v.n

                SUP_T = 9 * nb

                def g_wload(g, e=e, S=S, par=par, t_e0=t_e0, SUP_T=SUP_T):
                    if S >= 2:
                        g.wait_ge(sem_t.h, t_e0 + (S - 1) * SUP_T)
                    elif e >= 1:
                        g.wait_ge(sem_t.h, te_mark[e - 1] - (1 - S) * SUP_T)
                    g.dma_start(
                        out=w1s[:, par * NCC * 1024:(par + 1) * NCC * 1024]
                        .rearrange("p (cc j) -> p cc j", cc=NCC),
                        in_=w1_d[e * C:(e + 1) * C, S * 1024:(S + 1) * 1024]
                        .rearrange("(cc p) j -> p cc j", p=128),
                        max_dma_last_dim=8192).then_inc(sem_wq[par].h, 16)
                    g.dma_start(
                        out=w2s[:, par * NCC * 1024:(par + 1) * NCC * 1024]
                        .rearrange("p (hcb c) -> p hcb c", hcb=8),
                        in_=w2_d[e * H + S * 1024:e * H + (S + 1) * 1024, :]
                        .rearrange("(hcb p) c -> p hcb c", p=128),
                        max_dma_last_dim=8192).then_inc(sem_wq[par].h, 16)
                block.gpsimd(g_wload)
                sem_wq[par].inc(32)
                wtgt = sem_wq[par].n

                if e + 1 < E:
                    for bb in range(nb):
                        if bb % 4 == S:
                            emit_gather_e(bb, e + 1, te_mark.get(e - 1))

                def t_sec(t_, e=e, S=S, par=par, t0=t0, s0=s0, v0=v0,
                          wtgt=wtgt, ep=ep):
                    t_.wait_ge(sem_wq[par].h, wtgt)
                    if e == 0 and S == 0:
                        t_.wait_ge(sem_b2.h, b2_ready)
                    for hc in range(8):
                        for b in range(nb):
                            u = hc * nb + b
                            if S == 0 and hc == 0:
                                t_.wait_ge(sem_tr.h, buf_ready[(b, e)])
                            if u >= 2 or S >= 1 or e >= 1:
                                t_.wait_ge(sem_s.h, s0 + u - 1)
                            last = None
                            for cc in range(NCC):
                                base = (ep * nb + b) * NCC * CAP + cc * CAP
                                last = t_.matmul(
                                    psH[u % 2][:],
                                    w1s[:, par * NCC * 1024 + cc * 1024 +
                                        hc * 128:par * NCC * 1024 + cc * 1024 +
                                        hc * 128 + 128],
                                    bufT[:, base:base + CAP],
                                    start=(cc == 0), stop=(cc == NCC - 1))
                            last.then_inc(sem_t.h, 1)
                    for b in range(nb):
                        t_.wait_ge(sem_s.h, s0 + 8 * nb)
                        if b >= 1:
                            t_.wait_ge(sem_v.h, v0 + b)
                        elif e >= 1 or S >= 1:
                            t_.wait_ge(sem_v.h, v0)
                        last = None
                        if S == 0:
                            for ncx in range(3):
                                for half in range(2):
                                    last = t_.matmul(
                                        psY[ncx][:, half * 512:(half + 1) * 512],
                                        ones_row_bf[0:1, 0:NSZ[ncx]],
                                        b2_bf[0:1, e * C + half * 512:
                                              e * C + (half + 1) * 512],
                                        start=True, stop=False)
                        for hc in range(8):
                            for ncx in range(3):
                                for half in range(2):
                                    last = t_.matmul(
                                        psY[ncx][:, half * 512:(half + 1) * 512],
                                        h_sb[:, (b * 8 + hc) * CAP + NOF[ncx]:
                                             (b * 8 + hc) * CAP + NOF[ncx] +
                                             NSZ[ncx]],
                                        w2s[:, par * NCC * 1024 + hc * 1024 +
                                            half * 512:par * NCC * 1024 +
                                            hc * 1024 + (half + 1) * 512],
                                        start=(S >= 1 and hc == 0),
                                        stop=(hc == 7))
                        last.then_inc(sem_t.h, 1)
                block.tensor(t_sec)
                sem_t.inc(8 * nb + nb)

                def s_relu(x, e=e, S=S, t0=t0):
                    for hc in range(8):
                        for b in range(nb):
                            u = hc * nb + b
                            x.wait_ge(sem_t.h, t0 + u + 1)
                            x.activation(
                                out=h_sb[:, (b * 8 + hc) * CAP:
                                         (b * 8 + hc) * CAP + CAP],
                                in_=psH[u % 2][:], func=AF.Relu,
                                bias=b1_sb[:, e * NHC + S * 8 + hc:
                                           e * NHC + S * 8 + hc + 1],
                                scale=1.0).then_inc(sem_s.h, 1)
                sca(s_relu)
                sem_s.inc(8 * nb)

                def v_evac(v, e=e, S=S, t0=t0, v0=v0):
                    for b in range(nb):
                        v.wait_ge(sem_t.h, t0 + 8 * nb + b + 1)
                        if S == 0 and e >= 1:
                            v.wait_ge(sem_yfree.h, e)
                        ins = None
                        ysl = (e * nb + b) % NSY
                        for ncx in range(3):
                            dst = y_sb[0:NSZ[ncx], ysl * 3 * C + ncx * C:
                                       ysl * 3 * C + (ncx + 1) * C]
                            if S == 0:
                                ins = v.tensor_copy(out=dst,
                                                    in_=psY[ncx][0:NSZ[ncx], :])
                            else:
                                ins = v.tensor_add(out=dst,
                                                   in0=psY[ncx][0:NSZ[ncx], :],
                                                   in1=dst)
                        ins.then_inc(sem_v.h, 1)
                vec(v_evac)
                sem_v.inc(nb)
                last_v_evac = sem_v.n

            te_mark[e] = sem_t.n

            def s_ywr(sy, e=e, lv=last_v_evac):
                for b in range(nb):
                    eg = e * nb + b
                    if sem_yp[b % 2].n:
                        sy.wait_ge(sem_yp[b % 2].h, sem_yp[b % 2].n)
                    sy.wait_ge(sem_v.h, lv - nb + 1 + b)
                    r0 = (e * nb + b) * CAP
                    ysl = (e * nb + b) % NSY
                    sy.dma_start(out=y_d[r0:r0 + 256, :].rearrange(
                        "(k p) c -> p k c", p=128),
                        in_=y_sb[:, ysl * 3 * C:ysl * 3 * C + 2 * C]
                        .rearrange("p (k c) -> p k c", k=2)
                        ).then_inc(sem_yp[b % 2].h, 16)
                    sy.dma_start(out=y_d[r0 + 256:r0 + CAP, :],
                                 in_=y_sb[0:64, ysl * 3 * C + 2 * C:
                                          ysl * 3 * C + 3 * C]
                                 ).then_inc(sem_yp[b % 2].h, 16)
                    sem_yp[b % 2].inc(32)
                    if e == E - 1:
                        sy.wait_ge(sem_yp[0].h, sem_yp[0].n)
                        sy.wait_ge(sem_yp[1].h, sem_yp[1].n)
                        sy.sem_inc(sem_ydn.h, 1)
                        sem_ydn.inc(1)
                sy.wait_ge(sem_yp[0].h, sem_yp[0].n)
                if sem_yp[1].n:
                    sy.wait_ge(sem_yp[1].h, sem_yp[1].n)
                sy.sem_inc(sem_yfree.h, 1)
            block.sync(s_ywr)
            sem_yfree.inc(1)

        def emit_p3_tile(t):
            b = t // TPB
            tt = t % TPB
            tp = tt % 2

            def g_yg(g, t=t, tp=tp, b=b):
                g.wait_ge(sem_ydn.h, nb)
                g.wait_ge(sem_v.h, sem_v.n)
                g.indirect_dma_start(
                    out=y0t[:, tp * C:(tp + 1) * C], out_offset=None,
                    in_=y_d[:, :],
                    in_offset=IndirectOffsetOnAxis(
                        yidx_all[:, t * 2:t * 2 + 1], 0)
                ).then_inc(sem_gyg[tp].h, 16)
                g.indirect_dma_start(
                    out=y1t[:, tp * C:(tp + 1) * C], out_offset=None,
                    in_=y_d[:, :],
                    in_offset=IndirectOffsetOnAxis(
                        yidx_all[:, t * 2 + 1:t * 2 + 2], 0)
                ).then_inc(sem_gyg[tp].h, 16)
            block.gpsimd(g_yg)
            sem_gyg[tp].inc(32)
            ygb = sem_gyg[tp].n

            def s_p3ld(sy, t=t, tp=tp):
                if sem_op[tp].n:
                    sy.wait_ge(sem_op[tp].h, sem_op[tp].n)
                sy.sem_inc(sem_ofree.h, 1)
                sy.dma_start(out=sin_t[:, tp * C:(tp + 1) * C],
                             in_=s_d[t * 128:(t + 1) * 128, :]
                             ).then_inc(sem_p3p[tp].h, 16)
                sy.dma_start(out=xr_t[:, tp * C:(tp + 1) * C],
                             in_=x_d[t * 128:(t + 1) * 128, :]
                             ).then_inc(sem_p3p[tp].h, 16)
            block.sync(s_p3ld)
            sem_p3p[tp].inc(32)
            sem_ofree.inc(1)
            p3l = sem_p3p[tp].n
            ofr = sem_ofree.n

            def v_p3(v, t=t, tp=tp, ygb=ygb, p3l=p3l):
                v.wait_ge(sem_gyg[tp].h, ygb)
                v.wait_ge(sem_p3p[tp].h, p3l)
                v.tensor_scalar(out=accf[:], in0=y0t[:, tp * C:(tp + 1) * C],
                                scalar1=c12_all[:, t * 2:t * 2 + 1],
                                scalar2=None, op0=OP.mult)
                v.scalar_tensor_tensor(out=accf[:],
                                       in0=y1t[:, tp * C:(tp + 1) * C],
                                       scalar=c12_all[:, t * 2 + 1:t * 2 + 2],
                                       in1=accf[:], op0=OP.mult, op1=OP.add
                                       ).then_inc(sem_v.h, 1)
            vec(v_p3)
            sem_v.inc(1)

            def s_p3(x):
                x.wait_ge(sem_v.h, sem_v.n)
                x.activation(out=gsil[:], in_=accf[:], func=AF.Silu,
                             bias=0.0, scale=1.0).then_inc(sem_s.h, 1)
            sca(s_p3)
            sem_s.inc(1)

            def v_p3b(v, t=t, tp=tp, ofr=ofr):
                v.wait_ge(sem_s.h, sem_s.n)
                v.wait_ge(sem_ofree.h, ofr)
                v.tensor_mul(out=gsil[:], in0=gsil[:],
                             in1=sin_t[:, tp * C:(tp + 1) * C])
                v.tensor_add(out=outst[:, tp * C:(tp + 1) * C],
                             in0=xr_t[:, tp * C:(tp + 1) * C], in1=gsil[:]
                             ).then_inc(sem_v.h, 1)
            vec(v_p3b)
            sem_v.inc(1)
            vp3 = sem_v.n

            def s_owr(sy, t=t, tp=tp, vp3=vp3):
                sy.wait_ge(sem_v.h, vp3)
                sy.dma_start(out=out_d[t * 128:(t + 1) * 128, :],
                             in_=outst[:, tp * C:(tp + 1) * C]
                             ).then_inc(sem_op[tp].h, 16)
            block.sync(s_owr)
            sem_op[tp].inc(16)

        prev_yv = [0]
        t0_prev_exp = [None]
        tr_base_p2 = [sem_tr.n]

        for t3 in range(NTIL):
            emit_p3_tile(t3)

        def s_final(sy):
            sy.wait_ge(sem_op[0].h, sem_op[0].n)
            sy.wait_ge(sem_op[1].h, sem_op[1].n)
        block.sync(s_final)

        es_p2.close()
    es.close()
    return nc


_NC_CACHE = {}


def kernel(L0_attn_residual, L0_attn_w_fc, L0_attn_w_proj, L0_attn_ln_g,
           L0_attn_ln_b, L0_ffn_Wr, L0_ffn_W1, L0_ffn_b1, L0_ffn_W2, L0_ffn_b2):
    nb = 4
    n_cores = 8
    if "nc" not in _NC_CACHE:
        _NC_CACHE["nc"] = build_program(nb=nb)
    nc = _NC_CACHE["nc"]

    x = np.ascontiguousarray(np.asarray(L0_attn_residual, dtype=np.float32))
    B = x.shape[0]
    wfc = np.ascontiguousarray(np.asarray(L0_attn_w_fc, dtype=np.float32))
    wproj = np.ascontiguousarray(np.asarray(L0_attn_w_proj, dtype=np.float32))
    lng = np.ascontiguousarray(np.asarray(L0_attn_ln_g, dtype=np.float32)
                               .reshape(NCC, 128))
    lnb = np.ascontiguousarray(np.asarray(L0_attn_ln_b, dtype=np.float32)
                               .reshape(NCC, 128))
    wr = np.ascontiguousarray(np.asarray(L0_ffn_Wr, dtype=np.float32))
    w1 = np.ascontiguousarray(np.asarray(L0_ffn_W1, dtype=np.float32)
                              .reshape(E * C, H))
    b1 = np.ascontiguousarray(np.asarray(L0_ffn_b1, dtype=np.float32)
                              .reshape(E * H // 128, 128))
    w2 = np.ascontiguousarray(np.asarray(L0_ffn_W2, dtype=np.float32)
                              .reshape(E * H, C))
    b2 = np.ascontiguousarray(np.asarray(L0_ffn_b2, dtype=np.float32))

    in_maps = []
    for c in range(n_cores):
        xs = np.ascontiguousarray(
            x[c * nb:(c + 1) * nb].reshape(nb * T_SEQ, C))
        in_maps.append({"x": xs, "wfc": wfc, "wproj": wproj, "lng": lng,
                        "lnb": lnb, "wr": wr, "w1": w1, "b1": b1, "w2": w2,
                        "b2": b2})

    res = run_bass_kernel_spmd(nc, in_maps, core_ids=list(range(n_cores)))
    outs = [res.results[c]["out"].reshape(nb, T_SEQ, C) for c in range(n_cores)]
    return np.ascontiguousarray(np.concatenate(outs, axis=0))
